# version 15
# baseline (speedup 1.0000x reference)
import numpy as np
import ml_dtypes
from contextlib import ExitStack

import concourse.bass as bass
import concourse.mybir as mybir
from concourse.bass_utils import run_bass_kernel_spmd
from concourse.bass_core_rust import IndirectOffsetOnAxis

F32 = mybir.dt.float32
BF16 = mybir.dt.bfloat16
AF = mybir.ActivationFunctionType
ALU = mybir.AluOpType
AXX = mybir.AxisListType.X

B, S, D = 2, 4096, 4096
HD = 128
NTILE = 8
TT_ALL = 32
T_OWN = 1024
NKB = 32
SCALE = 1.0 / float(np.sqrt(128.0))
SHIFT = 12.0

ENGINES = ("sync", "gpsimd", "vector", "scalar", "tensor")


class Build:
    def __init__(self, nc):
        self.nc = nc
        self.sched = {e: [] for e in ENGINES}
        self.stack = ExitStack()
        self.semv = {}
        self.sems = {}
        for name in ("CC", "pe", "ac", "dv"):
            self.sems[name] = self.stack.enter_context(nc.semaphore(name))
            self.semv[name] = 0

    def on(self, eng):
        def deco(fn):
            self.sched[eng].append(fn)
            return None
        return deco

    def inc(self, name, n=1):
        if name not in self.sems:
            self.sems[name] = self.stack.enter_context(self.nc.semaphore(name))
            self.semv[name] = 0
        self.semv[name] += n
        return (self.sems[name], self.semv[name], n)

    def val(self, name):
        return (self.sems[name], self.semv[name])

    @staticmethod
    def w(eng_obj, tok):
        if tok is not None and tok[1] > 0:
            eng_obj.wait_ge(tok[0], tok[1])

    def bar(self, e, ins, name):
        tok = self.inc(name, 1)
        ins.then_inc(tok[0], 1)
        e.wait_ge(tok[0], tok[1])


def build_program():
    nc = bass.Bass("TRN2", target_bir_lowering=False)
    bld = Build(nc)
    W = Build.w

    x0h = nc.dram_tensor("x0", [T_OWN, D], F32, kind="ExternalInput").ap()
    Wqh = nc.dram_tensor("Wq_my", [D, 1024], F32, kind="ExternalInput").ap()
    Wkh = nc.dram_tensor("Wk_my", [D, 256], F32, kind="ExternalInput").ap()
    Wvh = nc.dram_tensor("Wv_my", [D, 256], F32, kind="ExternalInput").ap()
    Woh = nc.dram_tensor("Wo", [D, D], F32, kind="ExternalInput").ap()
    Wbrh = nc.dram_tensor("Wbr", [1024, 1024], F32, kind="ExternalInput").ap()
    Wcth = nc.dram_tensor("WcatT", [D, 24], F32, kind="ExternalInput").ap()
    bch = nc.dram_tensor("basecat", [1, 24], F32, kind="ExternalInput").ap()
    gateh = nc.dram_tensor("gate", [1, 1], F32, kind="ExternalInput").ap()
    cosh = nc.dram_tensor("cosf", [S, HD], F32, kind="ExternalInput").ap()
    sinh = nc.dram_tensor("sinf", [S, HD], F32, kind="ExternalInput").ap()
    trih = nc.dram_tensor("trimask", [128, 128], F32, kind="ExternalInput").ap()
    onesh = nc.dram_tensor("onesf", [128, 128], F32, kind="ExternalInput").ap()
    cvech = nc.dram_tensor("cvec", [128, 8], F32, kind="ExternalInput").ap()
    cidxh = nc.dram_tensor("ctxidx", [128, 32], mybir.dt.int32,
                           kind="ExternalInput").ap()
    yh = nc.dram_tensor("y", [T_OWN, D], F32, kind="ExternalOutput").ap()

    a_dram = nc.dram_tensor("a_dram", [T_OWN, D], BF16).ap()
    a_t = nc.dram_tensor("a_t", [D, T_OWN], BF16).ap()
    aall = nc.dram_tensor("aall", [4 * D, T_OWN], BF16).ap()
    q_dram = nc.dram_tensor("q_dram", [S, 1024], BF16).ap()
    k_dram = nc.dram_tensor("k_dram", [S, 256], BF16).ap()
    ctx_in = [nc.dram_tensor(f"ctx_in{i}", [2048, 1024], BF16).ap()
              for i in range(2)]
    ctxall = [nc.dram_tensor(f"ctxall{i}", [8192, 1024], BF16).ap()
              for i in range(2)]
    b_dram = nc.dram_tensor("b_dram", [T_OWN, D], F32).ap()
    x1_dram = nc.dram_tensor("x1_dram", [T_OWN, D], F32).ap()
    x1bf_dram = nc.dram_tensor("x1bf_dram", [T_OWN, D], BF16).ap()
    xpre_dram = nc.dram_tensor("xpre_dram", [T_OWN, 1024], BF16).ap()

    groups = {}

    def sbg(gname, name, shape, dt):
        g = nc.sbuf_tensor(name, shape, dt)
        h = g.__enter__()
        groups.setdefault(gname, []).append(g)
        return h

    def psg(gname, name, shape):
        g = nc.psum_tensor(name, shape, F32)
        h = g.__enter__()
        groups.setdefault(gname, []).append(g)
        return h

    def free(gname):
        for g in reversed(groups.pop(gname, [])):
            g.__exit__(None, None, None)

    P = "persist"
    cvec = sbg(P, "cvec_sb", [128, 8], F32)
    tri = sbg(P, "tri_sb", [128, 128], F32)
    ones_bf = sbg(P, "ones_bf", [128, 128], BF16)
    sqs = sbg(P, "sqs", [128, D], F32)
    m2a = sbg(P, "m2a", [128, 8], F32)
    r2a = sbg(P, "r2a", [128, 8], F32)
    rhoa = sbg(P, "rhoa", [128, 8], F32)
    rhm2 = sbg(P, "rhm2", [128, 8], F32)
    bsum = sbg(P, "bsum", [128, 8], F32)
    bsq = sbg(P, "bsq", [128, 8], F32)
    mba = sbg(P, "mba", [128, 8], F32)
    rba = sbg(P, "rba", [128, 8], F32)
    v1 = sbg(P, "v1", [128, 1], F32)
    v2 = sbg(P, "v2", [128, 1], F32)
    v3 = sbg(P, "v3", [128, 1], F32)
    vA1 = sbg(P, "vA1", [128, 8], F32)
    vA2 = sbg(P, "vA2", [128, 8], F32)
    vA3 = sbg(P, "vA3", [128, 8], F32)
    Hlog = sbg(P, "Hlog", [128, 8, 24], F32)
    M16 = sbg(P, "M16", [128, 8, 16], F32)
    t16 = sbg(P, "t16", [128, 8, 16], F32)
    zmx = sbg(P, "zmx", [128, 8, 4], F32)
    zmx2 = sbg(P, "zmx2", [128, 8, 1], F32)
    aS = sbg(P, "aS", [128, 8, 4], F32)
    bS = sbg(P, "bS", [128, 8, 4], F32)
    Hres = sbg(P, "Hres", [128, 8, 16], F32)
    Hpre = sbg(P, "Hpre", [128, 8, 4], F32)
    Hpost = sbg(P, "Hpost", [128, 8, 4], F32)
    u24 = sbg(P, "u24", [128, 24], F32)
    csb = sbg(P, "csb", [128, 24], F32)
    bbb = sbg(P, "bbb", [128, 24], F32)
    cs1 = sbg(P, "cs1", [1, 24], BF16)
    g11 = sbg(P, "g11", [1, 1], F32)
    g11b = sbg(P, "g11b", [1, 1], BF16)
    gcol = sbg(P, "gcol", [128, 1], F32)
    gm1 = sbg(P, "gm1", [128, 1], F32)
    bc1b = sbg(P, "bc1b", [1, 24], BF16)
    cidx_sb = sbg(P, "cidx_sb", [128, 32], mybir.dt.int32)

    C = {"eps1": cvec[:, 1:2], "eps2": cvec[:, 2:3], "inv": cvec[:, 3:4],
         "shift": cvec[:, 0:1]}

    GV = "gv"
    V_sb = sbg(GV, "V_sb", [128, 2, NKB, 128], BF16)
    GW = "gw"
    cos_sb = sbg(GW, "cos_sb", [128, TT_ALL, HD], BF16)
    sin_sb = sbg(GW, "sin_sb", [128, TT_ALL, HD], BF16)
    Wq_sb = sbg(GW, "Wq_sb", [128, 32, 1024], BF16)
    Wk_sb = sbg(GW, "Wk_sb", [128, 32, 256], BF16)
    Wv_sb = sbg(GW, "Wv_sb", [128, 32, 256], BF16)
    GA = "ga"
    x0t = [sbg(GA, f"x0t{i}", [128, D], F32) for i in range(2)]
    at = [sbg(GA, f"at{i}", [128, D], BF16) for i in range(2)]
    aFMt = [sbg(GA, f"aFMt{i}", [128, T_OWN], BF16) for i in range(4)]

    def emit_dma(qname, semname, out, in_, pre_wait=None):
        tok = bld.inc(semname, 16)

        @bld.on(qname)
        def _(e, out=out, in_=in_, tok=tok, pre_wait=pre_wait):
            if pre_wait is not None:
                for _t in (pre_wait if isinstance(pre_wait, list) else [pre_wait]):
                    W(e, _t)
            e.dma_start(out=out, in_=in_).then_inc(tok[0], 16)
        return (tok[0], tok[1])

    emit_dma("sync", "cst", cvec[:], cvech)
    emit_dma("sync", "cst", tri[:], trih)
    emit_dma("sync", "cst", g11[:], gateh)
    t_cst = emit_dma("sync", "cst", cidx_sb[:], cidxh)
    t_cvec = t_tri = t_g = t_cidx = t_cst
    emit_dma("gpsimd", "cstg", ones_bf[:], onesh)
    emit_dma("gpsimd", "cstg", cos_sb[:],
             cosh.rearrange("(t p) d -> p t d", p=128))
    emit_dma("gpsimd", "cstg", sin_sb[:],
             sinh.rearrange("(t p) d -> p t d", p=128))
    emit_dma("gpsimd", "cstg", Wq_sb[:],
             Wqh.rearrange("(i p) n -> p i n", p=128))
    emit_dma("gpsimd", "cstg", Wk_sb[:],
             Wkh.rearrange("(i p) n -> p i n", p=128))
    t_cstg = emit_dma("gpsimd", "cstg", Wv_sb[:],
                      Wvh.rearrange("(i p) n -> p i n", p=128))
    t_ones = t_cos = t_sin = t_wq = t_wk = t_wv = t_cstg

    GPB = "gpb"
    pq = [psg(GPB, f"pq{i}", [128, 1024]) for i in range(2)]
    pk = [psg(GPB, f"pk{i}", [128, 256]) for i in range(2)]
    pv = [psg(GPB, f"pv{i}", [128, 256]) for i in range(2)]

    a_norm = []
    a_sqred = []
    a_wr = []
    for t in range(NTILE):
        s1 = vA1[:, t:t + 1]
        s2 = vA2[:, t:t + 1]
        s3 = vA3[:, t:t + 1]
        tok_x0 = bld.inc("x0l", 16)

        @bld.on("scalar")
        def _(e, t=t, tok_x0=tok_x0,
              prev=(a_norm[t - 2] if t >= 2 else None)):
            if prev is not None:
                W(e, prev)
            e.dma_start(out=x0t[t % 2][:], in_=x0h[128 * t:128 * (t + 1), :]
                        ).then_inc(tok_x0[0], 16)
        tok_x0 = (tok_x0[0], tok_x0[1])

        tok_sq = bld.inc("ac")

        @bld.on("scalar")
        def _(e, t=t, tok_x0=tok_x0, tok_sq=tok_sq,
              prev_red=(a_sqred[t - 1] if t >= 1 else None)):
            W(e, tok_x0)
            if prev_red is not None:
                W(e, prev_red)
            e.activation(out=sqs[:], in_=x0t[t % 2][:], func=AF.Square
                         ).then_inc(tok_sq[0], 1)
        tok_sq = (tok_sq[0], tok_sq[1])

        tok_red = bld.inc("dv")
        tok_st = bld.inc("dv")

        @bld.on("vector")
        def _(e, t=t, s1=s1, s2=s2, s3=s3, tok_x0=tok_x0, tok_sq=tok_sq,
              tok_st=tok_st, tok_red=tok_red):
            if t == 0:
                W(e, t_cvec)
            W(e, tok_x0)
            e.reduce_sum(s1, x0t[t % 2][:], axis=AXX)
            W(e, tok_sq)
            e.reduce_sum(s2, sqs[:], axis=AXX).then_inc(tok_red[0], 1)
            bld.bar(e, e.tensor_scalar(out=s1, in0=s1, scalar1=C["inv"],
                                       scalar2=None, op0=ALU.mult), "dvb")
            bld.bar(e, e.tensor_mul(s3, s1, s1), "dvb")
            e.scalar_tensor_tensor(out=s2, in0=s2, scalar=C["inv"],
                                   in1=s3, op0=ALU.mult, op1=ALU.subtract
                                   ).then_inc(tok_st[0], 1)
        tok_st = (tok_st[0], tok_st[1])
        a_sqred.append((tok_red[0], tok_red[1]))

        tok_r = bld.inc("ac")

        @bld.on("scalar")
        def _(e, t=t, s2=s2, s3=s3, tok_st=tok_st, tok_r=tok_r):
            W(e, tok_st)
            bld.bar(e, e.activation(out=s3, in_=s2, func=AF.Ln,
                                    bias=C["eps1"]), "acb")
            e.activation(out=s3, in_=s3, func=AF.Exp, scale=-0.5
                         ).then_inc(tok_r[0], 1)
        tok_r = (tok_r[0], tok_r[1])

        tok_n = bld.inc("dv")

        @bld.on("vector")
        def _(e, t=t, s1=s1, s3=s3, tok_r=tok_r, tok_n=tok_n,
              prev_wr=(a_wr[t - 2] if t >= 2 else None)):
            W(e, tok_r)
            if prev_wr is not None:
                W(e, prev_wr)
            e.tensor_scalar(out=at[t % 2][:], in0=x0t[t % 2][:], scalar1=s1,
                            scalar2=s3, op0=ALU.subtract, op1=ALU.mult
                            ).then_inc(tok_n[0], 1)
        tok_n = (tok_n[0], tok_n[1])
        a_norm.append(tok_n)

        t_aw = emit_dma("sync", "aw", a_dram[128 * t:128 * (t + 1), :],
                        at[t % 2][:], pre_wait=tok_n)
        a_wr.append(t_aw)
    t_a_all = a_wr[-1]

    w_toks = []
    for c in range(32):
        tok_T = bld.inc(f"a2t{c % 4}", 16)

        @bld.on("sync")
        def _(e, c=c, tok_T=tok_T,
              prev_w=(w_toks[c - 4] if c >= 4 else None)):
            if c == 0:
                W(e, t_a_all)
            if prev_w is not None:
                W(e, prev_w)
            e.dma_start(out=aFMt[c % 4][:],
                        in_=a_dram[:, 128 * c:128 * (c + 1)],
                        transpose=True).then_inc(tok_T[0], 16)
        tok_T = (tok_T[0], tok_T[1])
        t_w = emit_dma("scalar", f"a2w{c % 4}", a_t[128 * c:128 * (c + 1), :],
                       aFMt[c % 4][:], pre_wait=tok_T)
        w_toks.append(t_w)
    t_at_all = w_toks[-1]

    tok_cc1 = bld.inc("CC")

    @bld.on("gpsimd")
    def _(e, tok_cc1=tok_cc1, w_fin=tuple(w_toks[-4:])):
        for wf in w_fin:
            W(e, wf)
        e.collective_compute(
            "AllGather", ALU.bypass,
            replica_groups=[[0, 1, 2, 3], [4, 5, 6, 7]],
            ins=[a_t], outs=[aall],
        ).then_inc(tok_cc1[0], 1)
    tok_cc1 = (tok_cc1[0], tok_cc1[1])

    free("ga")
    GB = "gb"
    attb = [sbg(GB, f"attb{i}", [128, 32, 128], BF16) for i in range(4)]
    ro1 = sbg(GB, "ro1", [128, 1024], F32)
    ro2 = sbg(GB, "ro2", [128, 512], F32)
    qbf = [sbg(GB, f"qbf{i}", [128, 1024], BF16) for i in range(4)]
    kbf = [sbg(GB, f"kbf{i}", [128, 256], BF16) for i in range(4)]

    aall_r = aall.rearrange("(r c p) t -> r p c t", r=4, c=32, p=128)
    rope_done, vev_done, qk_wr, pe_toks = [], [], [], []
    for tt in range(TT_ALL):
        r_src, t0 = tt // 8, 128 * (tt % 8)
        tok_ab = bld.inc(f"ab{tt % 4}", 16)

        @bld.on("gpsimd")
        def _(e, tt=tt, r_src=r_src, t0=t0, tok_ab=tok_ab,
              prev=(pe_toks[tt - 4] if tt >= 4 else None)):
            if tt == 0:
                W(e, tok_cc1)
            if prev is not None:
                W(e, prev)
            e.dma_start(out=attb[tt % 4][:],
                        in_=aall_r[r_src, :, :, t0:t0 + 128]
                        ).then_inc(tok_ab[0], 16)
        tok_ab = (tok_ab[0], tok_ab[1])

        tok_pe = bld.inc("pe")

        @bld.on("tensor")
        def _(e, tt=tt, tok_ab=tok_ab, tok_pe=tok_pe,
              prev_ro=(rope_done[tt - 2] if tt >= 2 else None),
              prev_ve=(vev_done[tt - 2] if tt >= 2 else None)):
            if tt == 0:
                W(e, t_wq); W(e, t_wk); W(e, t_wv)
            W(e, tok_ab)
            if prev_ro is not None:
                W(e, prev_ro)
            if prev_ve is not None:
                W(e, prev_ve)
            for i in range(32):
                lhs = attb[tt % 4][:, i, :]
                e.matmul(pq[tt % 2][:, 0:512], lhs, Wq_sb[:, i, 0:512],
                         start=(i == 0), stop=(i == 31))
                e.matmul(pq[tt % 2][:, 512:1024], lhs, Wq_sb[:, i, 512:1024],
                         start=(i == 0), stop=(i == 31))
                e.matmul(pk[tt % 2][:], lhs, Wk_sb[:, i, :],
                         start=(i == 0), stop=(i == 31))
                mm = e.matmul(pv[tt % 2][:], lhs, Wv_sb[:, i, :],
                              start=(i == 0), stop=(i == 31))
            mm.then_inc(tok_pe[0], 1)
        tok_pe = (tok_pe[0], tok_pe[1])
        pe_toks.append(tok_pe)

        tok_ve = bld.inc("ac")

        @bld.on("scalar")
        def _(e, tt=tt, tok_pe=tok_pe, tok_ve=tok_ve):
            W(e, tok_pe)
            e.copy(out=V_sb[:, 0, tt, :], in_=pv[tt % 2][:, 0:128])
            e.copy(out=V_sb[:, 1, tt, :], in_=pv[tt % 2][:, 128:256]
                   ).then_inc(tok_ve[0], 1)
        vev_done.append((tok_ve[0], tok_ve[1]))

        tok_ro = bld.inc("dv")

        @bld.on("vector")
        def _(e, tt=tt, tok_pe=tok_pe, tok_ro=tok_ro,
              prev_w=(qk_wr[tt - 4] if tt >= 4 else None)):
            if tt == 0:
                W(e, t_cos); W(e, t_sin)
            W(e, tok_pe)
            if prev_w is not None:
                W(e, prev_w[0])
                W(e, prev_w[1])
            cosv = cos_sb[:, tt:tt + 1, :]
            sinv = sin_sb[:, tt:tt + 1, :]
            for (ps, nh, obf) in ((pq[tt % 2], 8, qbf[tt % 4]),
                                  (pk[tt % 2], 2, kbf[tt % 4])):
                pvv = ps[:].rearrange("p (h d) -> p h d", h=nh)
                ov = obf[:].rearrange("p (h d) -> p h d", h=nh)
                t1v = ro1[:, 0:nh * 128].rearrange("p (h d) -> p h d", h=nh)
                t2v = ro2[:, 0:nh * 64].rearrange("p (h d) -> p h d", h=nh)
                cb = cosv.broadcast_to([128, nh, HD])
                sbc = sinv.broadcast_to([128, nh, HD])
                e.tensor_mul(t1v, pvv, cb)
                e.tensor_mul(t2v, pvv[:, :, 64:128], sbc[:, :, 0:64])
                e.tensor_sub(ov[:, :, 0:64], t1v[:, :, 0:64], t2v)
                e.tensor_mul(t2v, pvv[:, :, 0:64], sbc[:, :, 64:128])
                mm = e.tensor_add(ov[:, :, 64:128], t1v[:, :, 64:128], t2v)
            mm.then_inc(tok_ro[0], 1)
        tok_ro = (tok_ro[0], tok_ro[1])
        rope_done.append(tok_ro)

        t_qw = emit_dma("sync", f"qw{tt % 4}",
                        q_dram[128 * tt:128 * (tt + 1), :],
                        qbf[tt % 4][:], pre_wait=tok_ro)
        t_kw = emit_dma("sync", f"kw{tt % 4}",
                        k_dram[128 * tt:128 * (tt + 1), :],
                        kbf[tt % 4][:])
        qk_wr.append((t_qw, t_kw))
    t_b_done_all = [t for pair in qk_wr[-4:] for t in pair]

    free("gb"); free("gw"); free("gpb")
    GCF = "gcf"
    ctxFM = sbg(GCF, "ctxFM", [128, 32, 1024], BF16)
    GC = "gc"
    qFM = sbg(GC, "qFM", [128, 8, S], BF16)
    KFM = sbg(GC, "KFM", [128, 2, S], BF16)
    PT = [sbg(GC, f"PT{i}", [128, 1024], BF16) for i in range(4)]
    lnl = sbg(GC, "lnl", [128, 512], F32)
    rl = [sbg(GC, f"rl{i}", [128, 512], F32) for i in range(2)]
    ctxev = [sbg(GC, f"ctxev{i}", [128, 512], BF16) for i in range(2)]
    GPC = "gpc"
    psS = [psg(GPC, f"psS{i}", [128, 1024]) for i in range(2)]
    pctx = [psg(GPC, f"pctx{i}", [128, 512]) for i in range(2)]
    pl = [psg(GPC, f"pl{i}", [128, 512]) for i in range(2)]

    tp_q, tp_k = {}, {}
    _fm_first = [True]

    def emit_wave(kind, idx):
        for rb in range(4):
            tok = bld.inc("fmt", 16)

            @bld.on("sync")
            def _(e, kind=kind, idx=idx, rb=rb, tok=tok,
                  first=_fm_first[0]):
                if first:
                    for tk in t_b_done_all:
                        W(e, tk)
                if kind == "q":
                    e.dma_start(out=qFM[:, idx, 1024 * rb:1024 * (rb + 1)],
                                in_=q_dram[1024 * rb:1024 * (rb + 1),
                                           128 * idx:128 * (idx + 1)],
                                transpose=True).then_inc(tok[0], 16)
                else:
                    e.dma_start(out=KFM[:, idx, 1024 * rb:1024 * (rb + 1)],
                                in_=k_dram[1024 * rb:1024 * (rb + 1),
                                           128 * idx:128 * (idx + 1)],
                                transpose=True).then_inc(tok[0], 16)
            _fm_first[0] = False
            if kind == "q":
                tp_q[idx] = (tok[0], tok[1])
            else:
                tp_k[idx] = (tok[0], tok[1])

    emit_wave("k", 0)
    emit_wave("q", 0)

    it = 0
    evac_dv, lnl_ac, ctx_dmas = [], [], []
    for u in range(8):
        b_ = u // 4
        if u == 0:
            emit_wave("q", 1)
        elif u == 3:
            emit_wave("k", 1)
            emit_wave("q", 4)
        elif u < 7:
            emit_wave("q", u + 1)
        head_first = [True]
        for qc in (7, 0, 6, 1, 5, 2, 4, 3):
            par = it % 2
            n_kb = 4 * qc + 4
            npairs = (n_kb + 1) // 2
            sE, lctx_toks = [], []

            def emit_lctx(p, kbs):
                tok_l = bld.inc("pe")

                @bld.on("tensor")
                def _(e, u=u, qc=qc, b_=b_, p=p, kbs=tuple(kbs), par=par,
                      n_kb=n_kb, tok_l=tok_l, tok_e=sE[p]):
                    W(e, tok_e)
                    for kb in kbs:
                        d = kb - 4 * qc
                        off = 128 * d if d >= 0 else 0
                        col0 = 512 * (kb % 2)
                        pts = PT[p % 4][:, col0 + off:col0 + 512]
                        e.matmul(pl[par][:, off:512], ones_bf[:], pts,
                                 start=(kb == 0), stop=(kb == n_kb - 1))
                        mm = e.matmul(pctx[par][:, off:512],
                                      V_sb[:, b_, kb, :], pts,
                                      start=(kb == 0), stop=(kb == n_kb - 1))
                    mm.then_inc(tok_l[0], 1)
                lctx_toks.append((tok_l[0], tok_l[1]))

            for p in range(npairs):
                kbs = [kb for kb in (2 * p, 2 * p + 1) if kb < n_kb]
                tok_s = bld.inc("pe")

                @bld.on("tensor")
                def _(e, u=u, qc=qc, b_=b_, p=p, kbs=tuple(kbs), it=it,
                      par=par, tok_s=tok_s,
                      s_free=(sE[p - 2] if p >= 2 else None),
                      first=(head_first[0] and p == 0),
                      prev_ev=(evac_dv[it - 2] if (p == 0 and it >= 2) else None),
                      prev_ln=(lnl_ac[it - 2] if (p == 0 and it >= 2) else None),
                      head_w=((tp_q[u], tp_k[u // 4],
                               vev_done[-1]) if (head_first[0] and p == 0)
                              else None)):
                    if first and head_w is not None:
                        for hw in head_w:
                            W(e, hw)
                    if prev_ev is not None:
                        W(e, prev_ev)
                    if prev_ln is not None:
                        W(e, prev_ln)
                    if s_free is not None:
                        W(e, s_free)
                    for kb in kbs:
                        d = kb - 4 * qc
                        off = 128 * d if d >= 0 else 0
                        col0 = 512 * (kb % 2)
                        mm = e.matmul(
                            psS[p % 2][:, col0 + off:col0 + 512],
                            KFM[:, b_, 128 * kb:128 * (kb + 1)],
                            qFM[:, u, 512 * qc + off:512 * qc + 512],
                            start=True, stop=True)
                    mm.then_inc(tok_s[0], 1)
                tok_s = (tok_s[0], tok_s[1])
                head_first[0] = False

                has_diag = any(kb - 4 * qc >= 0 for kb in kbs)
                tok_m = None
                if has_diag:
                    tok_mi = bld.inc("dv")

                    @bld.on("vector")
                    def _(e, qc=qc, p=p, kbs=tuple(kbs), tok_mi=tok_mi,
                          tok_s=tok_s, first=_fm_first[0] or (it < 2)):
                        if first:
                            W(e, t_tri)
                        W(e, tok_s)
                        mm = None
                        for kb in kbs:
                            d = kb - 4 * qc
                            if d < 0:
                                continue
                            col0 = 512 * (kb % 2) + 128 * d
                            mm = e.tensor_add(psS[p % 2][:, col0:col0 + 128],
                                              psS[p % 2][:, col0:col0 + 128],
                                              tri[:])
                        mm.then_inc(tok_mi[0], 1)
                    tok_m = (tok_mi[0], tok_mi[1])

                tok_e = bld.inc("ac")

                @bld.on("scalar")
                def _(e, qc=qc, p=p, kbs=tuple(kbs), tok_e=tok_e,
                      tok_s=tok_s, tok_m=tok_m,
                      pt_free=(lctx_toks[p - 4] if p >= 4 else None)):
                    W(e, tok_s)
                    if tok_m is not None:
                        W(e, tok_m)
                    if pt_free is not None:
                        W(e, pt_free)
                    full = (len(kbs) == 2 and
                            all(kb - 4 * qc < 0 for kb in kbs))
                    if full:
                        mm = e.activation(out=PT[p % 4][:], in_=psS[p % 2][:],
                                          func=AF.Exp, scale=SCALE,
                                          bias=C["shift"])
                    else:
                        for kb in kbs:
                            d = kb - 4 * qc
                            off = 128 * d if d >= 0 else 0
                            col0 = 512 * (kb % 2)
                            mm = e.activation(
                                out=PT[p % 4][:, col0 + off:col0 + 512],
                                in_=psS[p % 2][:, col0 + off:col0 + 512],
                                func=AF.Exp, scale=SCALE, bias=C["shift"])
                    mm.then_inc(tok_e[0], 1)
                sE.append((tok_e[0], tok_e[1]))

                if p >= 1:
                    emit_lctx(p - 1, [kb for kb in (2 * (p - 1), 2 * p - 1)
                                      if kb < n_kb])
            emit_lctx(npairs - 1, [kb for kb in (2 * npairs - 2, 2 * npairs - 1)
                                   if kb < n_kb])

            tok_ln = bld.inc("ac", 2)

            @bld.on("scalar")
            def _(e, par=par, tok_ln=tok_ln, last=lctx_toks[-1]):
                W(e, last)
                e.activation(out=lnl[:], in_=pl[par][:], func=AF.Ln)
                e.activation(out=rl[par][:], in_=lnl[:], func=AF.Exp,
                             scale=-1.0).then_inc(tok_ln[0], 2)
            tok_ln = (tok_ln[0], tok_ln[1])
            lnl_ac.append(tok_ln)

            tok_ev = bld.inc("dv")

            @bld.on("vector")
            def _(e, par=par, tok_ev=tok_ev, tok_ln=tok_ln,
                  prev_dma=(ctx_dmas[it - 2] if it >= 2 else None)):
                W(e, tok_ln)
                if prev_dma is not None:
                    W(e, prev_dma)
                e.tensor_mul(ctxev[par][:], pctx[par][:], rl[par][:]
                             ).then_inc(tok_ev[0], 1)
            tok_ev = (tok_ev[0], tok_ev[1])
            evac_dv.append(tok_ev)

            row0 = 512 * (qc // 2) + 128 * (u % 4)
            col0 = 512 * (qc % 2)
            t_cd = emit_dma("sync", f"cx{par}",
                            ctx_in[u // 4][row0:row0 + 128, col0:col0 + 512],
                            ctxev[par][:], pre_wait=tok_ev)
            ctx_dmas.append(t_cd)
            it += 1
        if u == 3:
            tok_cc2a = bld.inc("CC")

            @bld.on("gpsimd")
            def _(e, tok_cc2a=tok_cc2a,
                  cx_fin=(ctx_dmas[-1], ctx_dmas[-2])):
                W(e, cx_fin[0])
                W(e, cx_fin[1])
                e.collective_compute(
                    "AllGather", ALU.bypass,
                    replica_groups=[[0, 1, 2, 3], [4, 5, 6, 7]],
                    ins=[ctx_in[0]], outs=[ctxall[0]],
                ).then_inc(tok_cc2a[0], 1)
            tok_cc2a = (tok_cc2a[0], tok_cc2a[1])
            for h in range(32):
                uu = (h % 4) if h < 16 else 4 + ((h - 16) % 4)
                if uu >= 4:
                    continue
                tok = bld.inc("cf", 16)

                @bld.on("gpsimd")
                def _(e, h=h, tok=tok, tok_cc2a=tok_cc2a):
                    if h == 0:
                        W(e, tok_cc2a)
                        W(e, t_cidx)
                    e.indirect_dma_start(
                        out=ctxFM[:, h, :], out_offset=None,
                        in_=ctxall[0],
                        in_offset=IndirectOffsetOnAxis(cidx_sb[:, h:h + 1], 0),
                    ).then_inc(tok[0], 16)
    t_ctx_all = ctx_dmas[-1]

    tok_cc2 = bld.inc("CC")

    @bld.on("gpsimd")
    def _(e, tok_cc2=tok_cc2, cx_fin=(ctx_dmas[-1], ctx_dmas[-2])):
        W(e, cx_fin[0])
        W(e, cx_fin[1])
        e.collective_compute(
            "AllGather", ALU.bypass,
            replica_groups=[[0, 1, 2, 3], [4, 5, 6, 7]],
            ins=[ctx_in[1]], outs=[ctxall[1]],
        ).then_inc(tok_cc2[0], 1)
    tok_cc2 = (tok_cc2[0], tok_cc2[1])

    free("gc"); free("gpc")
    GD = "gd"
    Wo_sb = [sbg(GD, f"Wo_sb{i}", [128, 32, 512], BF16) for i in range(2)]
    btev = [sbg(GD, f"btev{i}", [128, 512], F32) for i in range(2)]
    GPD = "gpd"
    pb = [psg(GPD, f"pb{i}", [128, 512]) for i in range(2)]

    for h in range(32):
        uu = (h % 4) if h < 16 else 4 + ((h - 16) % 4)
        if uu < 4:
            continue
        tok = bld.inc("cf", 16)

        @bld.on("gpsimd")
        def _(e, h=h, tok=tok):
            e.wait_ge(tok_cc2[0], tok_cc2[1])
            e.indirect_dma_start(
                out=ctxFM[:, h, :], out_offset=None,
                in_=ctxall[1],
                in_offset=IndirectOffsetOnAxis(cidx_sb[:, h:h + 1], 0),
            ).then_inc(tok[0], 16)
    t_cf_all = bld.val("cf")

    d_ac, d_dv, d_bw, d_pe_slice = [], [], [], []
    for s_ in range(8):
        tok_wo = bld.inc(f"wo{s_ % 2}", 16)

        @bld.on("gpsimd")
        def _(e, s_=s_, tok_wo=tok_wo,
              prev=(d_pe_slice[s_ - 2] if s_ >= 2 else None)):
            if prev is not None:
                W(e, prev)
            e.dma_start(out=Wo_sb[s_ % 2][:],
                        in_=Woh[:, 512 * s_:512 * (s_ + 1)]
                        .rearrange("(i p) n -> p i n", p=128)
                        ).then_inc(tok_wo[0], 16)
        tok_wo = (tok_wo[0], tok_wo[1])

        for t in range(NTILE):
            sti = s_ * 8 + t
            tok_pe = bld.inc("pe")

            @bld.on("tensor")
            def _(e, s_=s_, t=t, tok_pe=tok_pe, tok_wo=tok_wo,
                  first=(s_ == 0 and t == 0),
                  prev_ac=(d_ac[sti - 2] if sti >= 2 else None),
                  prev_dv=(d_dv[sti - 2] if sti >= 2 else None)):
                if first:
                    W(e, t_cf_all)
                if t == 0:
                    W(e, tok_wo)
                if prev_ac is not None:
                    W(e, prev_ac)
                if prev_dv is not None:
                    W(e, prev_dv)
                for i in range(32):
                    mm = e.matmul(pb[t % 2][:],
                                  ctxFM[:, i, 128 * t:128 * (t + 1)],
                                  Wo_sb[s_ % 2][:, i, :],
                                  start=(i == 0), stop=(i == 31))
                mm.then_inc(tok_pe[0], 1)
            tok_pe = (tok_pe[0], tok_pe[1])
            if t == NTILE - 1:
                d_pe_slice.append(tok_pe)

            tok_ac = bld.inc("ac", 2)

            @bld.on("scalar")
            def _(e, t=t, tok_pe=tok_pe, tok_ac=tok_ac,
                  prev_bw=(d_bw[sti - 2] if sti >= 2 else None),
                  prev_dv=(d_dv[sti - 1] if sti >= 1 else None)):
                W(e, tok_pe)
                if prev_bw is not None:
                    W(e, prev_bw)
                if prev_dv is not None:
                    W(e, prev_dv)
                e.copy(out=btev[t % 2][:], in_=pb[t % 2][:])
                e.activation(out=sqs[:, 0:512], in_=pb[t % 2][:],
                             func=AF.Square).then_inc(tok_ac[0], 2)
            tok_ac = (tok_ac[0], tok_ac[1])
            d_ac.append(tok_ac)

            tok_dv = bld.inc("dv")

            @bld.on("vector")
            def _(e, s_=s_, t=t, tok_pe=tok_pe, tok_ac=tok_ac, tok_dv=tok_dv):
                W(e, tok_ac)
                W(e, tok_pe)
                e.reduce_sum(v1[:], pb[t % 2][:], axis=AXX)
                e.reduce_sum(v2[:], sqs[:, 0:512], axis=AXX)
                if s_ == 0:
                    e.tensor_copy(out=bsum[:, t:t + 1], in_=v1[:])
                    mm = e.tensor_copy(out=bsq[:, t:t + 1], in_=v2[:])
                else:
                    e.tensor_add(bsum[:, t:t + 1], bsum[:, t:t + 1], v1[:])
                    mm = e.tensor_add(bsq[:, t:t + 1], bsq[:, t:t + 1], v2[:])
                mm.then_inc(tok_dv[0], 1)
            d_dv.append((tok_dv[0], tok_dv[1]))

            t_bw = emit_dma("sync", f"bw{t % 2}",
                            b_dram[128 * t:128 * (t + 1),
                                   512 * s_:512 * (s_ + 1)],
                            btev[t % 2][:], pre_wait=tok_ac)
            d_bw.append(t_bw)
    t_b_last = [d_bw[-1], d_bw[-2]]

    tok_bst = bld.inc("dv")

    @bld.on("vector")
    def _(e, tok_bst=tok_bst, last=d_dv[-1]):
        W(e, last)
        bld.bar(e, e.tensor_scalar(out=mba[:], in0=bsum[:],
                                   scalar1=1.0 / 4096.0, scalar2=None,
                                   op0=ALU.mult), "dvb")
        bld.bar(e, e.tensor_mul(u24[:, 0:8], mba[:], mba[:]), "dvb")
        e.scalar_tensor_tensor(out=rba[:], in0=bsq[:], scalar=1.0 / 4096.0,
                               in1=u24[:, 0:8], op0=ALU.mult,
                               op1=ALU.subtract).then_inc(tok_bst[0], 1)
    tok_bst = (tok_bst[0], tok_bst[1])
    tok_bst2 = bld.inc("ac")

    @bld.on("scalar")
    def _(e, tok_bst=tok_bst, tok_bst2=tok_bst2):
        W(e, tok_bst)
        bld.bar(e, e.activation(out=rba[:], in_=rba[:], func=AF.Ln,
                                bias=C["eps1"]), "acb")
        e.activation(out=rba[:], in_=rba[:], func=AF.Exp, scale=-0.5
                     ).then_inc(tok_bst2[0], 1)
    tok_bst2 = (tok_bst2[0], tok_bst2[1])

    free("gd"); free("gcf"); free("gv"); free("gpd")
    WcT = sbg("gwc", "WcT", [128, 32, 24], BF16)
    GE = "ge"
    bfull = sbg(GE, "bfull", [128, D], F32)
    x0e = sbg(GE, "x0e", [128, D], F32)
    x1t = sbg(GE, "x1t", [128, D], F32)
    x1bft = sbg(GE, "x1bft", [128, D], BF16)
    emit_dma("gpsimd", "wct", WcT[:],
             Wcth.rearrange("(i p) j -> p i j", p=128),
             pre_wait=[d_pe_slice[-1]] + t_b_last)
    t_wct = emit_dma("gpsimd", "wct", bc1b[:], bch)
    t_bc = t_wct

    e_x1w, e_bfw = [], []
    for t in range(NTILE):
        pw = None
        if t == 0:
            pw = t_b_last
        elif t >= 1:
            pw = e_bfw[t - 1]
        t_bl = emit_dma("scalar", "ebl", bfull[:],
                        b_dram[128 * t:128 * (t + 1), :],
                        pre_wait=pw)
        t_xl = emit_dma("scalar", "exl", x0e[:],
                        x0h[128 * t:128 * (t + 1), :])

        tok_x1 = bld.inc("dv")

        @bld.on("vector")
        def _(e, t=t, t_bl=t_bl, t_xl=t_xl, tok_x1=tok_x1, tok_bst2=tok_bst2,
              prev_w=(e_x1w[t - 1] if t >= 1 else None)):
            if t == 0:
                W(e, tok_bst2)
            W(e, t_bl)
            if prev_w is not None:
                W(e, prev_w)
            e.tensor_scalar(out=bfull[:], in0=bfull[:], scalar1=mba[:, t:t + 1],
                            scalar2=rba[:, t:t + 1], op0=ALU.subtract,
                            op1=ALU.mult)
            W(e, t_xl)
            e.tensor_add(x1t[:], x0e[:], bfull[:])
            e.reduce_sum(v1[:], x1t[:], axis=AXX)
            e.tensor_scalar(out=v1[:], in0=v1[:], scalar1=C["inv"],
                            scalar2=None, op0=ALU.mult)
            e.tensor_copy(out=x1bft[:], in_=x1t[:]).then_inc(tok_x1[0], 1)
        tok_x1 = (tok_x1[0], tok_x1[1])

        t_x1w = emit_dma("scalar", "x1w", x1_dram[128 * t:128 * (t + 1), :],
                         x1t[:], pre_wait=tok_x1)
        t_bfw = emit_dma("scalar", "bfw", x1bf_dram[128 * t:128 * (t + 1), :],
                         x1bft[:])
        e_x1w.append(t_x1w)
        e_bfw.append(t_bfw)

        tok_sq = bld.inc("ac")

        @bld.on("scalar")
        def _(e, t=t, tok_x1=tok_x1, tok_sq=tok_sq):
            W(e, tok_x1)
            e.activation(out=sqs[:], in_=x1t[:], func=AF.Square
                         ).then_inc(tok_sq[0], 1)
        tok_sq = (tok_sq[0], tok_sq[1])

        tok_f1 = bld.inc("dv")

        @bld.on("vector")
        def _(e, t=t, tok_sq=tok_sq, tok_f1=tok_f1):
            W(e, tok_sq)
            e.reduce_sum(v2[:], sqs[:], axis=AXX)
            e.tensor_copy(out=m2a[:, t:t + 1], in_=v1[:])
            bld.bar(e, e.tensor_mul(v3[:], v1[:], v1[:]), "dvb")
            e.scalar_tensor_tensor(out=v2[:], in0=v2[:], scalar=C["inv"],
                                   in1=v3[:], op0=ALU.mult, op1=ALU.subtract
                                   ).then_inc(tok_f1[0], 1)
        tok_f1 = (tok_f1[0], tok_f1[1])

        tok_f2 = bld.inc("ac")

        @bld.on("scalar")
        def _(e, tok_f1=tok_f1, tok_f2=tok_f2):
            W(e, tok_f1)
            bld.bar(e, e.activation(out=v3[:], in_=v2[:], func=AF.Ln,
                                    bias=C["eps1"]), "acb")
            e.activation(out=v3[:], in_=v3[:], func=AF.Exp, scale=-0.5
                         ).then_inc(tok_f2[0], 1)
        tok_f2 = (tok_f2[0], tok_f2[1])

        tok_f3 = bld.inc("dv")

        @bld.on("vector")
        def _(e, t=t, tok_f2=tok_f2, tok_f3=tok_f3):
            W(e, tok_f2)
            e.tensor_copy(out=r2a[:, t:t + 1], in_=v3[:])
            bld.bar(e, e.tensor_mul(v2[:], v2[:], v3[:]), "dvb")
            e.tensor_mul(v2[:], v2[:], v3[:]).then_inc(tok_f3[0], 1)
        tok_f3 = (tok_f3[0], tok_f3[1])

        tok_f4 = bld.inc("ac")

        @bld.on("scalar")
        def _(e, tok_f3=tok_f3, tok_f4=tok_f4):
            W(e, tok_f3)
            bld.bar(e, e.activation(out=v2[:], in_=v2[:], func=AF.Ln,
                                    bias=C["eps2"]), "acb")
            e.activation(out=v2[:], in_=v2[:], func=AF.Exp, scale=-0.5
                         ).then_inc(tok_f4[0], 1)
        tok_f4 = (tok_f4[0], tok_f4[1])

        tok_f5 = bld.inc("dv")

        @bld.on("vector")
        def _(e, t=t, tok_f4=tok_f4, tok_f5=tok_f5):
            W(e, tok_f4)
            bld.bar(e, e.tensor_mul(rhoa[:, t:t + 1], r2a[:, t:t + 1],
                                    v2[:]), "dvb")
            e.tensor_mul(rhm2[:, t:t + 1], rhoa[:, t:t + 1], m2a[:, t:t + 1]
                         ).then_inc(tok_f5[0], 1)
    t_x1bf_all = e_bfw[-1]
    t_rho_all = bld.val("dv")

    free("ge")
    GF = "gf"
    x1h = sbg(GF, "x1h", [128, 16, 1024], BF16)
    GPF = "gpf"
    pH = [psg(GPF, f"pH{i}", [128, 24]) for i in range(2)]
    pmisc = psg(GPF, "pmisc", [128, 24])

    tok_cs = bld.inc("pe")

    @bld.on("tensor")
    def _(e, tok_cs=tok_cs, tok_bst2=tok_bst2):
        W(e, t_wct)
        W(e, t_ones)
        W(e, tok_bst2)
        for i in range(32):
            mm = e.matmul(pmisc[0:1, :], ones_bf[:, 0:1], WcT[:, i, :],
                          start=(i == 0), stop=(i == 31))
        mm.then_inc(tok_cs[0], 1)
    tok_cs = (tok_cs[0], tok_cs[1])

    tok_cse = bld.inc("ac")

    @bld.on("scalar")
    def _(e, tok_cs=tok_cs, tok_cse=tok_cse):
        W(e, tok_cs)
        e.copy(out=cs1[:], in_=pmisc[0:1, :]).then_inc(tok_cse[0], 1)
    tok_cse = (tok_cse[0], tok_cse[1])

    tok_bcast = bld.inc("pe")

    @bld.on("tensor")
    def _(e, tok_cse=tok_cse, tok_bcast=tok_bcast):
        W(e, tok_cse)
        W(e, t_bc)
        e.matmul(pmisc[:], ones_bf[0:1, :], cs1[:], start=True, stop=True)
        e.matmul(pH[0][:], ones_bf[0:1, :], bc1b[:], start=True, stop=True
                 ).then_inc(tok_bcast[0], 1)
    tok_bcast = (tok_bcast[0], tok_bcast[1])

    tok_bce = bld.inc("ac", 2)

    @bld.on("scalar")
    def _(e, tok_bcast=tok_bcast, tok_bce=tok_bce):
        W(e, tok_bcast)
        e.copy(out=csb[:], in_=pmisc[:])
        e.copy(out=bbb[:], in_=pH[0][:]).then_inc(tok_bce[0], 2)
    tok_bce = (tok_bce[0], tok_bce[1])

    h_pe_half = []
    h_ev_toks = []
    for half in range(2):
        tr_last = None
        for i in range(16):
            tok = bld.inc("x1h", 16)

            @bld.on("scalar")
            def _(e, half=half, i=i, tok=tok,
                  pe_prev=(h_pe_half[-1] if (half == 1 and i == 0) else None)):
                if half == 0 and i == 0:
                    W(e, t_x1bf_all)
                    W(e, t_rho_all)
                if pe_prev is not None:
                    W(e, pe_prev)
                c = 16 * half + i
                e.dma_start(out=x1h[:, i, :],
                            in_=x1bf_dram[:, 128 * c:128 * (c + 1)],
                            transpose=True).then_inc(tok[0], 16)
            tr_last = (tok[0], tok[1])
        t_tr_all = tr_last

        for t in range(NTILE):
            hti = half * 8 + t
            tok_pe = bld.inc("pe")

            @bld.on("tensor")
            def _(e, half=half, t=t, tok_pe=tok_pe, t_tr_all=t_tr_all,
                  tok_bce=tok_bce,
                  prev_ev=(h_ev_toks[hti - 2] if hti >= 2 else None)):
                if t == 0:
                    W(e, t_tr_all)
                    if half == 0:
                        W(e, tok_bce)
                if prev_ev is not None:
                    W(e, prev_ev)
                for i in range(16):
                    mm = e.matmul(pH[t % 2][:],
                                  x1h[:, i, 128 * t:128 * (t + 1)],
                                  WcT[:, 16 * half + i, :],
                                  start=(i == 0), stop=(i == 15))
                mm.then_inc(tok_pe[0], 1)
            tok_pe = (tok_pe[0], tok_pe[1])
            if t == NTILE - 1:
                h_pe_half.append(tok_pe)

            tok_he = bld.inc("dv")

            @bld.on("vector")
            def _(e, half=half, t=t, tok_pe=tok_pe, tok_he=tok_he):
                W(e, tok_pe)
                if half == 0:
                    e.tensor_copy(out=Hlog[:, t, :], in_=pH[t % 2][:]
                                  ).then_inc(tok_he[0], 1)
                else:
                    e.tensor_add(Hlog[:, t, :], Hlog[:, t, :], pH[t % 2][:]
                                 ).then_inc(tok_he[0], 1)
            h_ev_toks.append((tok_he[0], tok_he[1]))

    tok_fix = bld.inc("dv")

    @bld.on("vector")
    def _(e, tok_fix=tok_fix, last_ev=h_ev_toks[-1], t_rho_all=t_rho_all):
        W(e, last_ev)
        W(e, t_rho_all)
        for t in range(NTILE):
            bld.bar(e, e.scalar_tensor_tensor(out=u24[:], in0=csb[:],
                                              scalar=rhm2[:, t:t + 1],
                                              in1=bbb[:], op0=ALU.mult,
                                              op1=ALU.subtract), "dvb")
            bld.bar(e, e.tensor_scalar(out=Hlog[:, t, :], in0=Hlog[:, t, :],
                                       scalar1=rhoa[:, t:t + 1], scalar2=None,
                                       op0=ALU.mult), "dvb")
            mm = e.tensor_sub(Hlog[:, t, :], Hlog[:, t, :], u24[:])
            if t < NTILE - 1:
                bld.bar(e, mm, "dvb")
        mm.then_inc(tok_fix[0], 1)
    tok_fix = (tok_fix[0], tok_fix[1])

    Z16 = Hlog[:, :, 0:16]
    Z4 = Z16.rearrange("p t (i j) -> p t i j", i=4)
    M4 = M16[:].rearrange("p t (i j) -> p t i j", i=4)
    T4 = t16[:].rearrange("p t (i j) -> p t i j", i=4)

    tok_g1 = bld.inc("dv")

    @bld.on("vector")
    def _(e, tok_g1=tok_g1, tok_fix=tok_fix):
        W(e, tok_fix)
        bld.bar(e, e.reduce_max(zmx[:], Z4, axis=AXX), "dvb")
        bld.bar(e, e.reduce_max(zmx2[:], zmx[:], axis=AXX), "dvb")
        e.tensor_sub(M16[:], Z16, zmx2[:].broadcast_to([128, 8, 16])
                     ).then_inc(tok_g1[0], 1)
    tok_g1 = (tok_g1[0], tok_g1[1])

    tok_g2 = bld.inc("ac")

    @bld.on("scalar")
    def _(e, tok_g1=tok_g1, tok_g2=tok_g2):
        W(e, tok_g1)
        e.activation(out=M16[:], in_=M16[:], func=AF.Exp, scale=10.0
                     ).then_inc(tok_g2[0], 1)
    tok_g2 = (tok_g2[0], tok_g2[1])

    tok_g3 = bld.inc("dv")

    @bld.on("vector")
    def _(e, tok_g2=tok_g2, tok_g3=tok_g3):
        W(e, tok_g2)
        bld.bar(e, e.tensor_scalar(out=bS[:], in0=Hlog[:, :, 0:4],
                                   scalar1=0.0, scalar2=1.0, op0=ALU.mult,
                                   op1=ALU.add), "dvb")
        for _i in range(50):
            bld.bar(e, e.tensor_mul(
                T4, M4, bS[:].unsqueeze(2).broadcast_to([128, 8, 4, 4])),
                "dvb")
            bld.bar(e, e.reduce_sum(aS[:], T4, axis=AXX), "dvb")
            bld.bar(e, e.reciprocal(aS[:], aS[:]), "dvb")
            bld.bar(e, e.tensor_mul(
                T4, M4, aS[:].unsqueeze(3).broadcast_to([128, 8, 4, 4])),
                "dvb")
            bld.bar(e, e.reduce_sum(bS[:], T4.transpose([0, 1, 3, 2]),
                                    axis=AXX), "dvb")
            bld.bar(e, e.reciprocal(bS[:], bS[:]), "dvb")
        bld.bar(e, e.tensor_mul(
            T4, M4, aS[:].unsqueeze(3).broadcast_to([128, 8, 4, 4])), "dvb")
        e.tensor_mul(Hres[:].rearrange("p t (i j) -> p t i j", i=4), T4,
                     bS[:].unsqueeze(2).broadcast_to([128, 8, 4, 4])
                     ).then_inc(tok_g3[0], 1)
    tok_g3 = (tok_g3[0], tok_g3[1])

    tok_g4 = bld.inc("ac")

    @bld.on("scalar")
    def _(e, tok_g4=tok_g4):
        W(e, t_g)
        e.activation(out=g11[:], in_=g11[:], func=AF.Exp, scale=-1.0
                     ).then_inc(tok_g4[0], 1)
    tok_g4 = (tok_g4[0], tok_g4[1])

    tok_g5 = bld.inc("dv")

    @bld.on("vector")
    def _(e, tok_g4=tok_g4, tok_g5=tok_g5):
        W(e, tok_g4)
        bld.bar(e, e.tensor_scalar(out=g11[:], in0=g11[:], scalar1=1.0,
                                   scalar2=None, op0=ALU.add), "dvb")
        bld.bar(e, e.reciprocal(g11[:], g11[:]), "dvb")
        e.tensor_copy(out=g11b[:], in_=g11[:]).then_inc(tok_g5[0], 1)
    tok_g5 = (tok_g5[0], tok_g5[1])

    tok_g6 = bld.inc("pe")

    @bld.on("tensor")
    def _(e, tok_g5=tok_g5, tok_g6=tok_g6):
        W(e, tok_g5)
        e.matmul(pmisc[:, 0:1], ones_bf[0:1, :], g11b[:], start=True,
                 stop=True).then_inc(tok_g6[0], 1)
    tok_g6 = (tok_g6[0], tok_g6[1])

    tok_g7 = bld.inc("ac")

    @bld.on("scalar")
    def _(e, tok_g6=tok_g6, tok_g7=tok_g7):
        W(e, tok_g6)
        e.copy(out=gcol[:], in_=pmisc[:, 0:1]).then_inc(tok_g7[0], 1)
    tok_g7 = (tok_g7[0], tok_g7[1])

    tok_g8 = bld.inc("dv")

    @bld.on("vector")
    def _(e, tok_g3=tok_g3, tok_g7=tok_g7, tok_g8=tok_g8):
        W(e, tok_g3)
        W(e, tok_g7)
        bld.bar(e, e.tensor_scalar(out=gm1[:], in0=gcol[:], scalar1=-1.0,
                                   scalar2=1.0, op0=ALU.mult, op1=ALU.add),
                "dvb")
        bld.bar(e, e.tensor_scalar(out=Hres[:], in0=Hres[:], scalar1=gcol[:],
                                   scalar2=None, op0=ALU.mult), "dvb")
        diag = Hres[:, :, 0:16:5]
        bld.bar(e, e.tensor_scalar(out=diag, in0=diag, scalar1=gm1[:],
                                   scalar2=None, op0=ALU.add), "dvb")
        for j, (src0, dst) in enumerate(((Hlog[:, :, 16:20], Hpre),
                                         (Hlog[:, :, 20:24], Hpost))):
            bld.bar(e, e.reduce_max(zmx2[:], src0, axis=AXX), "dvb")
            mm = e.tensor_sub(dst[:], src0,
                              zmx2[:].broadcast_to([128, 8, 4]))
            if j == 0:
                bld.bar(e, mm, "dvb")
        mm.then_inc(tok_g8[0], 1)
    tok_g8 = (tok_g8[0], tok_g8[1])

    tok_g9 = bld.inc("ac")

    @bld.on("scalar")
    def _(e, tok_g8=tok_g8, tok_g9=tok_g9):
        W(e, tok_g8)
        bld.bar(e, e.activation(out=Hpre[:], in_=Hpre[:], func=AF.Exp),
                "acb")
        e.activation(out=Hpost[:], in_=Hpost[:], func=AF.Exp
                     ).then_inc(tok_g9[0], 1)
    tok_g9 = (tok_g9[0], tok_g9[1])

    tok_g10 = bld.inc("dv")

    @bld.on("vector")
    def _(e, tok_g9=tok_g9, tok_g10=tok_g10):
        W(e, tok_g9)
        for j, dst in enumerate((Hpre, Hpost)):
            bld.bar(e, e.reduce_sum(zmx2[:], dst[:], axis=AXX), "dvb")
            bld.bar(e, e.reciprocal(zmx2[:], zmx2[:]), "dvb")
            mm = e.tensor_mul(dst[:], dst[:],
                              zmx2[:].broadcast_to([128, 8, 4]))
            if j == 0:
                bld.bar(e, mm, "dvb")
        mm.then_inc(tok_g10[0], 1)
    tok_g10 = (tok_g10[0], tok_g10[1])

    free("gf"); free("gwc"); free("gpf")
    GH = "gh"
    Wbr_sb = sbg(GH, "Wbr_sb", [128, 8, 1024], BF16)
    xprFM = sbg(GH, "xprFM", [128, 8, 1024], BF16)
    dt_sb = sbg(GH, "dt_sb", [128, D], F32)
    x1l = sbg(GH, "x1l", [128, D], F32)
    xp_sb = sbg(GH, "xp_sb", [128, 1024], F32)
    xpb_sb = sbg(GH, "xpb_sb", [128, 1024], BF16)
    off_sb = sbg(GH, "off_sb", [128, D], F32)
    GPH = "gph"
    py = [psg(GPH, f"py{i}", [128, 1024]) for i in range(2)]

    t_wbr = emit_dma("gpsimd", "wbr", Wbr_sb[:],
                     Wbrh.rearrange("(i p) n -> p i n", p=128),
                     pre_wait=h_pe_half[-1])

    xp_dmas, h1_dv = [], []
    for t in range(NTILE):
        t_x1l = emit_dma("scalar", "hx1a", x1l[:],
                         x1_dram[128 * t:128 * (t + 1), :],
                         pre_wait=(h1_dv[t - 1] if t >= 1 else None))
        tok_h1 = bld.inc("dv")

        @bld.on("vector")
        def _(e, t=t, tok_h1=tok_h1, tok_g10=tok_g10, t_x1l=t_x1l,
              prev=(xp_dmas[t - 1] if t >= 1 else None)):
            if t == 0:
                W(e, tok_g10)
            W(e, t_x1l)
            if prev is not None:
                W(e, prev)
            e.tensor_scalar(out=dt_sb[:], in0=x1l[:],
                            scalar1=m2a[:, t:t + 1], scalar2=r2a[:, t:t + 1],
                            op0=ALU.subtract, op1=ALU.mult)
            dv_ = dt_sb[:].rearrange("p (n c) -> p n c", n=4)
            e.tensor_scalar(out=xp_sb[:], in0=dv_[:, 0, :],
                            scalar1=Hpre[:, t, 0:1], scalar2=None,
                            op0=ALU.mult)
            for n in range(1, 4):
                e.scalar_tensor_tensor(out=xp_sb[:], in0=dv_[:, n, :],
                                       scalar=Hpre[:, t, n:n + 1],
                                       in1=xp_sb[:], op0=ALU.mult, op1=ALU.add)
            e.tensor_copy(out=xpb_sb[:], in_=xp_sb[:]).then_inc(tok_h1[0], 1)
        tok_h1 = (tok_h1[0], tok_h1[1])
        h1_dv.append(tok_h1)

        t_xpw = emit_dma("scalar", "xpw",
                         xpre_dram[128 * t:128 * (t + 1), :], xpb_sb[:],
                         pre_wait=tok_h1)
        xp_dmas.append(t_xpw)
    t_xp_all = xp_dmas[-1]

    xtr_last = None
    for c in range(8):
        tok = bld.inc("xpt", 16)

        @bld.on("scalar")
        def _(e, c=c, tok=tok):
            if c == 0:
                W(e, t_xp_all)
            e.dma_start(out=xprFM[:, c, :],
                        in_=xpre_dram[:, 128 * c:128 * (c + 1)],
                        transpose=True).then_inc(tok[0], 16)
        xtr_last = (tok[0], tok[1])
    t_xtr_all = xtr_last

    h2_dv, y_dmas = [], []
    for t in range(NTILE):
        t_x1l2 = emit_dma("scalar", "hx1b", x1l[:],
                          x1_dram[128 * t:128 * (t + 1), :],
                          pre_wait=(h2_dv[t - 1] if t >= 1 else None))
        tok_pe = bld.inc("pe")

        @bld.on("tensor")
        def _(e, t=t, tok_pe=tok_pe,
              prev_dv=(h2_dv[t - 2] if t >= 2 else None)):
            if t == 0:
                W(e, t_xtr_all)
                W(e, t_wbr)
            if prev_dv is not None:
                W(e, prev_dv)
            for c in range(8):
                e.matmul(py[t % 2][:, 0:512],
                         xprFM[:, c, 128 * t:128 * (t + 1)],
                         Wbr_sb[:, c, 0:512], start=(c == 0), stop=(c == 7))
                mm = e.matmul(py[t % 2][:, 512:1024],
                              xprFM[:, c, 128 * t:128 * (t + 1)],
                              Wbr_sb[:, c, 512:1024], start=(c == 0),
                              stop=(c == 7))
            mm.then_inc(tok_pe[0], 1)
        tok_pe = (tok_pe[0], tok_pe[1])

        tok_mix = bld.inc("dv")

        @bld.on("vector")
        def _(e, t=t, tok_pe=tok_pe, tok_mix=tok_mix, t_x1l2=t_x1l2,
              prev_y=(y_dmas[t - 1] if t >= 1 else None)):
            W(e, tok_pe)
            W(e, t_x1l2)
            if prev_y is not None:
                W(e, prev_y)
            e.tensor_scalar(out=dt_sb[:], in0=x1l[:],
                            scalar1=m2a[:, t:t + 1], scalar2=r2a[:, t:t + 1],
                            op0=ALU.subtract, op1=ALU.mult)
            dv_ = dt_sb[:].rearrange("p (n c) -> p n c", n=4)
            ov = off_sb[:].rearrange("p (n c) -> p n c", n=4)
            for n in range(4):
                e.tensor_scalar(out=xp_sb[:], in0=dv_[:, 0, :],
                                scalar1=Hres[:, t, 4 * n:4 * n + 1],
                                scalar2=None, op0=ALU.mult)
                for j in range(1, 4):
                    e.scalar_tensor_tensor(
                        out=xp_sb[:], in0=dv_[:, j, :],
                        scalar=Hres[:, t, 4 * n + j:4 * n + j + 1],
                        in1=xp_sb[:], op0=ALU.mult, op1=ALU.add)
                e.scalar_tensor_tensor(out=ov[:, n, :], in0=py[t % 2][:],
                                       scalar=Hpost[:, t, n:n + 1],
                                       in1=xp_sb[:], op0=ALU.mult, op1=ALU.add)
            e.reduce_sum(v1[:], off_sb[:], axis=AXX)
            e.tensor_scalar(out=v1[:], in0=v1[:], scalar1=C["inv"],
                            scalar2=None, op0=ALU.mult).then_inc(tok_mix[0], 1)
        tok_mix = (tok_mix[0], tok_mix[1])
        h2_dv.append(tok_mix)

        tok_hsq = bld.inc("ac")

        @bld.on("scalar")
        def _(e, tok_mix=tok_mix, tok_hsq=tok_hsq):
            W(e, tok_mix)
            e.activation(out=sqs[:], in_=off_sb[:], func=AF.Square
                         ).then_inc(tok_hsq[0], 1)
        tok_hsq = (tok_hsq[0], tok_hsq[1])

        tok_hf = bld.inc("dv")

        @bld.on("vector")
        def _(e, tok_hsq=tok_hsq, tok_hf=tok_hf):
            W(e, tok_hsq)
            e.reduce_sum(v2[:], sqs[:], axis=AXX)
            bld.bar(e, e.tensor_mul(v3[:], v1[:], v1[:]), "dvb")
            e.scalar_tensor_tensor(out=v2[:], in0=v2[:], scalar=C["inv"],
                                   in1=v3[:], op0=ALU.mult, op1=ALU.subtract
                                   ).then_inc(tok_hf[0], 1)
        tok_hf = (tok_hf[0], tok_hf[1])

        tok_hr = bld.inc("ac")

        @bld.on("scalar")
        def _(e, tok_hf=tok_hf, tok_hr=tok_hr):
            W(e, tok_hf)
            bld.bar(e, e.activation(out=v3[:], in_=v2[:], func=AF.Ln,
                                    bias=C["eps1"]), "acb")
            e.activation(out=v3[:], in_=v3[:], func=AF.Exp, scale=-0.5
                         ).then_inc(tok_hr[0], 1)
        tok_hr = (tok_hr[0], tok_hr[1])

        tok_out = bld.inc("dv")

        @bld.on("vector")
        def _(e, t=t, tok_hr=tok_hr, tok_out=tok_out):
            W(e, tok_hr)
            e.tensor_scalar(out=off_sb[:], in0=off_sb[:], scalar1=v1[:],
                            scalar2=v3[:], op0=ALU.subtract, op1=ALU.mult)
            e.tensor_add(off_sb[:], off_sb[:], x1l[:]).then_inc(tok_out[0], 1)
        tok_out = (tok_out[0], tok_out[1])

        t_y = emit_dma("scalar", "yw", yh[128 * t:128 * (t + 1), :],
                       off_sb[:], pre_wait=tok_out)
        y_dmas.append(t_y)

    t_final = y_dmas[-1]

    @bld.on("scalar")
    def _(e, t_final=t_final):
        W(e, t_final)

    with nc.Block() as block:
        @block.sync
        def _(e):
            for fn in bld.sched["sync"]:
                fn(e)

        @block.gpsimd
        def _(e):
            for fn in bld.sched["gpsimd"]:
                fn(e)

        @block.vector
        def _(e):
            for fn in bld.sched["vector"]:
                fn(e)

        @block.scalar
        def _(e):
            for fn in bld.sched["scalar"]:
                fn(e)

        @block.tensor
        def _(e):
            for fn in bld.sched["tensor"]:
                fn(e)

    for gname in reversed(list(groups)):
        free(gname)
    bld.stack.close()
    return nc


def kernel(L0_attn_x, L0_attn_Wq, L0_attn_Wk, L0_attn_Wv, L0_attn_Wo,
           L0_attn_cos, L0_attn_sin, L0_ffn_W_res, L0_ffn_W_pre,
           L0_ffn_W_post, L0_ffn_H_res_base, L0_ffn_H_pre_base,
           L0_ffn_H_post_base, L0_ffn_gate, L0_ffn_W_branch):
    x = np.ascontiguousarray(np.asarray(L0_attn_x, dtype=np.float32))
    Wq = np.asarray(L0_attn_Wq, np.float32)
    Wk = np.asarray(L0_attn_Wk, np.float32)
    Wv = np.asarray(L0_attn_Wv, np.float32)
    Wo = np.ascontiguousarray(np.asarray(L0_attn_Wo, np.float32))
    cos = np.ascontiguousarray(np.asarray(L0_attn_cos, np.float32))
    sin = np.ascontiguousarray(np.asarray(L0_attn_sin, np.float32))
    WcatT = np.ascontiguousarray(np.vstack([
        np.asarray(L0_ffn_W_res, np.float32),
        np.asarray(L0_ffn_W_pre, np.float32),
        np.asarray(L0_ffn_W_post, np.float32)]).T)
    basecat = np.ascontiguousarray(np.concatenate([
        np.asarray(L0_ffn_H_res_base, np.float32).reshape(-1),
        np.asarray(L0_ffn_H_pre_base, np.float32).reshape(-1),
        np.asarray(L0_ffn_H_post_base, np.float32).reshape(-1),
    ]).reshape(1, 24))
    gate = np.ascontiguousarray(np.asarray(L0_ffn_gate, np.float32)
                                .reshape(1, 1))
    Wbr = np.ascontiguousarray(np.asarray(L0_ffn_W_branch, np.float32))

    tri = np.ascontiguousarray(
        np.where(np.arange(128)[None, :] >= np.arange(128)[:, None],
                 np.float32(0.0), np.float32(-1e9)).astype(np.float32))
    onesf = np.ones((128, 128), np.float32)
    cvec = np.zeros((128, 8), np.float32)
    cvec[:, 0] = -SHIFT
    cvec[:, 1] = 1e-5
    cvec[:, 2] = 1.1920929e-07
    cvec[:, 3] = 1.0 / 4096.0
    cvec[:, 4] = 1.0

    in_maps = []
    for c in range(8):
        b, r = c // 4, c % 4
        qh = list(range(4 * r, 4 * r + 4)) + \
            list(range(16 + 4 * r, 16 + 4 * r + 4))
        qcols = np.concatenate([np.arange(128 * h, 128 * (h + 1)) for h in qh])
        kvcols = np.concatenate([np.arange(128 * r, 128 * (r + 1)),
                                 np.arange(128 * (r + 4), 128 * (r + 5))])
        cidx = np.zeros((128, 32), np.int32)
        for h in range(32):
            if h < 16:
                rsrc, uu = h // 4, h % 4
            else:
                rsrc, uu = (h - 16) // 4, 4 + (h - 16) % 4
            cidx[:, h] = 2048 * rsrc + 512 * r + 128 * (uu % 4) + \
                np.arange(128)
        in_maps.append({
            "x0": np.ascontiguousarray(x[b, 1024 * r:1024 * (r + 1)]),
            "Wq_my": np.ascontiguousarray(Wq[:, qcols]),
            "Wk_my": np.ascontiguousarray(Wk[:, kvcols]),
            "Wv_my": np.ascontiguousarray(Wv[:, kvcols]),
            "Wo": Wo, "Wbr": Wbr, "WcatT": WcatT, "basecat": basecat,
            "gate": gate, "cosf": cos, "sinf": sin, "trimask": tri,
            "onesf": onesf, "cvec": cvec, "ctxidx": cidx,
        })

    nc = build_program()
    res = run_bass_kernel_spmd(nc, in_maps, core_ids=list(range(8)))
    out = np.empty((B, S, D), np.float32)
    for c in range(8):
        b, r = c // 4, c % 4
        out[b, 1024 * r:1024 * (r + 1)] = res.results[c]["y"]
    return out




# version 16
# speedup vs baseline: 1.0590x; 1.0590x over previous
import numpy as np
import ml_dtypes
from contextlib import ExitStack

import concourse.bass as bass
import concourse.mybir as mybir
from concourse.bass_utils import run_bass_kernel_spmd
from concourse.bass_core_rust import IndirectOffsetOnAxis

F32 = mybir.dt.float32
BF16 = mybir.dt.bfloat16
AF = mybir.ActivationFunctionType
ALU = mybir.AluOpType
AXX = mybir.AxisListType.X

B, S, D = 2, 4096, 4096
HD = 128
NTILE = 8
TT_ALL = 32
T_OWN = 1024
NKB = 32
SCALE = 1.0 / float(np.sqrt(128.0))
SHIFT = 12.0

ENGINES = ("sync", "gpsimd", "vector", "scalar", "tensor")


class Build:
    def __init__(self, nc):
        self.nc = nc
        self.sched = {e: [] for e in ENGINES}
        self.stack = ExitStack()
        self.semv = {}
        self.sems = {}
        for name in ("CC", "pe", "ac", "dv"):
            self.sems[name] = self.stack.enter_context(nc.semaphore(name))
            self.semv[name] = 0

    def on(self, eng):
        def deco(fn):
            self.sched[eng].append(fn)
            return None
        return deco

    def inc(self, name, n=1):
        if name not in self.sems:
            self.sems[name] = self.stack.enter_context(self.nc.semaphore(name))
            self.semv[name] = 0
        self.semv[name] += n
        return (self.sems[name], self.semv[name], n)

    def val(self, name):
        return (self.sems[name], self.semv[name])

    @staticmethod
    def w(eng_obj, tok):
        if tok is not None and tok[1] > 0:
            eng_obj.wait_ge(tok[0], tok[1])

    def bar(self, e, ins, name):
        tok = self.inc(name, 1)
        ins.then_inc(tok[0], 1)
        e.wait_ge(tok[0], tok[1])


def build_program():
    nc = bass.Bass("TRN2", target_bir_lowering=False)
    bld = Build(nc)
    W = Build.w

    x0h = nc.dram_tensor("x0", [T_OWN, D], F32, kind="ExternalInput").ap()
    Wqh = nc.dram_tensor("Wq_my", [D, 1024], F32, kind="ExternalInput").ap()
    Wkh = nc.dram_tensor("Wk_my", [D, 256], F32, kind="ExternalInput").ap()
    Wvh = nc.dram_tensor("Wv_my", [D, 256], F32, kind="ExternalInput").ap()
    Woh = nc.dram_tensor("Wo", [D, D], F32, kind="ExternalInput").ap()
    Wbrh = nc.dram_tensor("Wbr", [1024, 1024], F32, kind="ExternalInput").ap()
    Wcth = nc.dram_tensor("WcatT", [D, 24], F32, kind="ExternalInput").ap()
    bch = nc.dram_tensor("basecat", [1, 24], F32, kind="ExternalInput").ap()
    gateh = nc.dram_tensor("gate", [1, 1], F32, kind="ExternalInput").ap()
    cosh = nc.dram_tensor("cosf", [S, HD], F32, kind="ExternalInput").ap()
    sinh = nc.dram_tensor("sinf", [S, HD], F32, kind="ExternalInput").ap()
    trih = nc.dram_tensor("trimask", [128, 128], F32, kind="ExternalInput").ap()
    onesh = nc.dram_tensor("onesf", [128, 128], F32, kind="ExternalInput").ap()
    cvech = nc.dram_tensor("cvec", [128, 8], F32, kind="ExternalInput").ap()
    cidxh = nc.dram_tensor("ctxidx", [128, 32], mybir.dt.int32,
                           kind="ExternalInput").ap()
    yh = nc.dram_tensor("y", [T_OWN, D], F32, kind="ExternalOutput").ap()

    a_dram = nc.dram_tensor("a_dram", [T_OWN, D], BF16).ap()
    a_t = [nc.dram_tensor(f"a_t{i}", [D, 512], BF16).ap() for i in range(2)]
    aall = [nc.dram_tensor(f"aall{i}", [4 * D, 512], BF16).ap()
            for i in range(2)]
    q_dram = nc.dram_tensor("q_dram", [S, 1024], BF16).ap()
    k_dram = nc.dram_tensor("k_dram", [S, 256], BF16).ap()
    ctx_in = [nc.dram_tensor(f"ctx_in{i}", [2048, 1024], BF16).ap()
              for i in range(2)]
    ctxall = [nc.dram_tensor(f"ctxall{i}", [8192, 1024], BF16).ap()
              for i in range(2)]
    b_dram = nc.dram_tensor("b_dram", [T_OWN, D], F32).ap()
    x1_dram = nc.dram_tensor("x1_dram", [T_OWN, D], F32).ap()
    x1bf_dram = nc.dram_tensor("x1bf_dram", [T_OWN, D], BF16).ap()
    xpre_dram = nc.dram_tensor("xpre_dram", [T_OWN, 1024], BF16).ap()

    groups = {}

    def sbg(gname, name, shape, dt):
        g = nc.sbuf_tensor(name, shape, dt)
        h = g.__enter__()
        groups.setdefault(gname, []).append(g)
        return h

    def psg(gname, name, shape):
        g = nc.psum_tensor(name, shape, F32)
        h = g.__enter__()
        groups.setdefault(gname, []).append(g)
        return h

    def free(gname):
        for g in reversed(groups.pop(gname, [])):
            g.__exit__(None, None, None)

    P = "persist"
    cvec = sbg(P, "cvec_sb", [128, 8], F32)
    tri = sbg(P, "tri_sb", [128, 128], F32)
    ones_bf = sbg(P, "ones_bf", [128, 128], BF16)
    sqs = sbg(P, "sqs", [128, D], F32)
    m2a = sbg(P, "m2a", [128, 8], F32)
    r2a = sbg(P, "r2a", [128, 8], F32)
    rhoa = sbg(P, "rhoa", [128, 8], F32)
    rhm2 = sbg(P, "rhm2", [128, 8], F32)
    bsum = sbg(P, "bsum", [128, 8], F32)
    bsq = sbg(P, "bsq", [128, 8], F32)
    mba = sbg(P, "mba", [128, 8], F32)
    rba = sbg(P, "rba", [128, 8], F32)
    v1 = sbg(P, "v1", [128, 1], F32)
    v2 = sbg(P, "v2", [128, 1], F32)
    v3 = sbg(P, "v3", [128, 1], F32)
    vA1 = sbg(P, "vA1", [128, 8], F32)
    vA2 = sbg(P, "vA2", [128, 8], F32)
    vA3 = sbg(P, "vA3", [128, 8], F32)
    Hlog = sbg(P, "Hlog", [128, 8, 24], F32)
    M16 = sbg(P, "M16", [128, 8, 16], F32)
    t16 = sbg(P, "t16", [128, 8, 16], F32)
    zmx = sbg(P, "zmx", [128, 8, 4], F32)
    zmx2 = sbg(P, "zmx2", [128, 8, 1], F32)
    aS = sbg(P, "aS", [128, 8, 4], F32)
    bS = sbg(P, "bS", [128, 8, 4], F32)
    Hres = sbg(P, "Hres", [128, 8, 16], F32)
    Hpre = sbg(P, "Hpre", [128, 8, 4], F32)
    Hpost = sbg(P, "Hpost", [128, 8, 4], F32)
    u24 = sbg(P, "u24", [128, 24], F32)
    csb = sbg(P, "csb", [128, 24], F32)
    bbb = sbg(P, "bbb", [128, 24], F32)
    cs1 = sbg(P, "cs1", [1, 24], BF16)
    g11 = sbg(P, "g11", [1, 1], F32)
    g11b = sbg(P, "g11b", [1, 1], BF16)
    gcol = sbg(P, "gcol", [128, 1], F32)
    gm1 = sbg(P, "gm1", [128, 1], F32)
    bc1b = sbg(P, "bc1b", [1, 24], BF16)
    cidx_sb = sbg(P, "cidx_sb", [128, 32], mybir.dt.int32)

    C = {"eps1": cvec[:, 1:2], "eps2": cvec[:, 2:3], "inv": cvec[:, 3:4],
         "shift": cvec[:, 0:1]}

    GV = "gv"
    V_sb = sbg(GV, "V_sb", [128, 2, NKB, 128], BF16)
    GW = "gw"
    cos_sb = sbg(GW, "cos_sb", [128, TT_ALL, HD], BF16)
    sin_sb = sbg(GW, "sin_sb", [128, TT_ALL, HD], BF16)
    Wq_sb = sbg(GW, "Wq_sb", [128, 32, 1024], BF16)
    Wk_sb = sbg(GW, "Wk_sb", [128, 32, 256], BF16)
    Wv_sb = sbg(GW, "Wv_sb", [128, 32, 256], BF16)
    GA = "ga"
    x0t = [sbg(GA, f"x0t{i}", [128, D], F32) for i in range(2)]
    at = [sbg(GA, f"at{i}", [128, D], BF16) for i in range(2)]
    aFMt = [sbg(GA, f"aFMt{i}", [128, T_OWN], BF16) for i in range(4)]

    def emit_dma(qname, semname, out, in_, pre_wait=None):
        tok = bld.inc(semname, 16)

        @bld.on(qname)
        def _(e, out=out, in_=in_, tok=tok, pre_wait=pre_wait):
            if pre_wait is not None:
                for _t in (pre_wait if isinstance(pre_wait, list) else [pre_wait]):
                    W(e, _t)
            e.dma_start(out=out, in_=in_).then_inc(tok[0], 16)
        return (tok[0], tok[1])

    emit_dma("sync", "cst", cvec[:], cvech)
    emit_dma("sync", "cst", tri[:], trih)
    emit_dma("sync", "cst", g11[:], gateh)
    t_cst = emit_dma("sync", "cst", cidx_sb[:], cidxh)
    t_cvec = t_tri = t_g = t_cidx = t_cst
    emit_dma("gpsimd", "cstg", ones_bf[:], onesh)
    emit_dma("gpsimd", "cstg", cos_sb[:],
             cosh.rearrange("(t p) d -> p t d", p=128))
    emit_dma("gpsimd", "cstg", sin_sb[:],
             sinh.rearrange("(t p) d -> p t d", p=128))
    emit_dma("gpsimd", "cstg", Wq_sb[:],
             Wqh.rearrange("(i p) n -> p i n", p=128))
    emit_dma("gpsimd", "cstg", Wk_sb[:],
             Wkh.rearrange("(i p) n -> p i n", p=128))
    t_cstg = emit_dma("gpsimd", "cstg", Wv_sb[:],
                      Wvh.rearrange("(i p) n -> p i n", p=128))
    t_ones = t_cos = t_sin = t_wq = t_wk = t_wv = t_cstg

    GPB = "gpb"
    pq = [psg(GPB, f"pq{i}", [128, 1024]) for i in range(2)]
    pk = [psg(GPB, f"pk{i}", [128, 256]) for i in range(2)]
    pv = [psg(GPB, f"pv{i}", [128, 256]) for i in range(2)]

    a_norm = []
    a_sqred = []
    a_wr = []
    for t in range(NTILE):
        s1 = vA1[:, t:t + 1]
        s2 = vA2[:, t:t + 1]
        s3 = vA3[:, t:t + 1]
        tok_x0 = bld.inc("x0l", 16)

        @bld.on("scalar")
        def _(e, t=t, tok_x0=tok_x0,
              prev=(a_norm[t - 2] if t >= 2 else None)):
            if prev is not None:
                W(e, prev)
            e.dma_start(out=x0t[t % 2][:], in_=x0h[128 * t:128 * (t + 1), :]
                        ).then_inc(tok_x0[0], 16)
        tok_x0 = (tok_x0[0], tok_x0[1])

        tok_sq = bld.inc("ac")

        @bld.on("scalar")
        def _(e, t=t, tok_x0=tok_x0, tok_sq=tok_sq,
              prev_red=(a_sqred[t - 1] if t >= 1 else None)):
            W(e, tok_x0)
            if prev_red is not None:
                W(e, prev_red)
            e.activation(out=sqs[:], in_=x0t[t % 2][:], func=AF.Square
                         ).then_inc(tok_sq[0], 1)
        tok_sq = (tok_sq[0], tok_sq[1])

        tok_red = bld.inc("dv")
        tok_st = bld.inc("dv")

        @bld.on("vector")
        def _(e, t=t, s1=s1, s2=s2, s3=s3, tok_x0=tok_x0, tok_sq=tok_sq,
              tok_st=tok_st, tok_red=tok_red):
            if t == 0:
                W(e, t_cvec)
            W(e, tok_x0)
            e.reduce_sum(s1, x0t[t % 2][:], axis=AXX)
            W(e, tok_sq)
            e.reduce_sum(s2, sqs[:], axis=AXX).then_inc(tok_red[0], 1)
            bld.bar(e, e.tensor_scalar(out=s1, in0=s1, scalar1=C["inv"],
                                       scalar2=None, op0=ALU.mult), "dvb")
            bld.bar(e, e.tensor_mul(s3, s1, s1), "dvb")
            e.scalar_tensor_tensor(out=s2, in0=s2, scalar=C["inv"],
                                   in1=s3, op0=ALU.mult, op1=ALU.subtract
                                   ).then_inc(tok_st[0], 1)
        tok_st = (tok_st[0], tok_st[1])
        a_sqred.append((tok_red[0], tok_red[1]))

        tok_r = bld.inc("ac")

        @bld.on("scalar")
        def _(e, t=t, s2=s2, s3=s3, tok_st=tok_st, tok_r=tok_r):
            W(e, tok_st)
            bld.bar(e, e.activation(out=s3, in_=s2, func=AF.Ln,
                                    bias=C["eps1"]), "acb")
            e.activation(out=s3, in_=s3, func=AF.Exp, scale=-0.5
                         ).then_inc(tok_r[0], 1)
        tok_r = (tok_r[0], tok_r[1])

        tok_n = bld.inc("dv")

        @bld.on("vector")
        def _(e, t=t, s1=s1, s3=s3, tok_r=tok_r, tok_n=tok_n,
              prev_wr=(a_wr[t - 2] if t >= 2 else None)):
            W(e, tok_r)
            if prev_wr is not None:
                W(e, prev_wr)
            e.tensor_scalar(out=at[t % 2][:], in0=x0t[t % 2][:], scalar1=s1,
                            scalar2=s3, op0=ALU.subtract, op1=ALU.mult
                            ).then_inc(tok_n[0], 1)
        tok_n = (tok_n[0], tok_n[1])
        a_norm.append(tok_n)

        t_aw = emit_dma("sync", "aw", a_dram[128 * t:128 * (t + 1), :],
                        at[t % 2][:], pre_wait=tok_n)
        a_wr.append(t_aw)
    t_a_all = a_wr[-1]

    w_toks = []
    for c in range(32):
        tok_T = bld.inc(f"a2t{c % 4}", 16)

        @bld.on("sync")
        def _(e, c=c, tok_T=tok_T,
              prev_w=(w_toks[c - 4] if c >= 4 else None)):
            if c == 0:
                W(e, t_a_all)
            if prev_w is not None:
                W(e, prev_w)
            e.dma_start(out=aFMt[c % 4][:],
                        in_=a_dram[:, 128 * c:128 * (c + 1)],
                        transpose=True).then_inc(tok_T[0], 16)
        tok_T = (tok_T[0], tok_T[1])
        emit_dma("scalar", f"a2w{c % 4}", a_t[0][128 * c:128 * (c + 1), :],
                 aFMt[c % 4][:, 0:512], pre_wait=tok_T)
        t_w = emit_dma("scalar", f"a2x{c % 4}",
                       a_t[1][128 * c:128 * (c + 1), :],
                       aFMt[c % 4][:, 512:1024])
        w_toks.append(t_w)
    t_at_all = w_toks[-1]

    tok_cc1a = bld.inc("CC")
    tok_cc1b = bld.inc("CC")

    @bld.on("gpsimd")
    def _(e, tok_cc1a=tok_cc1a, tok_cc1b=tok_cc1b, w_fin=tuple(w_toks[-4:])):
        for wf in w_fin:
            W(e, wf)
        e.collective_compute(
            "AllGather", ALU.bypass,
            replica_groups=[[0, 1, 2, 3], [4, 5, 6, 7]],
            ins=[a_t[0]], outs=[aall[0]],
        ).then_inc(tok_cc1a[0], 1)
        e.collective_compute(
            "AllGather", ALU.bypass,
            replica_groups=[[0, 1, 2, 3], [4, 5, 6, 7]],
            ins=[a_t[1]], outs=[aall[1]],
        ).then_inc(tok_cc1b[0], 1)
    tok_cc1a = (tok_cc1a[0], tok_cc1a[1])
    tok_cc1b = (tok_cc1b[0], tok_cc1b[1])

    free("ga")
    GB = "gb"
    attb = [sbg(GB, f"attb{i}", [128, 32, 128], BF16) for i in range(4)]
    ro1 = sbg(GB, "ro1", [128, 1024], F32)
    ro2 = sbg(GB, "ro2", [128, 512], F32)
    qbf = [sbg(GB, f"qbf{i}", [128, 1024], BF16) for i in range(4)]
    kbf = [sbg(GB, f"kbf{i}", [128, 256], BF16) for i in range(4)]

    aall_r = [a.rearrange("(r c p) t -> r p c t", r=4, c=32, p=128)
              for a in aall]
    rope_done, vev_done, qk_wr, pe_toks = [], [], [], []
    tt_order = [(half, r, lt) for half in range(2) for r in range(4)
                for lt in range(4)]
    for i, (half, r_src, lt) in enumerate(tt_order):
        tt = r_src * 8 + half * 4 + lt
        t0 = 128 * lt
        tok_ab = bld.inc(f"ab{i % 4}", 16)

        @bld.on("gpsimd")
        def _(e, i=i, half=half, r_src=r_src, t0=t0, tok_ab=tok_ab,
              prev=(pe_toks[i - 4] if i >= 4 else None)):
            if i == 0:
                W(e, tok_cc1a)
            if i == 16:
                W(e, tok_cc1b)
            if prev is not None:
                W(e, prev)
            e.dma_start(out=attb[i % 4][:],
                        in_=aall_r[half][r_src, :, :, t0:t0 + 128]
                        ).then_inc(tok_ab[0], 16)
        tok_ab = (tok_ab[0], tok_ab[1])

        tok_pe = bld.inc("pe")

        @bld.on("tensor")
        def _(e, i=i, tok_ab=tok_ab, tok_pe=tok_pe,
              prev_ro=(rope_done[i - 2] if i >= 2 else None),
              prev_ve=(vev_done[i - 2] if i >= 2 else None)):
            if i == 0:
                W(e, t_wq); W(e, t_wk); W(e, t_wv)
            W(e, tok_ab)
            if prev_ro is not None:
                W(e, prev_ro)
            if prev_ve is not None:
                W(e, prev_ve)
            for c in range(32):
                lhs = attb[i % 4][:, c, :]
                e.matmul(pq[i % 2][:, 0:512], lhs, Wq_sb[:, c, 0:512],
                         start=(c == 0), stop=(c == 31))
                e.matmul(pq[i % 2][:, 512:1024], lhs, Wq_sb[:, c, 512:1024],
                         start=(c == 0), stop=(c == 31))
                e.matmul(pk[i % 2][:], lhs, Wk_sb[:, c, :],
                         start=(c == 0), stop=(c == 31))
                mm = e.matmul(pv[i % 2][:], lhs, Wv_sb[:, c, :],
                              start=(c == 0), stop=(c == 31))
            mm.then_inc(tok_pe[0], 1)
        tok_pe = (tok_pe[0], tok_pe[1])
        pe_toks.append(tok_pe)

        tok_ve = bld.inc("ac")

        @bld.on("scalar")
        def _(e, i=i, tt=tt, tok_pe=tok_pe, tok_ve=tok_ve):
            W(e, tok_pe)
            e.copy(out=V_sb[:, 0, tt, :], in_=pv[i % 2][:, 0:128])
            e.copy(out=V_sb[:, 1, tt, :], in_=pv[i % 2][:, 128:256]
                   ).then_inc(tok_ve[0], 1)
        vev_done.append((tok_ve[0], tok_ve[1]))

        tok_ro = bld.inc("dv")

        @bld.on("vector")
        def _(e, i=i, tt=tt, tok_pe=tok_pe, tok_ro=tok_ro,
              prev_w=(qk_wr[i - 4] if i >= 4 else None)):
            if i == 0:
                W(e, t_cos); W(e, t_sin)
            W(e, tok_pe)
            if prev_w is not None:
                W(e, prev_w[0])
                W(e, prev_w[1])
            cosv = cos_sb[:, tt:tt + 1, :]
            sinv = sin_sb[:, tt:tt + 1, :]
            for (ps, nh, obf) in ((pq[i % 2], 8, qbf[i % 4]),
                                  (pk[i % 2], 2, kbf[i % 4])):
                pvv = ps[:].rearrange("p (h d) -> p h d", h=nh)
                ov = obf[:].rearrange("p (h d) -> p h d", h=nh)
                t1v = ro1[:, 0:nh * 128].rearrange("p (h d) -> p h d", h=nh)
                t2v = ro2[:, 0:nh * 64].rearrange("p (h d) -> p h d", h=nh)
                cb = cosv.broadcast_to([128, nh, HD])
                sbc = sinv.broadcast_to([128, nh, HD])
                e.tensor_mul(t1v, pvv, cb)
                e.tensor_mul(t2v, pvv[:, :, 64:128], sbc[:, :, 0:64])
                e.tensor_sub(ov[:, :, 0:64], t1v[:, :, 0:64], t2v)
                e.tensor_mul(t2v, pvv[:, :, 0:64], sbc[:, :, 64:128])
                mm = e.tensor_add(ov[:, :, 64:128], t1v[:, :, 64:128], t2v)
            mm.then_inc(tok_ro[0], 1)
        tok_ro = (tok_ro[0], tok_ro[1])
        rope_done.append(tok_ro)

        t_qw = emit_dma("sync", f"qw{i % 4}",
                        q_dram[128 * tt:128 * (tt + 1), :],
                        qbf[i % 4][:], pre_wait=tok_ro)
        t_kw = emit_dma("sync", f"kw{i % 4}",
                        k_dram[128 * tt:128 * (tt + 1), :],
                        kbf[i % 4][:])
        qk_wr.append((t_qw, t_kw))
    t_b_done_all = [t for pair in qk_wr[-4:] for t in pair]

    free("gb"); free("gw"); free("gpb")
    GCF = "gcf"
    ctxFM = sbg(GCF, "ctxFM", [128, 32, 1024], BF16)
    GC = "gc"
    qFM = sbg(GC, "qFM", [128, 8, S], BF16)
    KFM = sbg(GC, "KFM", [128, 2, S], BF16)
    PT = [sbg(GC, f"PT{i}", [128, 1024], BF16) for i in range(4)]
    lnl = sbg(GC, "lnl", [128, 512], F32)
    rl = [sbg(GC, f"rl{i}", [128, 512], F32) for i in range(2)]
    ctxev = [sbg(GC, f"ctxev{i}", [128, 512], BF16) for i in range(2)]
    GPC = "gpc"
    psS = [psg(GPC, f"psS{i}", [128, 1024]) for i in range(2)]
    pctx = [psg(GPC, f"pctx{i}", [128, 512]) for i in range(2)]
    pl = [psg(GPC, f"pl{i}", [128, 512]) for i in range(2)]

    tp_q, tp_k = {}, {}
    _fm_first = [True]

    def emit_wave(kind, idx):
        for rb in range(4):
            tok = bld.inc("fmt", 16)

            @bld.on("sync")
            def _(e, kind=kind, idx=idx, rb=rb, tok=tok,
                  first=_fm_first[0]):
                if first:
                    for tk in t_b_done_all:
                        W(e, tk)
                if kind == "q":
                    e.dma_start(out=qFM[:, idx, 1024 * rb:1024 * (rb + 1)],
                                in_=q_dram[1024 * rb:1024 * (rb + 1),
                                           128 * idx:128 * (idx + 1)],
                                transpose=True).then_inc(tok[0], 16)
                else:
                    e.dma_start(out=KFM[:, idx, 1024 * rb:1024 * (rb + 1)],
                                in_=k_dram[1024 * rb:1024 * (rb + 1),
                                           128 * idx:128 * (idx + 1)],
                                transpose=True).then_inc(tok[0], 16)
            _fm_first[0] = False
            if kind == "q":
                tp_q[idx] = (tok[0], tok[1])
            else:
                tp_k[idx] = (tok[0], tok[1])

    emit_wave("k", 0)
    emit_wave("q", 0)

    it = 0
    evac_dv, lnl_ac, ctx_dmas = [], [], []
    for u in range(8):
        b_ = u // 4
        if u == 0:
            emit_wave("q", 1)
        elif u == 3:
            emit_wave("k", 1)
            emit_wave("q", 4)
        elif u < 7:
            emit_wave("q", u + 1)
        head_first = [True]
        for qc in (7, 0, 6, 1, 5, 2, 4, 3):
            par = it % 2
            n_kb = 4 * qc + 4
            npairs = (n_kb + 1) // 2
            sE, lctx_toks = [], []

            def emit_lctx(p, kbs):
                tok_l = bld.inc("pe")

                @bld.on("tensor")
                def _(e, u=u, qc=qc, b_=b_, p=p, kbs=tuple(kbs), par=par,
                      n_kb=n_kb, tok_l=tok_l, tok_e=sE[p]):
                    W(e, tok_e)
                    for kb in kbs:
                        d = kb - 4 * qc
                        off = 128 * d if d >= 0 else 0
                        col0 = 512 * (kb % 2)
                        pts = PT[p % 4][:, col0 + off:col0 + 512]
                        e.matmul(pl[par][:, off:512], ones_bf[:], pts,
                                 start=(kb == 0), stop=(kb == n_kb - 1))
                        mm = e.matmul(pctx[par][:, off:512],
                                      V_sb[:, b_, kb, :], pts,
                                      start=(kb == 0), stop=(kb == n_kb - 1))
                    mm.then_inc(tok_l[0], 1)
                lctx_toks.append((tok_l[0], tok_l[1]))

            for p in range(npairs):
                kbs = [kb for kb in (2 * p, 2 * p + 1) if kb < n_kb]
                tok_s = bld.inc("pe")

                @bld.on("tensor")
                def _(e, u=u, qc=qc, b_=b_, p=p, kbs=tuple(kbs), it=it,
                      par=par, tok_s=tok_s,
                      s_free=(sE[p - 2] if p >= 2 else None),
                      first=(head_first[0] and p == 0),
                      prev_ev=(evac_dv[it - 2] if (p == 0 and it >= 2) else None),
                      prev_ln=(lnl_ac[it - 2] if (p == 0 and it >= 2) else None),
                      head_w=((tp_q[u], tp_k[u // 4],
                               vev_done[-1]) if (head_first[0] and p == 0)
                              else None)):
                    if first and head_w is not None:
                        for hw in head_w:
                            W(e, hw)
                    if prev_ev is not None:
                        W(e, prev_ev)
                    if prev_ln is not None:
                        W(e, prev_ln)
                    if s_free is not None:
                        W(e, s_free)
                    for kb in kbs:
                        d = kb - 4 * qc
                        off = 128 * d if d >= 0 else 0
                        col0 = 512 * (kb % 2)
                        mm = e.matmul(
                            psS[p % 2][:, col0 + off:col0 + 512],
                            KFM[:, b_, 128 * kb:128 * (kb + 1)],
                            qFM[:, u, 512 * qc + off:512 * qc + 512],
                            start=True, stop=True)
                    mm.then_inc(tok_s[0], 1)
                tok_s = (tok_s[0], tok_s[1])
                head_first[0] = False

                has_diag = any(kb - 4 * qc >= 0 for kb in kbs)
                tok_m = None
                if has_diag:
                    tok_mi = bld.inc("dv")

                    @bld.on("vector")
                    def _(e, qc=qc, p=p, kbs=tuple(kbs), tok_mi=tok_mi,
                          tok_s=tok_s, first=_fm_first[0] or (it < 2)):
                        if first:
                            W(e, t_tri)
                        W(e, tok_s)
                        mm = None
                        for kb in kbs:
                            d = kb - 4 * qc
                            if d < 0:
                                continue
                            col0 = 512 * (kb % 2) + 128 * d
                            mm = e.tensor_add(psS[p % 2][:, col0:col0 + 128],
                                              psS[p % 2][:, col0:col0 + 128],
                                              tri[:])
                        mm.then_inc(tok_mi[0], 1)
                    tok_m = (tok_mi[0], tok_mi[1])

                tok_e = bld.inc("ac")

                @bld.on("scalar")
                def _(e, qc=qc, p=p, kbs=tuple(kbs), tok_e=tok_e,
                      tok_s=tok_s, tok_m=tok_m,
                      pt_free=(lctx_toks[p - 4] if p >= 4 else None)):
                    W(e, tok_s)
                    if tok_m is not None:
                        W(e, tok_m)
                    if pt_free is not None:
                        W(e, pt_free)
                    full = (len(kbs) == 2 and
                            all(kb - 4 * qc < 0 for kb in kbs))
                    if full:
                        mm = e.activation(out=PT[p % 4][:], in_=psS[p % 2][:],
                                          func=AF.Exp, scale=SCALE,
                                          bias=C["shift"])
                    else:
                        for kb in kbs:
                            d = kb - 4 * qc
                            off = 128 * d if d >= 0 else 0
                            col0 = 512 * (kb % 2)
                            mm = e.activation(
                                out=PT[p % 4][:, col0 + off:col0 + 512],
                                in_=psS[p % 2][:, col0 + off:col0 + 512],
                                func=AF.Exp, scale=SCALE, bias=C["shift"])
                    mm.then_inc(tok_e[0], 1)
                sE.append((tok_e[0], tok_e[1]))

                if p >= 1:
                    emit_lctx(p - 1, [kb for kb in (2 * (p - 1), 2 * p - 1)
                                      if kb < n_kb])
            emit_lctx(npairs - 1, [kb for kb in (2 * npairs - 2, 2 * npairs - 1)
                                   if kb < n_kb])

            tok_ln = bld.inc("ac", 2)

            @bld.on("scalar")
            def _(e, par=par, tok_ln=tok_ln, last=lctx_toks[-1]):
                W(e, last)
                e.activation(out=lnl[:], in_=pl[par][:], func=AF.Ln)
                e.activation(out=rl[par][:], in_=lnl[:], func=AF.Exp,
                             scale=-1.0).then_inc(tok_ln[0], 2)
            tok_ln = (tok_ln[0], tok_ln[1])
            lnl_ac.append(tok_ln)

            tok_ev = bld.inc("dv")

            @bld.on("vector")
            def _(e, par=par, tok_ev=tok_ev, tok_ln=tok_ln,
                  prev_dma=(ctx_dmas[it - 2] if it >= 2 else None)):
                W(e, tok_ln)
                if prev_dma is not None:
                    W(e, prev_dma)
                e.tensor_mul(ctxev[par][:], pctx[par][:], rl[par][:]
                             ).then_inc(tok_ev[0], 1)
            tok_ev = (tok_ev[0], tok_ev[1])
            evac_dv.append(tok_ev)

            row0 = 512 * (qc // 2) + 128 * (u % 4)
            col0 = 512 * (qc % 2)
            t_cd = emit_dma("sync", f"cx{par}",
                            ctx_in[u // 4][row0:row0 + 128, col0:col0 + 512],
                            ctxev[par][:], pre_wait=tok_ev)
            ctx_dmas.append(t_cd)
            it += 1
        if u == 3:
            tok_cc2a = bld.inc("CC")

            @bld.on("gpsimd")
            def _(e, tok_cc2a=tok_cc2a,
                  cx_fin=(ctx_dmas[-1], ctx_dmas[-2])):
                W(e, cx_fin[0])
                W(e, cx_fin[1])
                e.collective_compute(
                    "AllGather", ALU.bypass,
                    replica_groups=[[0, 1, 2, 3], [4, 5, 6, 7]],
                    ins=[ctx_in[0]], outs=[ctxall[0]],
                ).then_inc(tok_cc2a[0], 1)
            tok_cc2a = (tok_cc2a[0], tok_cc2a[1])
            for h in range(32):
                uu = (h % 4) if h < 16 else 4 + ((h - 16) % 4)
                if uu >= 4:
                    continue
                tok = bld.inc("cf", 16)

                @bld.on("gpsimd")
                def _(e, h=h, tok=tok, tok_cc2a=tok_cc2a):
                    if h == 0:
                        W(e, tok_cc2a)
                        W(e, t_cidx)
                    e.indirect_dma_start(
                        out=ctxFM[:, h, :], out_offset=None,
                        in_=ctxall[0],
                        in_offset=IndirectOffsetOnAxis(cidx_sb[:, h:h + 1], 0),
                    ).then_inc(tok[0], 16)
    t_ctx_all = ctx_dmas[-1]

    tok_cc2 = bld.inc("CC")

    @bld.on("gpsimd")
    def _(e, tok_cc2=tok_cc2, cx_fin=(ctx_dmas[-1], ctx_dmas[-2])):
        W(e, cx_fin[0])
        W(e, cx_fin[1])
        e.collective_compute(
            "AllGather", ALU.bypass,
            replica_groups=[[0, 1, 2, 3], [4, 5, 6, 7]],
            ins=[ctx_in[1]], outs=[ctxall[1]],
        ).then_inc(tok_cc2[0], 1)
    tok_cc2 = (tok_cc2[0], tok_cc2[1])

    free("gc"); free("gpc")
    GD = "gd"
    Wo_sb = [sbg(GD, f"Wo_sb{i}", [128, 32, 512], BF16) for i in range(2)]
    btev = [sbg(GD, f"btev{i}", [128, 512], F32) for i in range(2)]
    GPD = "gpd"
    pb = [psg(GPD, f"pb{i}", [128, 512]) for i in range(2)]

    for h in range(32):
        uu = (h % 4) if h < 16 else 4 + ((h - 16) % 4)
        if uu < 4:
            continue
        tok = bld.inc("cf", 16)

        @bld.on("gpsimd")
        def _(e, h=h, tok=tok):
            e.wait_ge(tok_cc2[0], tok_cc2[1])
            e.indirect_dma_start(
                out=ctxFM[:, h, :], out_offset=None,
                in_=ctxall[1],
                in_offset=IndirectOffsetOnAxis(cidx_sb[:, h:h + 1], 0),
            ).then_inc(tok[0], 16)
    t_cf_all = bld.val("cf")

    d_ac, d_dv, d_bw, d_pe_slice = [], [], [], []
    for s_ in range(8):
        tok_wo = bld.inc(f"wo{s_ % 2}", 16)

        @bld.on("gpsimd")
        def _(e, s_=s_, tok_wo=tok_wo,
              prev=(d_pe_slice[s_ - 2] if s_ >= 2 else None)):
            if prev is not None:
                W(e, prev)
            e.dma_start(out=Wo_sb[s_ % 2][:],
                        in_=Woh[:, 512 * s_:512 * (s_ + 1)]
                        .rearrange("(i p) n -> p i n", p=128)
                        ).then_inc(tok_wo[0], 16)
        tok_wo = (tok_wo[0], tok_wo[1])

        for t in range(NTILE):
            sti = s_ * 8 + t
            tok_pe = bld.inc("pe")

            @bld.on("tensor")
            def _(e, s_=s_, t=t, tok_pe=tok_pe, tok_wo=tok_wo,
                  first=(s_ == 0 and t == 0),
                  prev_ac=(d_ac[sti - 2] if sti >= 2 else None),
                  prev_dv=(d_dv[sti - 2] if sti >= 2 else None)):
                if first:
                    W(e, t_cf_all)
                if t == 0:
                    W(e, tok_wo)
                if prev_ac is not None:
                    W(e, prev_ac)
                if prev_dv is not None:
                    W(e, prev_dv)
                for i in range(32):
                    mm = e.matmul(pb[t % 2][:],
                                  ctxFM[:, i, 128 * t:128 * (t + 1)],
                                  Wo_sb[s_ % 2][:, i, :],
                                  start=(i == 0), stop=(i == 31))
                mm.then_inc(tok_pe[0], 1)
            tok_pe = (tok_pe[0], tok_pe[1])
            if t == NTILE - 1:
                d_pe_slice.append(tok_pe)

            tok_ac = bld.inc("ac", 2)

            @bld.on("scalar")
            def _(e, t=t, tok_pe=tok_pe, tok_ac=tok_ac,
                  prev_bw=(d_bw[sti - 2] if sti >= 2 else None),
                  prev_dv=(d_dv[sti - 1] if sti >= 1 else None)):
                W(e, tok_pe)
                if prev_bw is not None:
                    W(e, prev_bw)
                if prev_dv is not None:
                    W(e, prev_dv)
                e.copy(out=btev[t % 2][:], in_=pb[t % 2][:])
                e.activation(out=sqs[:, 0:512], in_=pb[t % 2][:],
                             func=AF.Square).then_inc(tok_ac[0], 2)
            tok_ac = (tok_ac[0], tok_ac[1])
            d_ac.append(tok_ac)

            tok_dv = bld.inc("dv")

            @bld.on("vector")
            def _(e, s_=s_, t=t, tok_pe=tok_pe, tok_ac=tok_ac, tok_dv=tok_dv):
                W(e, tok_ac)
                W(e, tok_pe)
                e.reduce_sum(v1[:], pb[t % 2][:], axis=AXX)
                e.reduce_sum(v2[:], sqs[:, 0:512], axis=AXX)
                if s_ == 0:
                    e.tensor_copy(out=bsum[:, t:t + 1], in_=v1[:])
                    mm = e.tensor_copy(out=bsq[:, t:t + 1], in_=v2[:])
                else:
                    e.tensor_add(bsum[:, t:t + 1], bsum[:, t:t + 1], v1[:])
                    mm = e.tensor_add(bsq[:, t:t + 1], bsq[:, t:t + 1], v2[:])
                mm.then_inc(tok_dv[0], 1)
            d_dv.append((tok_dv[0], tok_dv[1]))

            t_bw = emit_dma("sync", f"bw{t % 2}",
                            b_dram[128 * t:128 * (t + 1),
                                   512 * s_:512 * (s_ + 1)],
                            btev[t % 2][:], pre_wait=tok_ac)
            d_bw.append(t_bw)
    t_b_last = [d_bw[-1], d_bw[-2]]

    tok_bst = bld.inc("dv")

    @bld.on("vector")
    def _(e, tok_bst=tok_bst, last=d_dv[-1]):
        W(e, last)
        bld.bar(e, e.tensor_scalar(out=mba[:], in0=bsum[:],
                                   scalar1=1.0 / 4096.0, scalar2=None,
                                   op0=ALU.mult), "dvb")
        bld.bar(e, e.tensor_mul(u24[:, 0:8], mba[:], mba[:]), "dvb")
        e.scalar_tensor_tensor(out=rba[:], in0=bsq[:], scalar=1.0 / 4096.0,
                               in1=u24[:, 0:8], op0=ALU.mult,
                               op1=ALU.subtract).then_inc(tok_bst[0], 1)
    tok_bst = (tok_bst[0], tok_bst[1])
    tok_bst2 = bld.inc("ac")

    @bld.on("scalar")
    def _(e, tok_bst=tok_bst, tok_bst2=tok_bst2):
        W(e, tok_bst)
        bld.bar(e, e.activation(out=rba[:], in_=rba[:], func=AF.Ln,
                                bias=C["eps1"]), "acb")
        e.activation(out=rba[:], in_=rba[:], func=AF.Exp, scale=-0.5
                     ).then_inc(tok_bst2[0], 1)
    tok_bst2 = (tok_bst2[0], tok_bst2[1])

    free("gd"); free("gcf"); free("gv"); free("gpd")
    WcT = sbg("gwc", "WcT", [128, 32, 24], BF16)
    GE = "ge"
    bfull = sbg(GE, "bfull", [128, D], F32)
    x0e = sbg(GE, "x0e", [128, D], F32)
    x1t = sbg(GE, "x1t", [128, D], F32)
    x1bft = sbg(GE, "x1bft", [128, D], BF16)
    emit_dma("gpsimd", "wct", WcT[:],
             Wcth.rearrange("(i p) j -> p i j", p=128),
             pre_wait=[d_pe_slice[-1]] + t_b_last)
    t_wct = emit_dma("gpsimd", "wct", bc1b[:], bch)
    t_bc = t_wct

    e_x1w, e_bfw = [], []
    for t in range(NTILE):
        pw = None
        if t == 0:
            pw = t_b_last
        elif t >= 1:
            pw = e_bfw[t - 1]
        t_bl = emit_dma("scalar", "ebl", bfull[:],
                        b_dram[128 * t:128 * (t + 1), :],
                        pre_wait=pw)
        t_xl = emit_dma("scalar", "exl", x0e[:],
                        x0h[128 * t:128 * (t + 1), :])

        tok_x1 = bld.inc("dv")

        @bld.on("vector")
        def _(e, t=t, t_bl=t_bl, t_xl=t_xl, tok_x1=tok_x1, tok_bst2=tok_bst2,
              prev_w=(e_x1w[t - 1] if t >= 1 else None)):
            if t == 0:
                W(e, tok_bst2)
            W(e, t_bl)
            if prev_w is not None:
                W(e, prev_w)
            e.tensor_scalar(out=bfull[:], in0=bfull[:], scalar1=mba[:, t:t + 1],
                            scalar2=rba[:, t:t + 1], op0=ALU.subtract,
                            op1=ALU.mult)
            W(e, t_xl)
            e.tensor_add(x1t[:], x0e[:], bfull[:])
            e.reduce_sum(v1[:], x1t[:], axis=AXX)
            e.tensor_scalar(out=v1[:], in0=v1[:], scalar1=C["inv"],
                            scalar2=None, op0=ALU.mult)
            e.tensor_copy(out=x1bft[:], in_=x1t[:]).then_inc(tok_x1[0], 1)
        tok_x1 = (tok_x1[0], tok_x1[1])

        t_x1w = emit_dma("scalar", "x1w", x1_dram[128 * t:128 * (t + 1), :],
                         x1t[:], pre_wait=tok_x1)
        t_bfw = emit_dma("scalar", "bfw", x1bf_dram[128 * t:128 * (t + 1), :],
                         x1bft[:])
        e_x1w.append(t_x1w)
        e_bfw.append(t_bfw)

        tok_sq = bld.inc("ac")

        @bld.on("scalar")
        def _(e, t=t, tok_x1=tok_x1, tok_sq=tok_sq):
            W(e, tok_x1)
            e.activation(out=sqs[:], in_=x1t[:], func=AF.Square
                         ).then_inc(tok_sq[0], 1)
        tok_sq = (tok_sq[0], tok_sq[1])

        tok_f1 = bld.inc("dv")

        @bld.on("vector")
        def _(e, t=t, tok_sq=tok_sq, tok_f1=tok_f1):
            W(e, tok_sq)
            e.reduce_sum(v2[:], sqs[:], axis=AXX)
            e.tensor_copy(out=m2a[:, t:t + 1], in_=v1[:])
            bld.bar(e, e.tensor_mul(v3[:], v1[:], v1[:]), "dvb")
            e.scalar_tensor_tensor(out=v2[:], in0=v2[:], scalar=C["inv"],
                                   in1=v3[:], op0=ALU.mult, op1=ALU.subtract
                                   ).then_inc(tok_f1[0], 1)
        tok_f1 = (tok_f1[0], tok_f1[1])

        tok_f2 = bld.inc("ac")

        @bld.on("scalar")
        def _(e, tok_f1=tok_f1, tok_f2=tok_f2):
            W(e, tok_f1)
            bld.bar(e, e.activation(out=v3[:], in_=v2[:], func=AF.Ln,
                                    bias=C["eps1"]), "acb")
            e.activation(out=v3[:], in_=v3[:], func=AF.Exp, scale=-0.5
                         ).then_inc(tok_f2[0], 1)
        tok_f2 = (tok_f2[0], tok_f2[1])

        tok_f3 = bld.inc("dv")

        @bld.on("vector")
        def _(e, t=t, tok_f2=tok_f2, tok_f3=tok_f3):
            W(e, tok_f2)
            e.tensor_copy(out=r2a[:, t:t + 1], in_=v3[:])
            bld.bar(e, e.tensor_mul(v2[:], v2[:], v3[:]), "dvb")
            e.tensor_mul(v2[:], v2[:], v3[:]).then_inc(tok_f3[0], 1)
        tok_f3 = (tok_f3[0], tok_f3[1])

        tok_f4 = bld.inc("ac")

        @bld.on("scalar")
        def _(e, tok_f3=tok_f3, tok_f4=tok_f4):
            W(e, tok_f3)
            bld.bar(e, e.activation(out=v2[:], in_=v2[:], func=AF.Ln,
                                    bias=C["eps2"]), "acb")
            e.activation(out=v2[:], in_=v2[:], func=AF.Exp, scale=-0.5
                         ).then_inc(tok_f4[0], 1)
        tok_f4 = (tok_f4[0], tok_f4[1])

        tok_f5 = bld.inc("dv")

        @bld.on("vector")
        def _(e, t=t, tok_f4=tok_f4, tok_f5=tok_f5):
            W(e, tok_f4)
            bld.bar(e, e.tensor_mul(rhoa[:, t:t + 1], r2a[:, t:t + 1],
                                    v2[:]), "dvb")
            e.tensor_mul(rhm2[:, t:t + 1], rhoa[:, t:t + 1], m2a[:, t:t + 1]
                         ).then_inc(tok_f5[0], 1)
    t_x1bf_all = e_bfw[-1]
    t_rho_all = bld.val("dv")

    free("ge")
    GF = "gf"
    x1h = sbg(GF, "x1h", [128, 16, 1024], BF16)
    GPF = "gpf"
    pH = [psg(GPF, f"pH{i}", [128, 24]) for i in range(2)]
    pmisc = psg(GPF, "pmisc", [128, 24])

    tok_cs = bld.inc("pe")

    @bld.on("tensor")
    def _(e, tok_cs=tok_cs, tok_bst2=tok_bst2):
        W(e, t_wct)
        W(e, t_ones)
        W(e, tok_bst2)
        for i in range(32):
            mm = e.matmul(pmisc[0:1, :], ones_bf[:, 0:1], WcT[:, i, :],
                          start=(i == 0), stop=(i == 31))
        mm.then_inc(tok_cs[0], 1)
    tok_cs = (tok_cs[0], tok_cs[1])

    tok_cse = bld.inc("ac")

    @bld.on("scalar")
    def _(e, tok_cs=tok_cs, tok_cse=tok_cse):
        W(e, tok_cs)
        e.copy(out=cs1[:], in_=pmisc[0:1, :]).then_inc(tok_cse[0], 1)
    tok_cse = (tok_cse[0], tok_cse[1])

    tok_bcast = bld.inc("pe")

    @bld.on("tensor")
    def _(e, tok_cse=tok_cse, tok_bcast=tok_bcast):
        W(e, tok_cse)
        W(e, t_bc)
        e.matmul(pmisc[:], ones_bf[0:1, :], cs1[:], start=True, stop=True)
        e.matmul(pH[0][:], ones_bf[0:1, :], bc1b[:], start=True, stop=True
                 ).then_inc(tok_bcast[0], 1)
    tok_bcast = (tok_bcast[0], tok_bcast[1])

    tok_bce = bld.inc("ac", 2)

    @bld.on("scalar")
    def _(e, tok_bcast=tok_bcast, tok_bce=tok_bce):
        W(e, tok_bcast)
        e.copy(out=csb[:], in_=pmisc[:])
        e.copy(out=bbb[:], in_=pH[0][:]).then_inc(tok_bce[0], 2)
    tok_bce = (tok_bce[0], tok_bce[1])

    h_pe_half = []
    h_ev_toks = []
    for half in range(2):
        tr_last = None
        for i in range(16):
            tok = bld.inc("x1h", 16)

            @bld.on("scalar")
            def _(e, half=half, i=i, tok=tok,
                  pe_prev=(h_pe_half[-1] if (half == 1 and i == 0) else None)):
                if half == 0 and i == 0:
                    W(e, t_x1bf_all)
                    W(e, t_rho_all)
                if pe_prev is not None:
                    W(e, pe_prev)
                c = 16 * half + i
                e.dma_start(out=x1h[:, i, :],
                            in_=x1bf_dram[:, 128 * c:128 * (c + 1)],
                            transpose=True).then_inc(tok[0], 16)
            tr_last = (tok[0], tok[1])
        t_tr_all = tr_last

        for t in range(NTILE):
            hti = half * 8 + t
            tok_pe = bld.inc("pe")

            @bld.on("tensor")
            def _(e, half=half, t=t, tok_pe=tok_pe, t_tr_all=t_tr_all,
                  tok_bce=tok_bce,
                  prev_ev=(h_ev_toks[hti - 2] if hti >= 2 else None)):
                if t == 0:
                    W(e, t_tr_all)
                    if half == 0:
                        W(e, tok_bce)
                if prev_ev is not None:
                    W(e, prev_ev)
                for i in range(16):
                    mm = e.matmul(pH[t % 2][:],
                                  x1h[:, i, 128 * t:128 * (t + 1)],
                                  WcT[:, 16 * half + i, :],
                                  start=(i == 0), stop=(i == 15))
                mm.then_inc(tok_pe[0], 1)
            tok_pe = (tok_pe[0], tok_pe[1])
            if t == NTILE - 1:
                h_pe_half.append(tok_pe)

            tok_he = bld.inc("dv")

            @bld.on("vector")
            def _(e, half=half, t=t, tok_pe=tok_pe, tok_he=tok_he):
                W(e, tok_pe)
                if half == 0:
                    e.tensor_copy(out=Hlog[:, t, :], in_=pH[t % 2][:]
                                  ).then_inc(tok_he[0], 1)
                else:
                    e.tensor_add(Hlog[:, t, :], Hlog[:, t, :], pH[t % 2][:]
                                 ).then_inc(tok_he[0], 1)
            h_ev_toks.append((tok_he[0], tok_he[1]))

    tok_fix = bld.inc("dv")

    @bld.on("vector")
    def _(e, tok_fix=tok_fix, last_ev=h_ev_toks[-1], t_rho_all=t_rho_all):
        W(e, last_ev)
        W(e, t_rho_all)
        for t in range(NTILE):
            bld.bar(e, e.scalar_tensor_tensor(out=u24[:], in0=csb[:],
                                              scalar=rhm2[:, t:t + 1],
                                              in1=bbb[:], op0=ALU.mult,
                                              op1=ALU.subtract), "dvb")
            bld.bar(e, e.tensor_scalar(out=Hlog[:, t, :], in0=Hlog[:, t, :],
                                       scalar1=rhoa[:, t:t + 1], scalar2=None,
                                       op0=ALU.mult), "dvb")
            mm = e.tensor_sub(Hlog[:, t, :], Hlog[:, t, :], u24[:])
            if t < NTILE - 1:
                bld.bar(e, mm, "dvb")
        mm.then_inc(tok_fix[0], 1)
    tok_fix = (tok_fix[0], tok_fix[1])

    Z16 = Hlog[:, :, 0:16]
    Z4 = Z16.rearrange("p t (i j) -> p t i j", i=4)
    M4 = M16[:].rearrange("p t (i j) -> p t i j", i=4)
    T4 = t16[:].rearrange("p t (i j) -> p t i j", i=4)

    tok_g1 = bld.inc("dv")

    @bld.on("vector")
    def _(e, tok_g1=tok_g1, tok_fix=tok_fix):
        W(e, tok_fix)
        bld.bar(e, e.reduce_max(zmx[:], Z4, axis=AXX), "dvb")
        bld.bar(e, e.reduce_max(zmx2[:], zmx[:], axis=AXX), "dvb")
        e.tensor_sub(M16[:], Z16, zmx2[:].broadcast_to([128, 8, 16])
                     ).then_inc(tok_g1[0], 1)
    tok_g1 = (tok_g1[0], tok_g1[1])

    tok_g2 = bld.inc("ac")

    @bld.on("scalar")
    def _(e, tok_g1=tok_g1, tok_g2=tok_g2):
        W(e, tok_g1)
        e.activation(out=M16[:], in_=M16[:], func=AF.Exp, scale=10.0
                     ).then_inc(tok_g2[0], 1)
    tok_g2 = (tok_g2[0], tok_g2[1])

    tok_g3 = bld.inc("dv")

    @bld.on("vector")
    def _(e, tok_g2=tok_g2, tok_g3=tok_g3):
        W(e, tok_g2)
        bld.bar(e, e.tensor_scalar(out=bS[:], in0=Hlog[:, :, 0:4],
                                   scalar1=0.0, scalar2=1.0, op0=ALU.mult,
                                   op1=ALU.add), "dvb")
        for _i in range(50):
            bld.bar(e, e.tensor_mul(
                T4, M4, bS[:].unsqueeze(2).broadcast_to([128, 8, 4, 4])),
                "dvb")
            bld.bar(e, e.reduce_sum(aS[:], T4, axis=AXX), "dvb")
            bld.bar(e, e.reciprocal(aS[:], aS[:]), "dvb")
            bld.bar(e, e.tensor_mul(
                T4, M4, aS[:].unsqueeze(3).broadcast_to([128, 8, 4, 4])),
                "dvb")
            bld.bar(e, e.reduce_sum(bS[:], T4.transpose([0, 1, 3, 2]),
                                    axis=AXX), "dvb")
            bld.bar(e, e.reciprocal(bS[:], bS[:]), "dvb")
        bld.bar(e, e.tensor_mul(
            T4, M4, aS[:].unsqueeze(3).broadcast_to([128, 8, 4, 4])), "dvb")
        e.tensor_mul(Hres[:].rearrange("p t (i j) -> p t i j", i=4), T4,
                     bS[:].unsqueeze(2).broadcast_to([128, 8, 4, 4])
                     ).then_inc(tok_g3[0], 1)
    tok_g3 = (tok_g3[0], tok_g3[1])

    tok_g4 = bld.inc("ac")

    @bld.on("scalar")
    def _(e, tok_g4=tok_g4):
        W(e, t_g)
        e.activation(out=g11[:], in_=g11[:], func=AF.Exp, scale=-1.0
                     ).then_inc(tok_g4[0], 1)
    tok_g4 = (tok_g4[0], tok_g4[1])

    tok_g5 = bld.inc("dv")

    @bld.on("vector")
    def _(e, tok_g4=tok_g4, tok_g5=tok_g5):
        W(e, tok_g4)
        bld.bar(e, e.tensor_scalar(out=g11[:], in0=g11[:], scalar1=1.0,
                                   scalar2=None, op0=ALU.add), "dvb")
        bld.bar(e, e.reciprocal(g11[:], g11[:]), "dvb")
        e.tensor_copy(out=g11b[:], in_=g11[:]).then_inc(tok_g5[0], 1)
    tok_g5 = (tok_g5[0], tok_g5[1])

    tok_g6 = bld.inc("pe")

    @bld.on("tensor")
    def _(e, tok_g5=tok_g5, tok_g6=tok_g6):
        W(e, tok_g5)
        e.matmul(pmisc[:, 0:1], ones_bf[0:1, :], g11b[:], start=True,
                 stop=True).then_inc(tok_g6[0], 1)
    tok_g6 = (tok_g6[0], tok_g6[1])

    tok_g7 = bld.inc("ac")

    @bld.on("scalar")
    def _(e, tok_g6=tok_g6, tok_g7=tok_g7):
        W(e, tok_g6)
        e.copy(out=gcol[:], in_=pmisc[:, 0:1]).then_inc(tok_g7[0], 1)
    tok_g7 = (tok_g7[0], tok_g7[1])

    tok_g8 = bld.inc("dv")

    @bld.on("vector")
    def _(e, tok_g3=tok_g3, tok_g7=tok_g7, tok_g8=tok_g8):
        W(e, tok_g3)
        W(e, tok_g7)
        bld.bar(e, e.tensor_scalar(out=gm1[:], in0=gcol[:], scalar1=-1.0,
                                   scalar2=1.0, op0=ALU.mult, op1=ALU.add),
                "dvb")
        bld.bar(e, e.tensor_scalar(out=Hres[:], in0=Hres[:], scalar1=gcol[:],
                                   scalar2=None, op0=ALU.mult), "dvb")
        diag = Hres[:, :, 0:16:5]
        bld.bar(e, e.tensor_scalar(out=diag, in0=diag, scalar1=gm1[:],
                                   scalar2=None, op0=ALU.add), "dvb")
        for j, (src0, dst) in enumerate(((Hlog[:, :, 16:20], Hpre),
                                         (Hlog[:, :, 20:24], Hpost))):
            bld.bar(e, e.reduce_max(zmx2[:], src0, axis=AXX), "dvb")
            mm = e.tensor_sub(dst[:], src0,
                              zmx2[:].broadcast_to([128, 8, 4]))
            if j == 0:
                bld.bar(e, mm, "dvb")
        mm.then_inc(tok_g8[0], 1)
    tok_g8 = (tok_g8[0], tok_g8[1])

    tok_g9 = bld.inc("ac")

    @bld.on("scalar")
    def _(e, tok_g8=tok_g8, tok_g9=tok_g9):
        W(e, tok_g8)
        bld.bar(e, e.activation(out=Hpre[:], in_=Hpre[:], func=AF.Exp),
                "acb")
        e.activation(out=Hpost[:], in_=Hpost[:], func=AF.Exp
                     ).then_inc(tok_g9[0], 1)
    tok_g9 = (tok_g9[0], tok_g9[1])

    tok_g10 = bld.inc("dv")

    @bld.on("vector")
    def _(e, tok_g9=tok_g9, tok_g10=tok_g10):
        W(e, tok_g9)
        for j, dst in enumerate((Hpre, Hpost)):
            bld.bar(e, e.reduce_sum(zmx2[:], dst[:], axis=AXX), "dvb")
            bld.bar(e, e.reciprocal(zmx2[:], zmx2[:]), "dvb")
            mm = e.tensor_mul(dst[:], dst[:],
                              zmx2[:].broadcast_to([128, 8, 4]))
            if j == 0:
                bld.bar(e, mm, "dvb")
        mm.then_inc(tok_g10[0], 1)
    tok_g10 = (tok_g10[0], tok_g10[1])

    free("gf"); free("gwc"); free("gpf")
    GH = "gh"
    Wbr_sb = sbg(GH, "Wbr_sb", [128, 8, 1024], BF16)
    xprFM = sbg(GH, "xprFM", [128, 8, 1024], BF16)
    dt_sb = sbg(GH, "dt_sb", [128, D], F32)
    x1l = sbg(GH, "x1l", [128, D], F32)
    xp_sb = sbg(GH, "xp_sb", [128, 1024], F32)
    xpb_sb = sbg(GH, "xpb_sb", [128, 1024], BF16)
    off_sb = sbg(GH, "off_sb", [128, D], F32)
    GPH = "gph"
    py = [psg(GPH, f"py{i}", [128, 1024]) for i in range(2)]

    t_wbr = emit_dma("gpsimd", "wbr", Wbr_sb[:],
                     Wbrh.rearrange("(i p) n -> p i n", p=128),
                     pre_wait=h_pe_half[-1])

    xp_dmas, h1_dv = [], []
    for t in range(NTILE):
        t_x1l = emit_dma("scalar", "hx1a", x1l[:],
                         x1_dram[128 * t:128 * (t + 1), :],
                         pre_wait=(h1_dv[t - 1] if t >= 1 else None))
        tok_h1 = bld.inc("dv")

        @bld.on("vector")
        def _(e, t=t, tok_h1=tok_h1, tok_g10=tok_g10, t_x1l=t_x1l,
              prev=(xp_dmas[t - 1] if t >= 1 else None)):
            if t == 0:
                W(e, tok_g10)
            W(e, t_x1l)
            if prev is not None:
                W(e, prev)
            e.tensor_scalar(out=dt_sb[:], in0=x1l[:],
                            scalar1=m2a[:, t:t + 1], scalar2=r2a[:, t:t + 1],
                            op0=ALU.subtract, op1=ALU.mult)
            dv_ = dt_sb[:].rearrange("p (n c) -> p n c", n=4)
            e.tensor_scalar(out=xp_sb[:], in0=dv_[:, 0, :],
                            scalar1=Hpre[:, t, 0:1], scalar2=None,
                            op0=ALU.mult)
            for n in range(1, 4):
                e.scalar_tensor_tensor(out=xp_sb[:], in0=dv_[:, n, :],
                                       scalar=Hpre[:, t, n:n + 1],
                                       in1=xp_sb[:], op0=ALU.mult, op1=ALU.add)
            e.tensor_copy(out=xpb_sb[:], in_=xp_sb[:]).then_inc(tok_h1[0], 1)
        tok_h1 = (tok_h1[0], tok_h1[1])
        h1_dv.append(tok_h1)

        t_xpw = emit_dma("scalar", "xpw",
                         xpre_dram[128 * t:128 * (t + 1), :], xpb_sb[:],
                         pre_wait=tok_h1)
        xp_dmas.append(t_xpw)
    t_xp_all = xp_dmas[-1]

    xtr_last = None
    for c in range(8):
        tok = bld.inc("xpt", 16)

        @bld.on("scalar")
        def _(e, c=c, tok=tok):
            if c == 0:
                W(e, t_xp_all)
            e.dma_start(out=xprFM[:, c, :],
                        in_=xpre_dram[:, 128 * c:128 * (c + 1)],
                        transpose=True).then_inc(tok[0], 16)
        xtr_last = (tok[0], tok[1])
    t_xtr_all = xtr_last

    h2_dv, y_dmas = [], []
    for t in range(NTILE):
        t_x1l2 = emit_dma("scalar", "hx1b", x1l[:],
                          x1_dram[128 * t:128 * (t + 1), :],
                          pre_wait=(h2_dv[t - 1] if t >= 1 else None))
        tok_pe = bld.inc("pe")

        @bld.on("tensor")
        def _(e, t=t, tok_pe=tok_pe,
              prev_dv=(h2_dv[t - 2] if t >= 2 else None)):
            if t == 0:
                W(e, t_xtr_all)
                W(e, t_wbr)
            if prev_dv is not None:
                W(e, prev_dv)
            for c in range(8):
                e.matmul(py[t % 2][:, 0:512],
                         xprFM[:, c, 128 * t:128 * (t + 1)],
                         Wbr_sb[:, c, 0:512], start=(c == 0), stop=(c == 7))
                mm = e.matmul(py[t % 2][:, 512:1024],
                              xprFM[:, c, 128 * t:128 * (t + 1)],
                              Wbr_sb[:, c, 512:1024], start=(c == 0),
                              stop=(c == 7))
            mm.then_inc(tok_pe[0], 1)
        tok_pe = (tok_pe[0], tok_pe[1])

        tok_mix = bld.inc("dv")

        @bld.on("vector")
        def _(e, t=t, tok_pe=tok_pe, tok_mix=tok_mix, t_x1l2=t_x1l2,
              prev_y=(y_dmas[t - 1] if t >= 1 else None)):
            W(e, tok_pe)
            W(e, t_x1l2)
            if prev_y is not None:
                W(e, prev_y)
            e.tensor_scalar(out=dt_sb[:], in0=x1l[:],
                            scalar1=m2a[:, t:t + 1], scalar2=r2a[:, t:t + 1],
                            op0=ALU.subtract, op1=ALU.mult)
            dv_ = dt_sb[:].rearrange("p (n c) -> p n c", n=4)
            ov = off_sb[:].rearrange("p (n c) -> p n c", n=4)
            for n in range(4):
                e.tensor_scalar(out=xp_sb[:], in0=dv_[:, 0, :],
                                scalar1=Hres[:, t, 4 * n:4 * n + 1],
                                scalar2=None, op0=ALU.mult)
                for j in range(1, 4):
                    e.scalar_tensor_tensor(
                        out=xp_sb[:], in0=dv_[:, j, :],
                        scalar=Hres[:, t, 4 * n + j:4 * n + j + 1],
                        in1=xp_sb[:], op0=ALU.mult, op1=ALU.add)
                e.scalar_tensor_tensor(out=ov[:, n, :], in0=py[t % 2][:],
                                       scalar=Hpost[:, t, n:n + 1],
                                       in1=xp_sb[:], op0=ALU.mult, op1=ALU.add)
            e.reduce_sum(v1[:], off_sb[:], axis=AXX)
            e.tensor_scalar(out=v1[:], in0=v1[:], scalar1=C["inv"],
                            scalar2=None, op0=ALU.mult).then_inc(tok_mix[0], 1)
        tok_mix = (tok_mix[0], tok_mix[1])
        h2_dv.append(tok_mix)

        tok_hsq = bld.inc("ac")

        @bld.on("scalar")
        def _(e, tok_mix=tok_mix, tok_hsq=tok_hsq):
            W(e, tok_mix)
            e.activation(out=sqs[:], in_=off_sb[:], func=AF.Square
                         ).then_inc(tok_hsq[0], 1)
        tok_hsq = (tok_hsq[0], tok_hsq[1])

        tok_hf = bld.inc("dv")

        @bld.on("vector")
        def _(e, tok_hsq=tok_hsq, tok_hf=tok_hf):
            W(e, tok_hsq)
            e.reduce_sum(v2[:], sqs[:], axis=AXX)
            bld.bar(e, e.tensor_mul(v3[:], v1[:], v1[:]), "dvb")
            e.scalar_tensor_tensor(out=v2[:], in0=v2[:], scalar=C["inv"],
                                   in1=v3[:], op0=ALU.mult, op1=ALU.subtract
                                   ).then_inc(tok_hf[0], 1)
        tok_hf = (tok_hf[0], tok_hf[1])

        tok_hr = bld.inc("ac")

        @bld.on("scalar")
        def _(e, tok_hf=tok_hf, tok_hr=tok_hr):
            W(e, tok_hf)
            bld.bar(e, e.activation(out=v3[:], in_=v2[:], func=AF.Ln,
                                    bias=C["eps1"]), "acb")
            e.activation(out=v3[:], in_=v3[:], func=AF.Exp, scale=-0.5
                         ).then_inc(tok_hr[0], 1)
        tok_hr = (tok_hr[0], tok_hr[1])

        tok_out = bld.inc("dv")

        @bld.on("vector")
        def _(e, t=t, tok_hr=tok_hr, tok_out=tok_out):
            W(e, tok_hr)
            e.tensor_scalar(out=off_sb[:], in0=off_sb[:], scalar1=v1[:],
                            scalar2=v3[:], op0=ALU.subtract, op1=ALU.mult)
            e.tensor_add(off_sb[:], off_sb[:], x1l[:]).then_inc(tok_out[0], 1)
        tok_out = (tok_out[0], tok_out[1])

        t_y = emit_dma("scalar", "yw", yh[128 * t:128 * (t + 1), :],
                       off_sb[:], pre_wait=tok_out)
        y_dmas.append(t_y)

    t_final = y_dmas[-1]

    @bld.on("scalar")
    def _(e, t_final=t_final):
        W(e, t_final)

    with nc.Block() as block:
        @block.sync
        def _(e):
            for fn in bld.sched["sync"]:
                fn(e)

        @block.gpsimd
        def _(e):
            for fn in bld.sched["gpsimd"]:
                fn(e)

        @block.vector
        def _(e):
            for fn in bld.sched["vector"]:
                fn(e)

        @block.scalar
        def _(e):
            for fn in bld.sched["scalar"]:
                fn(e)

        @block.tensor
        def _(e):
            for fn in bld.sched["tensor"]:
                fn(e)

    for gname in reversed(list(groups)):
        free(gname)
    bld.stack.close()
    return nc


def kernel(L0_attn_x, L0_attn_Wq, L0_attn_Wk, L0_attn_Wv, L0_attn_Wo,
           L0_attn_cos, L0_attn_sin, L0_ffn_W_res, L0_ffn_W_pre,
           L0_ffn_W_post, L0_ffn_H_res_base, L0_ffn_H_pre_base,
           L0_ffn_H_post_base, L0_ffn_gate, L0_ffn_W_branch):
    x = np.ascontiguousarray(np.asarray(L0_attn_x, dtype=np.float32))
    Wq = np.asarray(L0_attn_Wq, np.float32)
    Wk = np.asarray(L0_attn_Wk, np.float32)
    Wv = np.asarray(L0_attn_Wv, np.float32)
    Wo = np.ascontiguousarray(np.asarray(L0_attn_Wo, np.float32))
    cos = np.ascontiguousarray(np.asarray(L0_attn_cos, np.float32))
    sin = np.ascontiguousarray(np.asarray(L0_attn_sin, np.float32))
    WcatT = np.ascontiguousarray(np.vstack([
        np.asarray(L0_ffn_W_res, np.float32),
        np.asarray(L0_ffn_W_pre, np.float32),
        np.asarray(L0_ffn_W_post, np.float32)]).T)
    basecat = np.ascontiguousarray(np.concatenate([
        np.asarray(L0_ffn_H_res_base, np.float32).reshape(-1),
        np.asarray(L0_ffn_H_pre_base, np.float32).reshape(-1),
        np.asarray(L0_ffn_H_post_base, np.float32).reshape(-1),
    ]).reshape(1, 24))
    gate = np.ascontiguousarray(np.asarray(L0_ffn_gate, np.float32)
                                .reshape(1, 1))
    Wbr = np.ascontiguousarray(np.asarray(L0_ffn_W_branch, np.float32))

    tri = np.ascontiguousarray(
        np.where(np.arange(128)[None, :] >= np.arange(128)[:, None],
                 np.float32(0.0), np.float32(-1e9)).astype(np.float32))
    onesf = np.ones((128, 128), np.float32)
    cvec = np.zeros((128, 8), np.float32)
    cvec[:, 0] = -SHIFT
    cvec[:, 1] = 1e-5
    cvec[:, 2] = 1.1920929e-07
    cvec[:, 3] = 1.0 / 4096.0
    cvec[:, 4] = 1.0

    in_maps = []
    for c in range(8):
        b, r = c // 4, c % 4
        qh = list(range(4 * r, 4 * r + 4)) + \
            list(range(16 + 4 * r, 16 + 4 * r + 4))
        qcols = np.concatenate([np.arange(128 * h, 128 * (h + 1)) for h in qh])
        kvcols = np.concatenate([np.arange(128 * r, 128 * (r + 1)),
                                 np.arange(128 * (r + 4), 128 * (r + 5))])
        cidx = np.zeros((128, 32), np.int32)
        for h in range(32):
            if h < 16:
                rsrc, uu = h // 4, h % 4
            else:
                rsrc, uu = (h - 16) // 4, 4 + (h - 16) % 4
            cidx[:, h] = 2048 * rsrc + 512 * r + 128 * (uu % 4) + \
                np.arange(128)
        in_maps.append({
            "x0": np.ascontiguousarray(x[b, 1024 * r:1024 * (r + 1)]),
            "Wq_my": np.ascontiguousarray(Wq[:, qcols]),
            "Wk_my": np.ascontiguousarray(Wk[:, kvcols]),
            "Wv_my": np.ascontiguousarray(Wv[:, kvcols]),
            "Wo": Wo, "Wbr": Wbr, "WcatT": WcatT, "basecat": basecat,
            "gate": gate, "cosf": cos, "sinf": sin, "trimask": tri,
            "onesf": onesf, "cvec": cvec, "ctxidx": cidx,
        })

    nc = build_program()
    res = run_bass_kernel_spmd(nc, in_maps, core_ids=list(range(8)))
    out = np.empty((B, S, D), np.float32)
    for c in range(8):
        b, r = c // 4, c % 4
        out[b, 1024 * r:1024 * (r + 1)] = res.results[c]["y"]
    return out




# version 17 (final)
# speedup vs baseline: 1.0708x; 1.0111x over previous
import numpy as np
import ml_dtypes
from contextlib import ExitStack

import concourse.bass as bass
import concourse.mybir as mybir
from concourse.bass_utils import run_bass_kernel_spmd
from concourse.bass_core_rust import IndirectOffsetOnAxis

F32 = mybir.dt.float32
BF16 = mybir.dt.bfloat16
AF = mybir.ActivationFunctionType
ALU = mybir.AluOpType
AXX = mybir.AxisListType.X

B, S, D = 2, 4096, 4096
HD = 128
NTILE = 8
TT_ALL = 32
T_OWN = 1024
NKB = 32
SCALE = 1.0 / float(np.sqrt(128.0))
SHIFT = 12.0

ENGINES = ("sync", "gpsimd", "vector", "scalar", "tensor")


class Build:
    def __init__(self, nc):
        self.nc = nc
        self.sched = {e: [] for e in ENGINES}
        self.stack = ExitStack()
        self.semv = {}
        self.sems = {}
        for name in ("CC", "pe", "ac", "dv"):
            self.sems[name] = self.stack.enter_context(nc.semaphore(name))
            self.semv[name] = 0

    def on(self, eng):
        def deco(fn):
            self.sched[eng].append(fn)
            return None
        return deco

    def inc(self, name, n=1):
        if name not in self.sems:
            self.sems[name] = self.stack.enter_context(self.nc.semaphore(name))
            self.semv[name] = 0
        self.semv[name] += n
        return (self.sems[name], self.semv[name], n)

    def val(self, name):
        return (self.sems[name], self.semv[name])

    @staticmethod
    def w(eng_obj, tok):
        if tok is not None and tok[1] > 0:
            eng_obj.wait_ge(tok[0], tok[1])

    def bar(self, e, ins, name):
        tok = self.inc(name, 1)
        ins.then_inc(tok[0], 1)
        e.wait_ge(tok[0], tok[1])


def build_program():
    nc = bass.Bass("TRN2", target_bir_lowering=False)
    bld = Build(nc)
    W = Build.w

    x0h = nc.dram_tensor("x0", [T_OWN, D], F32, kind="ExternalInput").ap()
    Wqh = nc.dram_tensor("Wq_my", [D, 1024], F32, kind="ExternalInput").ap()
    Wkh = nc.dram_tensor("Wk_my", [D, 256], F32, kind="ExternalInput").ap()
    Wvh = nc.dram_tensor("Wv_my", [D, 256], F32, kind="ExternalInput").ap()
    Woh = nc.dram_tensor("Wo", [D, D], F32, kind="ExternalInput").ap()
    Wbrh = nc.dram_tensor("Wbr", [1024, 1024], F32, kind="ExternalInput").ap()
    Wcth = nc.dram_tensor("WcatT", [D, 24], F32, kind="ExternalInput").ap()
    bch = nc.dram_tensor("basecat", [1, 24], F32, kind="ExternalInput").ap()
    gateh = nc.dram_tensor("gate", [1, 1], F32, kind="ExternalInput").ap()
    cosh = nc.dram_tensor("cosf", [S, HD], F32, kind="ExternalInput").ap()
    sinh = nc.dram_tensor("sinf", [S, HD], F32, kind="ExternalInput").ap()
    trih = nc.dram_tensor("trimask", [128, 128], F32, kind="ExternalInput").ap()
    onesh = nc.dram_tensor("onesf", [128, 128], F32, kind="ExternalInput").ap()
    cvech = nc.dram_tensor("cvec", [128, 8], F32, kind="ExternalInput").ap()
    cidxh = nc.dram_tensor("ctxidx", [128, 32], mybir.dt.int32,
                           kind="ExternalInput").ap()
    yh = nc.dram_tensor("y", [T_OWN, D], F32, kind="ExternalOutput").ap()

    a_dram = nc.dram_tensor("a_dram", [T_OWN, D], BF16).ap()
    a_t = [nc.dram_tensor(f"a_t{i}", [D, 512], BF16).ap() for i in range(2)]
    aall = [nc.dram_tensor(f"aall{i}", [4 * D, 512], BF16).ap()
            for i in range(2)]
    q_dram = nc.dram_tensor("q_dram", [S, 1024], BF16).ap()
    k_dram = nc.dram_tensor("k_dram", [S, 256], BF16).ap()
    ctx_in = [nc.dram_tensor(f"ctx_in{i}", [2048, 1024], BF16).ap()
              for i in range(2)]
    ctxall = [nc.dram_tensor(f"ctxall{i}", [8192, 1024], BF16).ap()
              for i in range(2)]
    b_dram = nc.dram_tensor("b_dram", [T_OWN, D], F32).ap()
    x1_dram = nc.dram_tensor("x1_dram", [T_OWN, D], F32).ap()
    x1bf_dram = nc.dram_tensor("x1bf_dram", [T_OWN, D], BF16).ap()
    xpre_dram = nc.dram_tensor("xpre_dram", [T_OWN, 1024], BF16).ap()

    groups = {}

    def sbg(gname, name, shape, dt):
        g = nc.sbuf_tensor(name, shape, dt)
        h = g.__enter__()
        groups.setdefault(gname, []).append(g)
        return h

    def psg(gname, name, shape):
        g = nc.psum_tensor(name, shape, F32)
        h = g.__enter__()
        groups.setdefault(gname, []).append(g)
        return h

    def free(gname):
        for g in reversed(groups.pop(gname, [])):
            g.__exit__(None, None, None)

    P = "persist"
    cvec = sbg(P, "cvec_sb", [128, 8], F32)
    tri = sbg(P, "tri_sb", [128, 128], F32)
    ones_bf = sbg(P, "ones_bf", [128, 128], BF16)
    sqs = sbg(P, "sqs", [128, D], F32)
    m2a = sbg(P, "m2a", [128, 8], F32)
    r2a = sbg(P, "r2a", [128, 8], F32)
    rhoa = sbg(P, "rhoa", [128, 8], F32)
    rhm2 = sbg(P, "rhm2", [128, 8], F32)
    bsum = sbg(P, "bsum", [128, 8], F32)
    bsq = sbg(P, "bsq", [128, 8], F32)
    mba = sbg(P, "mba", [128, 8], F32)
    rba = sbg(P, "rba", [128, 8], F32)
    v1 = sbg(P, "v1", [128, 1], F32)
    v2 = sbg(P, "v2", [128, 1], F32)
    v3 = sbg(P, "v3", [128, 1], F32)
    vA1 = sbg(P, "vA1", [128, 8], F32)
    vA2 = sbg(P, "vA2", [128, 8], F32)
    vA3 = sbg(P, "vA3", [128, 8], F32)
    Hlog = sbg(P, "Hlog", [128, 8, 24], F32)
    M16 = sbg(P, "M16", [128, 8, 16], F32)
    t16 = sbg(P, "t16", [128, 8, 16], F32)
    zmx = sbg(P, "zmx", [128, 8, 4], F32)
    zmx2 = sbg(P, "zmx2", [128, 8, 1], F32)
    aS = sbg(P, "aS", [128, 8, 4], F32)
    bS = sbg(P, "bS", [128, 8, 4], F32)
    Hres = sbg(P, "Hres", [128, 8, 16], F32)
    Hpre = sbg(P, "Hpre", [128, 8, 4], F32)
    Hpost = sbg(P, "Hpost", [128, 8, 4], F32)
    u24 = sbg(P, "u24", [128, 24], F32)
    csb = sbg(P, "csb", [128, 24], F32)
    bbb = sbg(P, "bbb", [128, 24], F32)
    cs1 = sbg(P, "cs1", [1, 24], BF16)
    g11 = sbg(P, "g11", [1, 1], F32)
    g11b = sbg(P, "g11b", [1, 1], BF16)
    gcol = sbg(P, "gcol", [128, 1], F32)
    gm1 = sbg(P, "gm1", [128, 1], F32)
    bc1b = sbg(P, "bc1b", [1, 24], BF16)
    cidx_sb = sbg(P, "cidx_sb", [128, 32], mybir.dt.int32)

    C = {"eps1": cvec[:, 1:2], "eps2": cvec[:, 2:3], "inv": cvec[:, 3:4],
         "shift": cvec[:, 0:1]}

    GV = "gv"
    V_sb = sbg(GV, "V_sb", [128, 2, NKB, 128], BF16)
    GW = "gw"
    cos_sb = sbg(GW, "cos_sb", [128, TT_ALL, HD], BF16)
    sin_sb = sbg(GW, "sin_sb", [128, TT_ALL, HD], BF16)
    Wq_sb = sbg(GW, "Wq_sb", [128, 32, 1024], BF16)
    Wk_sb = sbg(GW, "Wk_sb", [128, 32, 256], BF16)
    Wv_sb = sbg(GW, "Wv_sb", [128, 32, 256], BF16)
    GA = "ga"
    x0t = [sbg(GA, f"x0t{i}", [128, D], F32) for i in range(2)]
    at = [sbg(GA, f"at{i}", [128, D], BF16) for i in range(2)]
    aFMt = [sbg(GA, f"aFMt{i}", [128, T_OWN], BF16) for i in range(4)]

    def emit_dma(qname, semname, out, in_, pre_wait=None):
        tok = bld.inc(semname, 16)

        @bld.on(qname)
        def _(e, out=out, in_=in_, tok=tok, pre_wait=pre_wait):
            if pre_wait is not None:
                for _t in (pre_wait if isinstance(pre_wait, list) else [pre_wait]):
                    W(e, _t)
            e.dma_start(out=out, in_=in_).then_inc(tok[0], 16)
        return (tok[0], tok[1])

    emit_dma("sync", "cst", cvec[:], cvech)
    emit_dma("sync", "cst", tri[:], trih)
    emit_dma("sync", "cst", g11[:], gateh)
    t_cst = emit_dma("sync", "cst", cidx_sb[:], cidxh)
    t_cvec = t_tri = t_g = t_cidx = t_cst
    emit_dma("gpsimd", "cstg", ones_bf[:], onesh)
    emit_dma("gpsimd", "cstg", cos_sb[:],
             cosh.rearrange("(t p) d -> p t d", p=128))
    emit_dma("gpsimd", "cstg", sin_sb[:],
             sinh.rearrange("(t p) d -> p t d", p=128))
    emit_dma("gpsimd", "cstg", Wq_sb[:],
             Wqh.rearrange("(i p) n -> p i n", p=128))
    emit_dma("gpsimd", "cstg", Wk_sb[:],
             Wkh.rearrange("(i p) n -> p i n", p=128))
    t_cstg = emit_dma("gpsimd", "cstg", Wv_sb[:],
                      Wvh.rearrange("(i p) n -> p i n", p=128))
    t_ones = t_cos = t_sin = t_wq = t_wk = t_wv = t_cstg

    GPB = "gpb"
    pq = [psg(GPB, f"pq{i}", [128, 1024]) for i in range(2)]
    pk = [psg(GPB, f"pk{i}", [128, 256]) for i in range(2)]
    pv = [psg(GPB, f"pv{i}", [128, 256]) for i in range(2)]

    a_norm = []
    a_sqred = []
    a_wr = []
    for t in range(NTILE):
        s1 = vA1[:, t:t + 1]
        s2 = vA2[:, t:t + 1]
        s3 = vA3[:, t:t + 1]
        tok_x0 = bld.inc("x0l", 16)

        @bld.on("scalar")
        def _(e, t=t, tok_x0=tok_x0,
              prev=(a_norm[t - 2] if t >= 2 else None)):
            if prev is not None:
                W(e, prev)
            e.dma_start(out=x0t[t % 2][:], in_=x0h[128 * t:128 * (t + 1), :]
                        ).then_inc(tok_x0[0], 16)
        tok_x0 = (tok_x0[0], tok_x0[1])

        tok_sq = bld.inc("ac")

        @bld.on("scalar")
        def _(e, t=t, tok_x0=tok_x0, tok_sq=tok_sq,
              prev_red=(a_sqred[t - 1] if t >= 1 else None)):
            W(e, tok_x0)
            if prev_red is not None:
                W(e, prev_red)
            e.activation(out=sqs[:], in_=x0t[t % 2][:], func=AF.Square
                         ).then_inc(tok_sq[0], 1)
        tok_sq = (tok_sq[0], tok_sq[1])

        tok_red = bld.inc("dv")
        tok_st = bld.inc("dv")

        @bld.on("vector")
        def _(e, t=t, s1=s1, s2=s2, s3=s3, tok_x0=tok_x0, tok_sq=tok_sq,
              tok_st=tok_st, tok_red=tok_red):
            if t == 0:
                W(e, t_cvec)
            W(e, tok_x0)
            e.reduce_sum(s1, x0t[t % 2][:], axis=AXX)
            W(e, tok_sq)
            e.reduce_sum(s2, sqs[:], axis=AXX).then_inc(tok_red[0], 1)
            bld.bar(e, e.tensor_scalar(out=s1, in0=s1, scalar1=C["inv"],
                                       scalar2=None, op0=ALU.mult), "dvb")
            bld.bar(e, e.tensor_mul(s3, s1, s1), "dvb")
            e.scalar_tensor_tensor(out=s2, in0=s2, scalar=C["inv"],
                                   in1=s3, op0=ALU.mult, op1=ALU.subtract
                                   ).then_inc(tok_st[0], 1)
        tok_st = (tok_st[0], tok_st[1])
        a_sqred.append((tok_red[0], tok_red[1]))

        tok_r = bld.inc("ac")

        @bld.on("scalar")
        def _(e, t=t, s2=s2, s3=s3, tok_st=tok_st, tok_r=tok_r):
            W(e, tok_st)
            bld.bar(e, e.activation(out=s3, in_=s2, func=AF.Ln,
                                    bias=C["eps1"]), "acb")
            e.activation(out=s3, in_=s3, func=AF.Exp, scale=-0.5
                         ).then_inc(tok_r[0], 1)
        tok_r = (tok_r[0], tok_r[1])

        tok_n = bld.inc("dv")

        @bld.on("vector")
        def _(e, t=t, s1=s1, s3=s3, tok_r=tok_r, tok_n=tok_n,
              prev_wr=(a_wr[t - 2] if t >= 2 else None)):
            W(e, tok_r)
            if prev_wr is not None:
                W(e, prev_wr)
            e.tensor_scalar(out=at[t % 2][:], in0=x0t[t % 2][:], scalar1=s1,
                            scalar2=s3, op0=ALU.subtract, op1=ALU.mult
                            ).then_inc(tok_n[0], 1)
        tok_n = (tok_n[0], tok_n[1])
        a_norm.append(tok_n)

        t_aw = emit_dma("sync", "aw", a_dram[128 * t:128 * (t + 1), :],
                        at[t % 2][:], pre_wait=tok_n)
        a_wr.append(t_aw)
    t_a_all = a_wr[-1]

    w_toks = []
    for c in range(32):
        tok_T = bld.inc(f"a2t{c % 4}", 16)

        @bld.on("sync")
        def _(e, c=c, tok_T=tok_T,
              prev_w=(w_toks[c - 4] if c >= 4 else None)):
            if c == 0:
                W(e, t_a_all)
            if prev_w is not None:
                W(e, prev_w)
            e.dma_start(out=aFMt[c % 4][:],
                        in_=a_dram[:, 128 * c:128 * (c + 1)],
                        transpose=True).then_inc(tok_T[0], 16)
        tok_T = (tok_T[0], tok_T[1])
        emit_dma("scalar", f"a2w{c % 4}", a_t[0][128 * c:128 * (c + 1), :],
                 aFMt[c % 4][:, 0:512], pre_wait=tok_T)
        t_w = emit_dma("scalar", f"a2x{c % 4}",
                       a_t[1][128 * c:128 * (c + 1), :],
                       aFMt[c % 4][:, 512:1024])
        w_toks.append(t_w)
    t_at_all = w_toks[-1]

    tok_cc1a = bld.inc("CC")
    tok_cc1b = bld.inc("CC")

    @bld.on("gpsimd")
    def _(e, tok_cc1a=tok_cc1a, tok_cc1b=tok_cc1b, w_fin=tuple(w_toks[-4:])):
        for wf in w_fin:
            W(e, wf)
        e.collective_compute(
            "AllGather", ALU.bypass,
            replica_groups=[[0, 1, 2, 3], [4, 5, 6, 7]],
            ins=[a_t[0]], outs=[aall[0]],
        ).then_inc(tok_cc1a[0], 1)
        e.collective_compute(
            "AllGather", ALU.bypass,
            replica_groups=[[0, 1, 2, 3], [4, 5, 6, 7]],
            ins=[a_t[1]], outs=[aall[1]],
        ).then_inc(tok_cc1b[0], 1)
    tok_cc1a = (tok_cc1a[0], tok_cc1a[1])
    tok_cc1b = (tok_cc1b[0], tok_cc1b[1])

    free("ga")
    GB = "gb"
    attb = [sbg(GB, f"attb{i}", [128, 32, 128], BF16) for i in range(4)]
    ro1 = sbg(GB, "ro1", [128, 1024], F32)
    ro2 = sbg(GB, "ro2", [128, 512], F32)
    qbf = [sbg(GB, f"qbf{i}", [128, 1024], BF16) for i in range(4)]
    kbf = [sbg(GB, f"kbf{i}", [128, 256], BF16) for i in range(4)]

    aall_r = [a.rearrange("(r c p) t -> r p c t", r=4, c=32, p=128)
              for a in aall]
    rope_done, vev_done, qk_wr, pe_toks = [], [], [], []
    tt_order = [(half, r, lt) for half in range(2) for r in range(4)
                for lt in range(4)]
    for i, (half, r_src, lt) in enumerate(tt_order):
        tt = r_src * 8 + half * 4 + lt
        t0 = 128 * lt
        tok_ab = bld.inc(f"ab{i % 4}", 16)

        @bld.on("gpsimd")
        def _(e, i=i, half=half, r_src=r_src, t0=t0, tok_ab=tok_ab,
              prev=(pe_toks[i - 4] if i >= 4 else None)):
            if i == 0:
                W(e, tok_cc1a)
            if i == 16:
                W(e, tok_cc1b)
            if prev is not None:
                W(e, prev)
            e.dma_start(out=attb[i % 4][:],
                        in_=aall_r[half][r_src, :, :, t0:t0 + 128]
                        ).then_inc(tok_ab[0], 16)
        tok_ab = (tok_ab[0], tok_ab[1])

        tok_pe = bld.inc("pe")

        @bld.on("tensor")
        def _(e, i=i, tok_ab=tok_ab, tok_pe=tok_pe,
              prev_ro=(rope_done[i - 2] if i >= 2 else None),
              prev_ve=(vev_done[i - 2] if i >= 2 else None)):
            if i == 0:
                W(e, t_wq); W(e, t_wk); W(e, t_wv)
            W(e, tok_ab)
            if prev_ro is not None:
                W(e, prev_ro)
            if prev_ve is not None:
                W(e, prev_ve)
            for c in range(32):
                lhs = attb[i % 4][:, c, :]
                e.matmul(pq[i % 2][:, 0:512], lhs, Wq_sb[:, c, 0:512],
                         start=(c == 0), stop=(c == 31))
                e.matmul(pq[i % 2][:, 512:1024], lhs, Wq_sb[:, c, 512:1024],
                         start=(c == 0), stop=(c == 31))
                e.matmul(pk[i % 2][:], lhs, Wk_sb[:, c, :],
                         start=(c == 0), stop=(c == 31))
                mm = e.matmul(pv[i % 2][:], lhs, Wv_sb[:, c, :],
                              start=(c == 0), stop=(c == 31))
            mm.then_inc(tok_pe[0], 1)
        tok_pe = (tok_pe[0], tok_pe[1])
        pe_toks.append(tok_pe)

        tok_ve = bld.inc("ac")

        @bld.on("scalar")
        def _(e, i=i, tt=tt, tok_pe=tok_pe, tok_ve=tok_ve):
            W(e, tok_pe)
            e.copy(out=V_sb[:, 0, tt, :], in_=pv[i % 2][:, 0:128])
            e.copy(out=V_sb[:, 1, tt, :], in_=pv[i % 2][:, 128:256]
                   ).then_inc(tok_ve[0], 1)
        vev_done.append((tok_ve[0], tok_ve[1]))

        tok_ro = bld.inc("dv")

        @bld.on("vector")
        def _(e, i=i, tt=tt, tok_pe=tok_pe, tok_ro=tok_ro,
              prev_w=(qk_wr[i - 4] if i >= 4 else None)):
            if i == 0:
                W(e, t_cos); W(e, t_sin)
            W(e, tok_pe)
            if prev_w is not None:
                W(e, prev_w[0])
                W(e, prev_w[1])
            cosv = cos_sb[:, tt:tt + 1, :]
            sinv = sin_sb[:, tt:tt + 1, :]
            for (ps, nh, obf) in ((pq[i % 2], 8, qbf[i % 4]),
                                  (pk[i % 2], 2, kbf[i % 4])):
                pvv = ps[:].rearrange("p (h d) -> p h d", h=nh)
                ov = obf[:].rearrange("p (h d) -> p h d", h=nh)
                t1v = ro1[:, 0:nh * 128].rearrange("p (h d) -> p h d", h=nh)
                t2v = ro2[:, 0:nh * 64].rearrange("p (h d) -> p h d", h=nh)
                cb = cosv.broadcast_to([128, nh, HD])
                sbc = sinv.broadcast_to([128, nh, HD])
                e.tensor_mul(t1v, pvv, cb)
                e.tensor_mul(t2v, pvv[:, :, 64:128], sbc[:, :, 0:64])
                e.tensor_sub(ov[:, :, 0:64], t1v[:, :, 0:64], t2v)
                e.tensor_mul(t2v, pvv[:, :, 0:64], sbc[:, :, 64:128])
                mm = e.tensor_add(ov[:, :, 64:128], t1v[:, :, 64:128], t2v)
            mm.then_inc(tok_ro[0], 1)
        tok_ro = (tok_ro[0], tok_ro[1])
        rope_done.append(tok_ro)

        t_qw = emit_dma("sync", f"qw{i % 4}",
                        q_dram[128 * tt:128 * (tt + 1), :],
                        qbf[i % 4][:], pre_wait=tok_ro)
        t_kw = emit_dma("sync", f"kw{i % 4}",
                        k_dram[128 * tt:128 * (tt + 1), :],
                        kbf[i % 4][:])
        qk_wr.append((t_qw, t_kw))
    t_b_done_all = [t for pair in qk_wr[-4:] for t in pair]

    free("gb"); free("gw"); free("gpb")
    GCF = "gcf"
    ctxFM = sbg(GCF, "ctxFM", [128, 32, 1024], BF16)
    GC = "gc"
    qFM = sbg(GC, "qFM", [128, 8, S], BF16)
    KFM = sbg(GC, "KFM", [128, 2, S], BF16)
    PT = [sbg(GC, f"PT{i}", [128, 1024], BF16) for i in range(4)]
    lnl = sbg(GC, "lnl", [128, 512], F32)
    rl = [sbg(GC, f"rl{i}", [128, 512], F32) for i in range(2)]
    ctxev = [sbg(GC, f"ctxev{i}", [128, 512], BF16) for i in range(2)]
    GPC = "gpc"
    psS = [psg(GPC, f"psS{i}", [128, 1024]) for i in range(2)]
    pctx = [psg(GPC, f"pctx{i}", [128, 512]) for i in range(2)]
    pl = [psg(GPC, f"pl{i}", [128, 512]) for i in range(2)]

    tp_q, tp_k = {}, {}
    _fm_first = [True]

    def emit_wave(kind, idx):
        for rb in range(4):
            tok = bld.inc("fmt", 16)

            @bld.on("sync")
            def _(e, kind=kind, idx=idx, rb=rb, tok=tok,
                  first=_fm_first[0]):
                if first:
                    for tk in t_b_done_all:
                        W(e, tk)
                if kind == "q":
                    e.dma_start(out=qFM[:, idx, 1024 * rb:1024 * (rb + 1)],
                                in_=q_dram[1024 * rb:1024 * (rb + 1),
                                           128 * idx:128 * (idx + 1)],
                                transpose=True).then_inc(tok[0], 16)
                else:
                    e.dma_start(out=KFM[:, idx, 1024 * rb:1024 * (rb + 1)],
                                in_=k_dram[1024 * rb:1024 * (rb + 1),
                                           128 * idx:128 * (idx + 1)],
                                transpose=True).then_inc(tok[0], 16)
            _fm_first[0] = False
            if kind == "q":
                tp_q[idx] = (tok[0], tok[1])
            else:
                tp_k[idx] = (tok[0], tok[1])

    emit_wave("k", 0)
    emit_wave("q", 0)

    it = 0
    evac_dv, lnl_ac, ctx_dmas = [], [], []
    for u in range(8):
        b_ = u // 4
        if u == 0:
            emit_wave("q", 1)
        elif u == 3:
            emit_wave("k", 1)
            emit_wave("q", 4)
        elif u < 7:
            emit_wave("q", u + 1)
        head_first = [True]
        for qc in (7, 0, 6, 1, 5, 2, 4, 3):
            par = it % 2
            n_kb = 4 * qc + 4
            npairs = (n_kb + 1) // 2
            sE, lctx_toks = [], []

            def emit_lctx(p, kbs):
                tok_l = bld.inc("pe")

                @bld.on("tensor")
                def _(e, u=u, qc=qc, b_=b_, p=p, kbs=tuple(kbs), par=par,
                      n_kb=n_kb, tok_l=tok_l, tok_e=sE[p]):
                    W(e, tok_e)
                    for kb in kbs:
                        d = kb - 4 * qc
                        off = 128 * d if d >= 0 else 0
                        col0 = 512 * (kb % 2)
                        pts = PT[p % 4][:, col0 + off:col0 + 512]
                        e.matmul(pl[par][:, off:512], ones_bf[:], pts,
                                 start=(kb == 0), stop=(kb == n_kb - 1))
                        mm = e.matmul(pctx[par][:, off:512],
                                      V_sb[:, b_, kb, :], pts,
                                      start=(kb == 0), stop=(kb == n_kb - 1))
                    mm.then_inc(tok_l[0], 1)
                lctx_toks.append((tok_l[0], tok_l[1]))

            for p in range(npairs):
                kbs = [kb for kb in (2 * p, 2 * p + 1) if kb < n_kb]
                tok_s = bld.inc("pe")

                @bld.on("tensor")
                def _(e, u=u, qc=qc, b_=b_, p=p, kbs=tuple(kbs), it=it,
                      par=par, tok_s=tok_s,
                      s_free=(sE[p - 2] if p >= 2 else None),
                      first=(head_first[0] and p == 0),
                      prev_ev=(evac_dv[it - 2] if (p == 0 and it >= 2) else None),
                      prev_ln=(lnl_ac[it - 2] if (p == 0 and it >= 2) else None),
                      head_w=((tp_q[u], tp_k[u // 4],
                               vev_done[-1]) if (head_first[0] and p == 0)
                              else None)):
                    if first and head_w is not None:
                        for hw in head_w:
                            W(e, hw)
                    if prev_ev is not None:
                        W(e, prev_ev)
                    if prev_ln is not None:
                        W(e, prev_ln)
                    if s_free is not None:
                        W(e, s_free)
                    for kb in kbs:
                        d = kb - 4 * qc
                        off = 128 * d if d >= 0 else 0
                        col0 = 512 * (kb % 2)
                        mm = e.matmul(
                            psS[p % 2][:, col0 + off:col0 + 512],
                            KFM[:, b_, 128 * kb:128 * (kb + 1)],
                            qFM[:, u, 512 * qc + off:512 * qc + 512],
                            start=True, stop=True)
                    mm.then_inc(tok_s[0], 1)
                tok_s = (tok_s[0], tok_s[1])
                head_first[0] = False

                has_diag = any(kb - 4 * qc >= 0 for kb in kbs)
                tok_m = None
                if has_diag:
                    tok_mi = bld.inc("dv")

                    @bld.on("vector")
                    def _(e, qc=qc, p=p, kbs=tuple(kbs), tok_mi=tok_mi,
                          tok_s=tok_s, first=_fm_first[0] or (it < 2)):
                        if first:
                            W(e, t_tri)
                        W(e, tok_s)
                        mm = None
                        for kb in kbs:
                            d = kb - 4 * qc
                            if d < 0:
                                continue
                            col0 = 512 * (kb % 2) + 128 * d
                            mm = e.tensor_add(psS[p % 2][:, col0:col0 + 128],
                                              psS[p % 2][:, col0:col0 + 128],
                                              tri[:])
                        mm.then_inc(tok_mi[0], 1)
                    tok_m = (tok_mi[0], tok_mi[1])

                tok_e = bld.inc("ac")

                @bld.on("scalar")
                def _(e, qc=qc, p=p, kbs=tuple(kbs), tok_e=tok_e,
                      tok_s=tok_s, tok_m=tok_m,
                      pt_free=(lctx_toks[p - 4] if p >= 4 else None)):
                    W(e, tok_s)
                    if tok_m is not None:
                        W(e, tok_m)
                    if pt_free is not None:
                        W(e, pt_free)
                    full = (len(kbs) == 2 and
                            all(kb - 4 * qc < 0 for kb in kbs))
                    if full:
                        mm = e.activation(out=PT[p % 4][:], in_=psS[p % 2][:],
                                          func=AF.Exp, scale=SCALE,
                                          bias=C["shift"])
                    else:
                        for kb in kbs:
                            d = kb - 4 * qc
                            off = 128 * d if d >= 0 else 0
                            col0 = 512 * (kb % 2)
                            mm = e.activation(
                                out=PT[p % 4][:, col0 + off:col0 + 512],
                                in_=psS[p % 2][:, col0 + off:col0 + 512],
                                func=AF.Exp, scale=SCALE, bias=C["shift"])
                    mm.then_inc(tok_e[0], 1)
                sE.append((tok_e[0], tok_e[1]))

                if p >= 1:
                    emit_lctx(p - 1, [kb for kb in (2 * (p - 1), 2 * p - 1)
                                      if kb < n_kb])
            emit_lctx(npairs - 1, [kb for kb in (2 * npairs - 2, 2 * npairs - 1)
                                   if kb < n_kb])

            tok_ln = bld.inc("ac", 2)

            @bld.on("scalar")
            def _(e, par=par, tok_ln=tok_ln, last=lctx_toks[-1]):
                W(e, last)
                e.activation(out=lnl[:], in_=pl[par][:], func=AF.Ln)
                e.activation(out=rl[par][:], in_=lnl[:], func=AF.Exp,
                             scale=-1.0).then_inc(tok_ln[0], 2)
            tok_ln = (tok_ln[0], tok_ln[1])
            lnl_ac.append(tok_ln)

            tok_ev = bld.inc("dv")

            @bld.on("vector")
            def _(e, par=par, tok_ev=tok_ev, tok_ln=tok_ln,
                  prev_dma=(ctx_dmas[it - 2] if it >= 2 else None)):
                W(e, tok_ln)
                if prev_dma is not None:
                    W(e, prev_dma)
                e.tensor_mul(ctxev[par][:], pctx[par][:], rl[par][:]
                             ).then_inc(tok_ev[0], 1)
            tok_ev = (tok_ev[0], tok_ev[1])
            evac_dv.append(tok_ev)

            row0 = 512 * (qc // 2) + 128 * (u % 4)
            col0 = 512 * (qc % 2)
            t_cd = emit_dma("sync", f"cx{par}",
                            ctx_in[u // 4][row0:row0 + 128, col0:col0 + 512],
                            ctxev[par][:], pre_wait=tok_ev)
            ctx_dmas.append(t_cd)
            it += 1
        if u == 3:
            tok_cc2a = bld.inc("CC")

            @bld.on("gpsimd")
            def _(e, tok_cc2a=tok_cc2a,
                  cx_fin=(ctx_dmas[-1], ctx_dmas[-2])):
                W(e, cx_fin[0])
                W(e, cx_fin[1])
                e.collective_compute(
                    "AllGather", ALU.bypass,
                    replica_groups=[[0, 1, 2, 3], [4, 5, 6, 7]],
                    ins=[ctx_in[0]], outs=[ctxall[0]],
                ).then_inc(tok_cc2a[0], 1)
            tok_cc2a = (tok_cc2a[0], tok_cc2a[1])
            for h in range(32):
                uu = (h % 4) if h < 16 else 4 + ((h - 16) % 4)
                if uu >= 4:
                    continue
                tok = bld.inc("cf", 16)

                @bld.on("gpsimd")
                def _(e, h=h, tok=tok, tok_cc2a=tok_cc2a):
                    if h == 0:
                        W(e, tok_cc2a)
                        W(e, t_cidx)
                    e.indirect_dma_start(
                        out=ctxFM[:, h, :], out_offset=None,
                        in_=ctxall[0],
                        in_offset=IndirectOffsetOnAxis(cidx_sb[:, h:h + 1], 0),
                    ).then_inc(tok[0], 16)
    t_ctx_all = ctx_dmas[-1]

    tok_cc2 = bld.inc("CC")

    @bld.on("gpsimd")
    def _(e, tok_cc2=tok_cc2, cx_fin=(ctx_dmas[-1], ctx_dmas[-2])):
        W(e, cx_fin[0])
        W(e, cx_fin[1])
        e.collective_compute(
            "AllGather", ALU.bypass,
            replica_groups=[[0, 1, 2, 3], [4, 5, 6, 7]],
            ins=[ctx_in[1]], outs=[ctxall[1]],
        ).then_inc(tok_cc2[0], 1)
    tok_cc2 = (tok_cc2[0], tok_cc2[1])

    free("gc"); free("gpc")
    GD = "gd"
    Wo_sb = [sbg(GD, f"Wo_sb{i}", [128, 32, 512], BF16) for i in range(2)]
    btev = [sbg(GD, f"btev{i}", [128, 512], F32) for i in range(2)]
    GPD = "gpd"
    pb = [psg(GPD, f"pb{i}", [128, 512]) for i in range(2)]

    for h in range(32):
        uu = (h % 4) if h < 16 else 4 + ((h - 16) % 4)
        if uu < 4:
            continue
        tok = bld.inc("cf", 16)

        @bld.on("gpsimd")
        def _(e, h=h, tok=tok):
            e.wait_ge(tok_cc2[0], tok_cc2[1])
            e.indirect_dma_start(
                out=ctxFM[:, h, :], out_offset=None,
                in_=ctxall[1],
                in_offset=IndirectOffsetOnAxis(cidx_sb[:, h:h + 1], 0),
            ).then_inc(tok[0], 16)
    t_cf_all = bld.val("cf")

    d_ac, d_dv, d_bw, d_pe_slice = [], [], [], []
    for s_ in range(8):
        tok_wo = bld.inc(f"wo{s_ % 2}", 16)

        @bld.on("gpsimd")
        def _(e, s_=s_, tok_wo=tok_wo,
              prev=(d_pe_slice[s_ - 2] if s_ >= 2 else None)):
            if prev is not None:
                W(e, prev)
            e.dma_start(out=Wo_sb[s_ % 2][:],
                        in_=Woh[:, 512 * s_:512 * (s_ + 1)]
                        .rearrange("(i p) n -> p i n", p=128)
                        ).then_inc(tok_wo[0], 16)
        tok_wo = (tok_wo[0], tok_wo[1])

        for t in range(NTILE):
            sti = s_ * 8 + t
            tok_pe = bld.inc("pe")

            @bld.on("tensor")
            def _(e, s_=s_, t=t, tok_pe=tok_pe, tok_wo=tok_wo,
                  first=(s_ == 0 and t == 0),
                  prev_ac=(d_ac[sti - 2] if sti >= 2 else None),
                  prev_dv=(d_dv[sti - 2] if sti >= 2 else None)):
                if first:
                    W(e, t_cf_all)
                if t == 0:
                    W(e, tok_wo)
                if prev_ac is not None:
                    W(e, prev_ac)
                if prev_dv is not None:
                    W(e, prev_dv)
                for i in range(32):
                    mm = e.matmul(pb[t % 2][:],
                                  ctxFM[:, i, 128 * t:128 * (t + 1)],
                                  Wo_sb[s_ % 2][:, i, :],
                                  start=(i == 0), stop=(i == 31))
                mm.then_inc(tok_pe[0], 1)
            tok_pe = (tok_pe[0], tok_pe[1])
            if t == NTILE - 1:
                d_pe_slice.append(tok_pe)

            tok_ac = bld.inc("ac", 2)

            @bld.on("scalar")
            def _(e, t=t, tok_pe=tok_pe, tok_ac=tok_ac,
                  prev_bw=(d_bw[sti - 2] if sti >= 2 else None),
                  prev_dv=(d_dv[sti - 1] if sti >= 1 else None)):
                W(e, tok_pe)
                if prev_bw is not None:
                    W(e, prev_bw)
                if prev_dv is not None:
                    W(e, prev_dv)
                e.copy(out=btev[t % 2][:], in_=pb[t % 2][:])
                e.activation(out=sqs[:, 0:512], in_=pb[t % 2][:],
                             func=AF.Square).then_inc(tok_ac[0], 2)
            tok_ac = (tok_ac[0], tok_ac[1])
            d_ac.append(tok_ac)

            tok_dv = bld.inc("dv")

            @bld.on("vector")
            def _(e, s_=s_, t=t, tok_pe=tok_pe, tok_ac=tok_ac, tok_dv=tok_dv):
                W(e, tok_ac)
                W(e, tok_pe)
                e.reduce_sum(v1[:], pb[t % 2][:], axis=AXX)
                e.reduce_sum(v2[:], sqs[:, 0:512], axis=AXX)
                if s_ == 0:
                    e.tensor_copy(out=bsum[:, t:t + 1], in_=v1[:])
                    mm = e.tensor_copy(out=bsq[:, t:t + 1], in_=v2[:])
                else:
                    e.tensor_add(bsum[:, t:t + 1], bsum[:, t:t + 1], v1[:])
                    mm = e.tensor_add(bsq[:, t:t + 1], bsq[:, t:t + 1], v2[:])
                mm.then_inc(tok_dv[0], 1)
            d_dv.append((tok_dv[0], tok_dv[1]))

            t_bw = emit_dma("sync", f"bw{t % 2}",
                            b_dram[128 * t:128 * (t + 1),
                                   512 * s_:512 * (s_ + 1)],
                            btev[t % 2][:], pre_wait=tok_ac)
            d_bw.append(t_bw)
    t_b_last = [d_bw[-1], d_bw[-2]]

    tok_bst = bld.inc("dv")

    @bld.on("vector")
    def _(e, tok_bst=tok_bst, last=d_dv[-1]):
        W(e, last)
        bld.bar(e, e.tensor_scalar(out=mba[:], in0=bsum[:],
                                   scalar1=1.0 / 4096.0, scalar2=None,
                                   op0=ALU.mult), "dvb")
        bld.bar(e, e.tensor_mul(u24[:, 0:8], mba[:], mba[:]), "dvb")
        e.scalar_tensor_tensor(out=rba[:], in0=bsq[:], scalar=1.0 / 4096.0,
                               in1=u24[:, 0:8], op0=ALU.mult,
                               op1=ALU.subtract).then_inc(tok_bst[0], 1)
    tok_bst = (tok_bst[0], tok_bst[1])
    tok_bst2 = bld.inc("ac")

    @bld.on("scalar")
    def _(e, tok_bst=tok_bst, tok_bst2=tok_bst2):
        W(e, tok_bst)
        bld.bar(e, e.activation(out=rba[:], in_=rba[:], func=AF.Ln,
                                bias=C["eps1"]), "acb")
        e.activation(out=rba[:], in_=rba[:], func=AF.Exp, scale=-0.5
                     ).then_inc(tok_bst2[0], 1)
    tok_bst2 = (tok_bst2[0], tok_bst2[1])

    free("gd"); free("gcf"); free("gv"); free("gpd")
    WcT = sbg("gwc", "WcT", [128, 32, 24], BF16)
    GE = "ge"
    bfull = sbg(GE, "bfull", [128, D], F32)
    x0e = sbg(GE, "x0e", [128, D], F32)
    x1t = sbg(GE, "x1t", [128, D], F32)
    x1bft = sbg(GE, "x1bft", [128, D], BF16)
    emit_dma("gpsimd", "wct", WcT[:],
             Wcth.rearrange("(i p) j -> p i j", p=128),
             pre_wait=[d_pe_slice[-1]] + t_b_last)
    t_wct = emit_dma("gpsimd", "wct", bc1b[:], bch)
    t_bc = t_wct

    e_x1w, e_bfw = [], []
    for t in range(NTILE):
        pw = None
        if t == 0:
            pw = t_b_last
        elif t >= 1:
            pw = e_bfw[t - 1]
        t_bl = emit_dma("scalar", "ebl", bfull[:],
                        b_dram[128 * t:128 * (t + 1), :],
                        pre_wait=pw)
        t_xl = emit_dma("scalar", "exl", x0e[:],
                        x0h[128 * t:128 * (t + 1), :])

        tok_x1 = bld.inc("dv")

        @bld.on("vector")
        def _(e, t=t, t_bl=t_bl, t_xl=t_xl, tok_x1=tok_x1, tok_bst2=tok_bst2,
              prev_w=(e_x1w[t - 1] if t >= 1 else None)):
            if t == 0:
                W(e, tok_bst2)
            W(e, t_bl)
            if prev_w is not None:
                W(e, prev_w)
            e.tensor_scalar(out=bfull[:], in0=bfull[:], scalar1=mba[:, t:t + 1],
                            scalar2=rba[:, t:t + 1], op0=ALU.subtract,
                            op1=ALU.mult)
            W(e, t_xl)
            e.tensor_add(x1t[:], x0e[:], bfull[:])
            e.reduce_sum(v1[:], x1t[:], axis=AXX)
            e.tensor_scalar(out=v1[:], in0=v1[:], scalar1=C["inv"],
                            scalar2=None, op0=ALU.mult)
            e.tensor_copy(out=x1bft[:], in_=x1t[:]).then_inc(tok_x1[0], 1)
        tok_x1 = (tok_x1[0], tok_x1[1])

        t_x1w = emit_dma("scalar", "x1w", x1_dram[128 * t:128 * (t + 1), :],
                         x1t[:], pre_wait=tok_x1)
        t_bfw = emit_dma("scalar", "bfw", x1bf_dram[128 * t:128 * (t + 1), :],
                         x1bft[:])
        e_x1w.append(t_x1w)
        e_bfw.append(t_bfw)

        tok_sq = bld.inc("ac")

        @bld.on("scalar")
        def _(e, t=t, tok_x1=tok_x1, tok_sq=tok_sq):
            W(e, tok_x1)
            e.activation(out=sqs[:], in_=x1t[:], func=AF.Square
                         ).then_inc(tok_sq[0], 1)
        tok_sq = (tok_sq[0], tok_sq[1])

        tok_f1 = bld.inc("dv")

        @bld.on("vector")
        def _(e, t=t, tok_sq=tok_sq, tok_f1=tok_f1):
            W(e, tok_sq)
            e.reduce_sum(v2[:], sqs[:], axis=AXX)
            e.tensor_copy(out=m2a[:, t:t + 1], in_=v1[:])
            bld.bar(e, e.tensor_mul(v3[:], v1[:], v1[:]), "dvb")
            e.scalar_tensor_tensor(out=v2[:], in0=v2[:], scalar=C["inv"],
                                   in1=v3[:], op0=ALU.mult, op1=ALU.subtract
                                   ).then_inc(tok_f1[0], 1)
        tok_f1 = (tok_f1[0], tok_f1[1])

        tok_f2 = bld.inc("ac")

        @bld.on("scalar")
        def _(e, tok_f1=tok_f1, tok_f2=tok_f2):
            W(e, tok_f1)
            bld.bar(e, e.activation(out=v3[:], in_=v2[:], func=AF.Ln,
                                    bias=C["eps1"]), "acb")
            e.activation(out=v3[:], in_=v3[:], func=AF.Exp, scale=-0.5
                         ).then_inc(tok_f2[0], 1)
        tok_f2 = (tok_f2[0], tok_f2[1])

        tok_f3 = bld.inc("dv")

        @bld.on("vector")
        def _(e, t=t, tok_f2=tok_f2, tok_f3=tok_f3):
            W(e, tok_f2)
            e.tensor_copy(out=r2a[:, t:t + 1], in_=v3[:])
            bld.bar(e, e.tensor_mul(v2[:], v2[:], v3[:]), "dvb")
            e.tensor_mul(v2[:], v2[:], v3[:]).then_inc(tok_f3[0], 1)
        tok_f3 = (tok_f3[0], tok_f3[1])

        tok_f4 = bld.inc("ac")

        @bld.on("scalar")
        def _(e, tok_f3=tok_f3, tok_f4=tok_f4):
            W(e, tok_f3)
            bld.bar(e, e.activation(out=v2[:], in_=v2[:], func=AF.Ln,
                                    bias=C["eps2"]), "acb")
            e.activation(out=v2[:], in_=v2[:], func=AF.Exp, scale=-0.5
                         ).then_inc(tok_f4[0], 1)
        tok_f4 = (tok_f4[0], tok_f4[1])

        tok_f5 = bld.inc("dv")

        @bld.on("vector")
        def _(e, t=t, tok_f4=tok_f4, tok_f5=tok_f5):
            W(e, tok_f4)
            bld.bar(e, e.tensor_mul(rhoa[:, t:t + 1], r2a[:, t:t + 1],
                                    v2[:]), "dvb")
            e.tensor_mul(rhm2[:, t:t + 1], rhoa[:, t:t + 1], m2a[:, t:t + 1]
                         ).then_inc(tok_f5[0], 1)
    t_x1bf_all = e_bfw[-1]
    t_rho_all = bld.val("dv")

    free("ge")
    GF = "gf"
    x1h = sbg(GF, "x1h", [128, 16, 1024], BF16)
    GPF = "gpf"
    pH = [psg(GPF, f"pH{i}", [128, 24]) for i in range(2)]
    pmisc = psg(GPF, "pmisc", [128, 24])

    tok_cs = bld.inc("pe")

    @bld.on("tensor")
    def _(e, tok_cs=tok_cs, tok_bst2=tok_bst2):
        W(e, t_wct)
        W(e, t_ones)
        W(e, tok_bst2)
        for i in range(32):
            mm = e.matmul(pmisc[0:1, :], ones_bf[:, 0:1], WcT[:, i, :],
                          start=(i == 0), stop=(i == 31))
        mm.then_inc(tok_cs[0], 1)
    tok_cs = (tok_cs[0], tok_cs[1])

    tok_cse = bld.inc("ac")

    @bld.on("scalar")
    def _(e, tok_cs=tok_cs, tok_cse=tok_cse):
        W(e, tok_cs)
        e.copy(out=cs1[:], in_=pmisc[0:1, :]).then_inc(tok_cse[0], 1)
    tok_cse = (tok_cse[0], tok_cse[1])

    tok_bcast = bld.inc("pe")

    @bld.on("tensor")
    def _(e, tok_cse=tok_cse, tok_bcast=tok_bcast):
        W(e, tok_cse)
        W(e, t_bc)
        e.matmul(pmisc[:], ones_bf[0:1, :], cs1[:], start=True, stop=True)
        e.matmul(pH[0][:], ones_bf[0:1, :], bc1b[:], start=True, stop=True
                 ).then_inc(tok_bcast[0], 1)
    tok_bcast = (tok_bcast[0], tok_bcast[1])

    tok_bce = bld.inc("ac", 2)

    @bld.on("scalar")
    def _(e, tok_bcast=tok_bcast, tok_bce=tok_bce):
        W(e, tok_bcast)
        e.copy(out=csb[:], in_=pmisc[:])
        e.copy(out=bbb[:], in_=pH[0][:]).then_inc(tok_bce[0], 2)
    tok_bce = (tok_bce[0], tok_bce[1])

    h_pe_half = []
    h_ev_toks = []
    for half in range(2):
        tr_last = None
        for i in range(16):
            tok = bld.inc("x1h", 16)

            @bld.on("scalar")
            def _(e, half=half, i=i, tok=tok,
                  pe_prev=(h_pe_half[-1] if (half == 1 and i == 0) else None)):
                if half == 0 and i == 0:
                    W(e, t_x1bf_all)
                    W(e, t_rho_all)
                if pe_prev is not None:
                    W(e, pe_prev)
                c = 16 * half + i
                e.dma_start(out=x1h[:, i, :],
                            in_=x1bf_dram[:, 128 * c:128 * (c + 1)],
                            transpose=True).then_inc(tok[0], 16)
            tr_last = (tok[0], tok[1])
        t_tr_all = tr_last

        for t in range(NTILE):
            hti = half * 8 + t
            tok_pe = bld.inc("pe")

            @bld.on("tensor")
            def _(e, half=half, t=t, tok_pe=tok_pe, t_tr_all=t_tr_all,
                  tok_bce=tok_bce,
                  prev_ev=(h_ev_toks[hti - 2] if hti >= 2 else None)):
                if t == 0:
                    W(e, t_tr_all)
                    if half == 0:
                        W(e, tok_bce)
                if prev_ev is not None:
                    W(e, prev_ev)
                for i in range(16):
                    mm = e.matmul(pH[t % 2][:],
                                  x1h[:, i, 128 * t:128 * (t + 1)],
                                  WcT[:, 16 * half + i, :],
                                  start=(i == 0), stop=(i == 15))
                mm.then_inc(tok_pe[0], 1)
            tok_pe = (tok_pe[0], tok_pe[1])
            if t == NTILE - 1:
                h_pe_half.append(tok_pe)

            tok_he = bld.inc("dv")

            @bld.on("vector")
            def _(e, half=half, t=t, tok_pe=tok_pe, tok_he=tok_he):
                W(e, tok_pe)
                if half == 0:
                    e.tensor_copy(out=Hlog[:, t, :], in_=pH[t % 2][:]
                                  ).then_inc(tok_he[0], 1)
                else:
                    e.tensor_add(Hlog[:, t, :], Hlog[:, t, :], pH[t % 2][:]
                                 ).then_inc(tok_he[0], 1)
            h_ev_toks.append((tok_he[0], tok_he[1]))

    tok_fix = bld.inc("dv")

    @bld.on("vector")
    def _(e, tok_fix=tok_fix, last_ev=h_ev_toks[-1], t_rho_all=t_rho_all):
        W(e, last_ev)
        W(e, t_rho_all)
        for t in range(NTILE):
            bld.bar(e, e.scalar_tensor_tensor(out=u24[:], in0=csb[:],
                                              scalar=rhm2[:, t:t + 1],
                                              in1=bbb[:], op0=ALU.mult,
                                              op1=ALU.subtract), "dvb")
            bld.bar(e, e.tensor_scalar(out=Hlog[:, t, :], in0=Hlog[:, t, :],
                                       scalar1=rhoa[:, t:t + 1], scalar2=None,
                                       op0=ALU.mult), "dvb")
            mm = e.tensor_sub(Hlog[:, t, :], Hlog[:, t, :], u24[:])
            if t < NTILE - 1:
                bld.bar(e, mm, "dvb")
        mm.then_inc(tok_fix[0], 1)
    tok_fix = (tok_fix[0], tok_fix[1])

    Z16 = Hlog[:, :, 0:16]
    Z4 = Z16.rearrange("p t (i j) -> p t i j", i=4)
    M4 = M16[:].rearrange("p t (i j) -> p t i j", i=4)
    T4 = t16[:].rearrange("p t (i j) -> p t i j", i=4)

    tok_g1 = bld.inc("dv")

    @bld.on("vector")
    def _(e, tok_g1=tok_g1, tok_fix=tok_fix):
        W(e, tok_fix)
        bld.bar(e, e.reduce_max(zmx[:], Z4, axis=AXX), "dvb")
        bld.bar(e, e.reduce_max(zmx2[:], zmx[:], axis=AXX), "dvb")
        e.tensor_sub(M16[:], Z16, zmx2[:].broadcast_to([128, 8, 16])
                     ).then_inc(tok_g1[0], 1)
    tok_g1 = (tok_g1[0], tok_g1[1])

    tok_g2 = bld.inc("ac")

    @bld.on("scalar")
    def _(e, tok_g1=tok_g1, tok_g2=tok_g2):
        W(e, tok_g1)
        e.activation(out=M16[:], in_=M16[:], func=AF.Exp, scale=10.0
                     ).then_inc(tok_g2[0], 1)
    tok_g2 = (tok_g2[0], tok_g2[1])

    tok_g3 = bld.inc("dv")

    @bld.on("vector")
    def _(e, tok_g2=tok_g2, tok_g3=tok_g3):
        W(e, tok_g2)
        bld.bar(e, e.tensor_scalar(out=bS[:], in0=Hlog[:, :, 0:4],
                                   scalar1=0.0, scalar2=1.0, op0=ALU.mult,
                                   op1=ALU.add), "dvb")
        for _i in range(50):
            bld.bar(e, e.tensor_mul(
                T4, M4, bS[:].unsqueeze(2).broadcast_to([128, 8, 4, 4])),
                "dvb")
            bld.bar(e, e.reduce_sum(aS[:], T4, axis=AXX), "dvb")
            bld.bar(e, e.reciprocal(aS[:], aS[:]), "dvb")
            bld.bar(e, e.tensor_mul(
                T4, M4, aS[:].unsqueeze(3).broadcast_to([128, 8, 4, 4])),
                "dvb")
            bld.bar(e, e.reduce_sum(bS[:], T4.transpose([0, 1, 3, 2]),
                                    axis=AXX), "dvb")
            bld.bar(e, e.reciprocal(bS[:], bS[:]), "dvb")
        bld.bar(e, e.tensor_mul(
            T4, M4, aS[:].unsqueeze(3).broadcast_to([128, 8, 4, 4])), "dvb")
        e.tensor_mul(Hres[:].rearrange("p t (i j) -> p t i j", i=4), T4,
                     bS[:].unsqueeze(2).broadcast_to([128, 8, 4, 4])
                     ).then_inc(tok_g3[0], 1)
    tok_g3 = (tok_g3[0], tok_g3[1])

    tok_g4 = bld.inc("ac")

    @bld.on("scalar")
    def _(e, tok_g4=tok_g4):
        W(e, t_g)
        e.activation(out=g11[:], in_=g11[:], func=AF.Exp, scale=-1.0
                     ).then_inc(tok_g4[0], 1)
    tok_g4 = (tok_g4[0], tok_g4[1])

    tok_g5 = bld.inc("dv")

    @bld.on("vector")
    def _(e, tok_g4=tok_g4, tok_g5=tok_g5):
        W(e, tok_g4)
        bld.bar(e, e.tensor_scalar(out=g11[:], in0=g11[:], scalar1=1.0,
                                   scalar2=None, op0=ALU.add), "dvb")
        bld.bar(e, e.reciprocal(g11[:], g11[:]), "dvb")
        e.tensor_copy(out=g11b[:], in_=g11[:]).then_inc(tok_g5[0], 1)
    tok_g5 = (tok_g5[0], tok_g5[1])

    tok_g6 = bld.inc("pe")

    @bld.on("tensor")
    def _(e, tok_g5=tok_g5, tok_g6=tok_g6):
        W(e, tok_g5)
        e.matmul(pmisc[:, 0:1], ones_bf[0:1, :], g11b[:], start=True,
                 stop=True).then_inc(tok_g6[0], 1)
    tok_g6 = (tok_g6[0], tok_g6[1])

    tok_g7 = bld.inc("ac")

    @bld.on("scalar")
    def _(e, tok_g6=tok_g6, tok_g7=tok_g7):
        W(e, tok_g6)
        e.copy(out=gcol[:], in_=pmisc[:, 0:1]).then_inc(tok_g7[0], 1)
    tok_g7 = (tok_g7[0], tok_g7[1])

    tok_g8 = bld.inc("dv")

    @bld.on("vector")
    def _(e, tok_g3=tok_g3, tok_g7=tok_g7, tok_g8=tok_g8):
        W(e, tok_g3)
        W(e, tok_g7)
        bld.bar(e, e.tensor_scalar(out=gm1[:], in0=gcol[:], scalar1=-1.0,
                                   scalar2=1.0, op0=ALU.mult, op1=ALU.add),
                "dvb")
        bld.bar(e, e.tensor_scalar(out=Hres[:], in0=Hres[:], scalar1=gcol[:],
                                   scalar2=None, op0=ALU.mult), "dvb")
        diag = Hres[:, :, 0:16:5]
        bld.bar(e, e.tensor_scalar(out=diag, in0=diag, scalar1=gm1[:],
                                   scalar2=None, op0=ALU.add), "dvb")
        for j, (src0, dst) in enumerate(((Hlog[:, :, 16:20], Hpre),
                                         (Hlog[:, :, 20:24], Hpost))):
            bld.bar(e, e.reduce_max(zmx2[:], src0, axis=AXX), "dvb")
            mm = e.tensor_sub(dst[:], src0,
                              zmx2[:].broadcast_to([128, 8, 4]))
            if j == 0:
                bld.bar(e, mm, "dvb")
        mm.then_inc(tok_g8[0], 1)
    tok_g8 = (tok_g8[0], tok_g8[1])

    tok_g9 = bld.inc("ac")

    @bld.on("scalar")
    def _(e, tok_g8=tok_g8, tok_g9=tok_g9):
        W(e, tok_g8)
        bld.bar(e, e.activation(out=Hpre[:], in_=Hpre[:], func=AF.Exp),
                "acb")
        e.activation(out=Hpost[:], in_=Hpost[:], func=AF.Exp
                     ).then_inc(tok_g9[0], 1)
    tok_g9 = (tok_g9[0], tok_g9[1])

    tok_g10 = bld.inc("dv")

    @bld.on("vector")
    def _(e, tok_g9=tok_g9, tok_g10=tok_g10):
        W(e, tok_g9)
        for j, dst in enumerate((Hpre, Hpost)):
            bld.bar(e, e.reduce_sum(zmx2[:], dst[:], axis=AXX), "dvb")
            bld.bar(e, e.reciprocal(zmx2[:], zmx2[:]), "dvb")
            mm = e.tensor_mul(dst[:], dst[:],
                              zmx2[:].broadcast_to([128, 8, 4]))
            if j == 0:
                bld.bar(e, mm, "dvb")
        mm.then_inc(tok_g10[0], 1)
    tok_g10 = (tok_g10[0], tok_g10[1])

    free("gf"); free("gwc"); free("gpf")
    GH = "gh"
    Wbr_sb = sbg(GH, "Wbr_sb", [128, 8, 1024], BF16)
    xprFM = sbg(GH, "xprFM", [128, 8, 1024], BF16)
    dt_sb = sbg(GH, "dt_sb", [128, D], F32)
    x1l = [sbg(GH, f"x1l{i}", [128, D], F32) for i in range(2)]
    xp_sb = sbg(GH, "xp_sb", [128, 1024], F32)
    xpb_sb = sbg(GH, "xpb_sb", [128, 1024], BF16)
    off_sb = [sbg(GH, f"off_sb{i}", [128, D], F32) for i in range(2)]
    GPH = "gph"
    py = [psg(GPH, f"py{i}", [128, 1024]) for i in range(2)]

    t_wbr = emit_dma("gpsimd", "wbr", Wbr_sb[:],
                     Wbrh.rearrange("(i p) n -> p i n", p=128),
                     pre_wait=h_pe_half[-1])

    xp_dmas, h1_dv = [], []
    for t in range(NTILE):
        t_x1l = emit_dma("scalar", "hx1a", x1l[t % 2][:],
                         x1_dram[128 * t:128 * (t + 1), :],
                         pre_wait=(h1_dv[t - 2] if t >= 2 else None))
        tok_h1 = bld.inc("dv")

        @bld.on("vector")
        def _(e, t=t, tok_h1=tok_h1, tok_g10=tok_g10, t_x1l=t_x1l,
              prev=(xp_dmas[t - 1] if t >= 1 else None)):
            if t == 0:
                W(e, tok_g10)
            W(e, t_x1l)
            if prev is not None:
                W(e, prev)
            e.tensor_scalar(out=dt_sb[:], in0=x1l[t % 2][:],
                            scalar1=m2a[:, t:t + 1], scalar2=r2a[:, t:t + 1],
                            op0=ALU.subtract, op1=ALU.mult)
            dv_ = dt_sb[:].rearrange("p (n c) -> p n c", n=4)
            e.tensor_scalar(out=xp_sb[:], in0=dv_[:, 0, :],
                            scalar1=Hpre[:, t, 0:1], scalar2=None,
                            op0=ALU.mult)
            for n in range(1, 4):
                e.scalar_tensor_tensor(out=xp_sb[:], in0=dv_[:, n, :],
                                       scalar=Hpre[:, t, n:n + 1],
                                       in1=xp_sb[:], op0=ALU.mult, op1=ALU.add)
            e.tensor_copy(out=xpb_sb[:], in_=xp_sb[:]).then_inc(tok_h1[0], 1)
        tok_h1 = (tok_h1[0], tok_h1[1])
        h1_dv.append(tok_h1)

        t_xpw = emit_dma("scalar", "xpw",
                         xpre_dram[128 * t:128 * (t + 1), :], xpb_sb[:],
                         pre_wait=tok_h1)
        xp_dmas.append(t_xpw)
    t_xp_all = xp_dmas[-1]

    xtr_last = None
    for c in range(8):
        tok = bld.inc("xpt", 16)

        @bld.on("scalar")
        def _(e, c=c, tok=tok):
            if c == 0:
                W(e, t_xp_all)
            e.dma_start(out=xprFM[:, c, :],
                        in_=xpre_dram[:, 128 * c:128 * (c + 1)],
                        transpose=True).then_inc(tok[0], 16)
        xtr_last = (tok[0], tok[1])
    t_xtr_all = xtr_last

    h2_dv, h2_out, y_dmas = [], [], []

    def emit_h2_head(t):
        t_x1l2 = emit_dma("scalar", "hx1b", x1l[t % 2][:],
                          x1_dram[128 * t:128 * (t + 1), :],
                          pre_wait=(h2_out[t - 2] if t >= 2 else
                                    (t_xtr_all if t == 0 else None)))
        tok_pe = bld.inc("pe")

        @bld.on("tensor")
        def _(e, t=t, tok_pe=tok_pe,
              prev_dv=(h2_dv[t - 2] if t >= 2 else None)):
            if t == 0:
                W(e, t_xtr_all)
                W(e, t_wbr)
            if prev_dv is not None:
                W(e, prev_dv)
            for c in range(8):
                e.matmul(py[t % 2][:, 0:512],
                         xprFM[:, c, 128 * t:128 * (t + 1)],
                         Wbr_sb[:, c, 0:512], start=(c == 0), stop=(c == 7))
                mm = e.matmul(py[t % 2][:, 512:1024],
                              xprFM[:, c, 128 * t:128 * (t + 1)],
                              Wbr_sb[:, c, 512:1024], start=(c == 0),
                              stop=(c == 7))
            mm.then_inc(tok_pe[0], 1)
        tok_pe = (tok_pe[0], tok_pe[1])

        tok_mix = bld.inc("dv")

        @bld.on("vector")
        def _(e, t=t, tok_pe=tok_pe, tok_mix=tok_mix, t_x1l2=t_x1l2,
              prev_y=(y_dmas[t - 2] if t >= 2 else None),
              tok_g10=tok_g10):
            if t == 0:
                W(e, tok_g10)
            W(e, tok_pe)
            W(e, t_x1l2)
            if prev_y is not None:
                W(e, prev_y)
            e.tensor_scalar(out=dt_sb[:], in0=x1l[t % 2][:],
                            scalar1=m2a[:, t:t + 1], scalar2=r2a[:, t:t + 1],
                            op0=ALU.subtract, op1=ALU.mult)
            dv_ = dt_sb[:].rearrange("p (n c) -> p n c", n=4)
            ov = off_sb[t % 2][:].rearrange("p (n c) -> p n c", n=4)
            for n in range(4):
                e.tensor_scalar(out=xp_sb[:], in0=dv_[:, 0, :],
                                scalar1=Hres[:, t, 4 * n:4 * n + 1],
                                scalar2=None, op0=ALU.mult)
                for j in range(1, 4):
                    e.scalar_tensor_tensor(
                        out=xp_sb[:], in0=dv_[:, j, :],
                        scalar=Hres[:, t, 4 * n + j:4 * n + j + 1],
                        in1=xp_sb[:], op0=ALU.mult, op1=ALU.add)
                e.scalar_tensor_tensor(out=ov[:, n, :], in0=py[t % 2][:],
                                       scalar=Hpost[:, t, n:n + 1],
                                       in1=xp_sb[:], op0=ALU.mult, op1=ALU.add)
            e.reduce_sum(vA1[:, t:t + 1], off_sb[t % 2][:], axis=AXX)
            e.tensor_scalar(out=vA1[:, t:t + 1], in0=vA1[:, t:t + 1],
                            scalar1=C["inv"], scalar2=None, op0=ALU.mult
                            ).then_inc(tok_mix[0], 1)
        tok_mix = (tok_mix[0], tok_mix[1])
        h2_dv.append(tok_mix)
        return tok_mix

    def emit_h2_tail(t, tok_mix):
        tok_hsq = bld.inc("ac")

        @bld.on("scalar")
        def _(e, t=t, tok_mix=tok_mix, tok_hsq=tok_hsq,
              prev_hf=(h2_hf[t - 1] if t >= 1 else None)):
            W(e, tok_mix)
            if prev_hf is not None:
                W(e, prev_hf)
            e.activation(out=sqs[:], in_=off_sb[t % 2][:], func=AF.Square
                         ).then_inc(tok_hsq[0], 1)
        tok_hsq = (tok_hsq[0], tok_hsq[1])

        tok_hf = bld.inc("dv")

        @bld.on("vector")
        def _(e, t=t, tok_hsq=tok_hsq, tok_hf=tok_hf):
            W(e, tok_hsq)
            e.reduce_sum(vA2[:, t:t + 1], sqs[:], axis=AXX)
            bld.bar(e, e.tensor_mul(vA3[:, t:t + 1], vA1[:, t:t + 1],
                                    vA1[:, t:t + 1]), "dvb")
            e.scalar_tensor_tensor(out=vA2[:, t:t + 1], in0=vA2[:, t:t + 1],
                                   scalar=C["inv"], in1=vA3[:, t:t + 1],
                                   op0=ALU.mult, op1=ALU.subtract
                                   ).then_inc(tok_hf[0], 1)
        tok_hf = (tok_hf[0], tok_hf[1])
        h2_hf.append(tok_hf)

        tok_hr = bld.inc("ac")

        @bld.on("scalar")
        def _(e, t=t, tok_hf=tok_hf, tok_hr=tok_hr):
            W(e, tok_hf)
            bld.bar(e, e.activation(out=vA3[:, t:t + 1], in_=vA2[:, t:t + 1],
                                    func=AF.Ln, bias=C["eps1"]), "acb")
            e.activation(out=vA3[:, t:t + 1], in_=vA3[:, t:t + 1],
                         func=AF.Exp, scale=-0.5).then_inc(tok_hr[0], 1)
        tok_hr = (tok_hr[0], tok_hr[1])

        tok_out = bld.inc("dv")

        @bld.on("vector")
        def _(e, t=t, tok_hr=tok_hr, tok_out=tok_out):
            W(e, tok_hr)
            e.tensor_scalar(out=off_sb[t % 2][:], in0=off_sb[t % 2][:],
                            scalar1=vA1[:, t:t + 1], scalar2=vA3[:, t:t + 1],
                            op0=ALU.subtract, op1=ALU.mult)
            e.tensor_add(off_sb[t % 2][:], off_sb[t % 2][:], x1l[t % 2][:]
                         ).then_inc(tok_out[0], 1)
        tok_out = (tok_out[0], tok_out[1])
        h2_out.append(tok_out)

        t_y = emit_dma("scalar", "yw", yh[128 * t:128 * (t + 1), :],
                       off_sb[t % 2][:], pre_wait=tok_out)
        y_dmas.append(t_y)

    h2_hf = []
    mixes = []
    for t in range(NTILE):
        mixes.append(emit_h2_head(t))
        if t >= 1:
            emit_h2_tail(t - 1, mixes[t - 1])
    emit_h2_tail(NTILE - 1, mixes[-1])

    t_final = y_dmas[-1]

    @bld.on("scalar")
    def _(e, t_final=t_final):
        W(e, t_final)

    with nc.Block() as block:
        @block.sync
        def _(e):
            for fn in bld.sched["sync"]:
                fn(e)

        @block.gpsimd
        def _(e):
            for fn in bld.sched["gpsimd"]:
                fn(e)

        @block.vector
        def _(e):
            for fn in bld.sched["vector"]:
                fn(e)

        @block.scalar
        def _(e):
            for fn in bld.sched["scalar"]:
                fn(e)

        @block.tensor
        def _(e):
            for fn in bld.sched["tensor"]:
                fn(e)

    for gname in reversed(list(groups)):
        free(gname)
    bld.stack.close()
    return nc


def kernel(L0_attn_x, L0_attn_Wq, L0_attn_Wk, L0_attn_Wv, L0_attn_Wo,
           L0_attn_cos, L0_attn_sin, L0_ffn_W_res, L0_ffn_W_pre,
           L0_ffn_W_post, L0_ffn_H_res_base, L0_ffn_H_pre_base,
           L0_ffn_H_post_base, L0_ffn_gate, L0_ffn_W_branch):
    x = np.ascontiguousarray(np.asarray(L0_attn_x, dtype=np.float32))
    Wq = np.asarray(L0_attn_Wq, np.float32)
    Wk = np.asarray(L0_attn_Wk, np.float32)
    Wv = np.asarray(L0_attn_Wv, np.float32)
    Wo = np.ascontiguousarray(np.asarray(L0_attn_Wo, np.float32))
    cos = np.ascontiguousarray(np.asarray(L0_attn_cos, np.float32))
    sin = np.ascontiguousarray(np.asarray(L0_attn_sin, np.float32))
    WcatT = np.ascontiguousarray(np.vstack([
        np.asarray(L0_ffn_W_res, np.float32),
        np.asarray(L0_ffn_W_pre, np.float32),
        np.asarray(L0_ffn_W_post, np.float32)]).T)
    basecat = np.ascontiguousarray(np.concatenate([
        np.asarray(L0_ffn_H_res_base, np.float32).reshape(-1),
        np.asarray(L0_ffn_H_pre_base, np.float32).reshape(-1),
        np.asarray(L0_ffn_H_post_base, np.float32).reshape(-1),
    ]).reshape(1, 24))
    gate = np.ascontiguousarray(np.asarray(L0_ffn_gate, np.float32)
                                .reshape(1, 1))
    Wbr = np.ascontiguousarray(np.asarray(L0_ffn_W_branch, np.float32))

    tri = np.ascontiguousarray(
        np.where(np.arange(128)[None, :] >= np.arange(128)[:, None],
                 np.float32(0.0), np.float32(-1e9)).astype(np.float32))
    onesf = np.ones((128, 128), np.float32)
    cvec = np.zeros((128, 8), np.float32)
    cvec[:, 0] = -SHIFT
    cvec[:, 1] = 1e-5
    cvec[:, 2] = 1.1920929e-07
    cvec[:, 3] = 1.0 / 4096.0
    cvec[:, 4] = 1.0

    in_maps = []
    for c in range(8):
        b, r = c // 4, c % 4
        qh = list(range(4 * r, 4 * r + 4)) + \
            list(range(16 + 4 * r, 16 + 4 * r + 4))
        qcols = np.concatenate([np.arange(128 * h, 128 * (h + 1)) for h in qh])
        kvcols = np.concatenate([np.arange(128 * r, 128 * (r + 1)),
                                 np.arange(128 * (r + 4), 128 * (r + 5))])
        cidx = np.zeros((128, 32), np.int32)
        for h in range(32):
            if h < 16:
                rsrc, uu = h // 4, h % 4
            else:
                rsrc, uu = (h - 16) // 4, 4 + (h - 16) % 4
            cidx[:, h] = 2048 * rsrc + 512 * r + 128 * (uu % 4) + \
                np.arange(128)
        in_maps.append({
            "x0": np.ascontiguousarray(x[b, 1024 * r:1024 * (r + 1)]),
            "Wq_my": np.ascontiguousarray(Wq[:, qcols]),
            "Wk_my": np.ascontiguousarray(Wk[:, kvcols]),
            "Wv_my": np.ascontiguousarray(Wv[:, kvcols]),
            "Wo": Wo, "Wbr": Wbr, "WcatT": WcatT, "basecat": basecat,
            "gate": gate, "cosf": cos, "sinf": sin, "trimask": tri,
            "onesf": onesf, "cvec": cvec, "ctxidx": cidx,
        })

    nc = build_program()
    res = run_bass_kernel_spmd(nc, in_maps, core_ids=list(range(8)))
    out = np.empty((B, S, D), np.float32)
    for c in range(8):
        b, r = c // 4, c % 4
        out[b, 1024 * r:1024 * (r + 1)] = res.results[c]["y"]
    return out


